# version 30
# baseline (speedup 1.0000x reference)
import contextlib
import numpy as np
import ml_dtypes
import concourse.bass as bass
import concourse.mybir as mybir
from concourse.bass_utils import run_bass_kernel_spmd

F32 = mybir.dt.float32
BF16 = mybir.dt.bfloat16
AF = mybir.ActivationFunctionType
ALU = mybir.AluOpType
BFnp = ml_dtypes.bfloat16

D = 1024
NCH = 8
LT = 4096
OWN = 2048
LORA = 16
TOT = OWN + LORA
NH = 16
HD = 64
CS = 512
NSL = LT // CS
WQT = 512
KCN = 32
KCB = 8
FS = 1024
NFS = OWN // FS

ENGS = ("sync", "gpsimd", "vector", "scalar", "tensor")
SEMS = ("pe", "ve", "se", "lds", "ldg", "wl", "wo", "x0", "x1", "q0", "q1",
        "v0", "v1", "t0", "t1", "qq", "lr", "lr0", "lr1", "ev", "o0", "o1",
        "g0", "g1", "g2", "h0", "h1", "h2", "rs", "db", "p0", "p1")


class Prog:
    def __init__(self):
        self.ops = {e: [] for e in ENGS}
        self.cnt = {s: 0 for s in SEMS}

    def emit(self, eng, fn, wait=(), inc=None, n=1, nodrain=False):
        w = tuple((k, int(v)) for (k, v) in wait if v)
        tok = None
        if inc is not None:
            self.cnt[inc] += n
            tok = self.cnt[inc]
        drain = (eng == "vector") and not nodrain
        self.ops[eng].append((fn, w, inc, n, drain))
        return tok

    def dma(self, eng, out, in_, wait=(), sem=None):
        if sem is None:
            sem = "lds" if eng == "sync" else "ldg"
        return self.emit(eng, (lambda e, o=out, i=in_: e.dma_start(out=o, in_=i)),
                         wait, sem, 16, nodrain=True)


def build_nc(dbg=False):
    nc = bass.Bass("TRN2", target_bir_lowering=False)
    dt = lambda name, shape, dtype, kind: nc.dram_tensor(
        name, shape, dtype, kind=kind).ap()

    xt = dt("xt", [128, NCH, LT], F32, "ExternalInput")
    xlt = dt("xlt", [128, NCH, LORA], F32, "ExternalInput")
    wqx = dt("wqx", [128, NCH, D], F32, "ExternalInput")
    wkx = dt("wkx", [128, NCH, D], F32, "ExternalInput")
    wvx = dt("wvx", [128, NCH, D], F32, "ExternalInput")
    wox = dt("wox", [128, NCH, D], F32, "ExternalInput")
    w1x = dt("w1x", [8, 128, 4, NCH, 128], F32, "ExternalInput")
    w2x = dt("w2x", [8, 128, 32, 128], F32, "ExternalInput")
    w1e = dt("w1e", [8, 128, 4096], BF16, "Internal")
    rl_hbm = dt("rl_hbm", [5, 16, 512], BF16, "Internal")
    w2e = dt("w2e", [8, 128, 4096], BF16, "Internal")
    w2r = dt("w2r", [8, 128, 4096], BF16, "Internal")
    ccx = dt("ccx", [128, 72], F32, "ExternalInput")
    bvr = dt("bvr", [1, D], F32, "ExternalInput")
    wmodx = dt("wmodx", [128, NCH, 6 * D], F32, "ExternalInput")
    btabx = dt("btabx", [128, 512], F32, "ExternalInput")
    eyex = dt("eyex", [128, 128], BF16, "ExternalInput")
    sclx = dt("sclx", [128, 1], F32, "ExternalInput")

    kind_scr = "ExternalOutput" if dbg else "Internal"
    q_hbm = dt("q_hbm", [128, NCH, TOT], BF16, kind_scr)
    v_hbm = dt("v_hbm", [128, KCN, NH * 65], BF16, kind_scr)
    out_t = dt("out_t", [128, NCH, OWN], F32, "ExternalOutput")
    dbg_x2 = dt("dbg_x2", [128, NCH * TOT], BF16, kind_scr) if dbg else None
    dbg_mod = dt("dbg_mod", [128, 48], F32, kind_scr) if dbg else None
    dbg_k = dt("dbg_k", [128, 32768], BF16, kind_scr) if dbg else None
    dbg_h2l = dt("dbg_h2l", [128, NCH * LORA], BF16, kind_scr) if dbg else None
    dbg_g = dt("dbg_g", [128, 32768], BF16, kind_scr) if dbg else None

    P = Prog()
    es = contextlib.ExitStack()
    with es:
        sbt = lambda name, shape, dtype: es.enter_context(
            nc.sbuf_tensor(name, shape, dtype))
        x2 = sbt("x2", [128, NCH * TOT], BF16)
        rows = sbt("rows", [1, 4096], F32)
        btab = sbt("btab", [128, 512], F32)
        lq = sbt("lq", [128, 512], BF16)
        rlq = sbt("rlq", [128, 512], BF16)
        rlp = sbt("rlp", [1, 1024], BF16)
        rlb = sbt("rlb", [128, 1024], BF16)
        modc = sbt("modc", [128, 48], F32)
        cc = sbt("cc", [128, 72], F32)
        eye = sbt("eye", [128, 128], BF16)
        onec = sbt("onec", [128, 1], BF16)
        oner = sbt("oner", [1, 128], BF16)
        onerf = sbt("onerf", [1, 128], F32)
        one1 = sbt("one1", [1, 1], BF16)
        h2l = sbt("h2l", [128, NCH * LORA], BF16)
        odds = sbt("odds", [128, 64], F32)
        sil = sbt("sil", [128, 8], BF16)
        sclc = sbt("sclc", [128, 1], F32)
        bigA = sbt("bigA", [128, 32768], BF16)
        bigB = sbt("bigB", [128, 24832], BF16)
        bigD = sbt("bigD", [128, 18688], BF16)
        psA = es.enter_context(nc.psum_tensor("psA", [128, 1024], F32))
        psB = es.enter_context(nc.psum_tensor("psB", [128, 1024], F32))
        psS = es.enter_context(nc.psum_tensor("psS", [128, 2048], F32))
        psC = psS[:, 0:1024]
        psD = psS[:, 1024:2048]
        sems = {s: es.enter_context(nc.semaphore(s)) for s in SEMS}

        bigD_f32 = bigD[:].bitcast(F32)
        bvrb = bigD[0:1, 16416:17440]
        rlp2 = bigD[0:1, 17440:18464]
        trows = bigD[0:16, 17440:18464]
        tyv = lambda sl: rlb[:, sl * 512:(sl + 1) * 512]

        x2v = lambda ch: x2[:, ch * TOT:(ch + 1) * TOT]
        x2r = x2[:].rearrange("p (c t) -> p c t", c=NCH)
        K_sb = lambda j: bigA[:, j * 4096:(j + 1) * 4096]
        g_sb = lambda hch: bigA[:, hch * 1024:(hch + 1) * 1024]
        wk_sb = lambda dch: bigB[:, dch * 1024:(dch + 1) * 1024]
        wv_sb = lambda dch: bigB[:, 8320 + dch * 1024:8320 + (dch + 1) * 1024]
        wq_sb = lambda dch: bigB[:, 16640 + dch * 1024:16640 + (dch + 1) * 1024]
        vtb = lambda slot: (bigB[:, 0:8320] if slot == 0
                            else bigB[:, 16512:24832])
        wo_sb = bigB[:, 8320:16512]
        h2_sb = lambda dch: bigB[:, dch * 1024:(dch + 1) * 1024]
        ebc = lambda i: bigB[:, 8192 + i * 1024:8192 + (i + 1) * 1024]
        wsl = lambda slot: bigB[:, 12288 + slot * 4096:12288 + (slot + 1) * 4096]
        xoth = lambda slot: bigD[:, slot * 4096:(slot + 1) * 4096]
        h1s = bigD[:, 8192:12288]
        abcC = bigD[:, 12288:12800]
        bbcC = bigD[:, 12800:13312]
        qtmp = lambda slot: bigD[:, 13312 + slot * 512:13312 + (slot + 1) * 512]
        vtl = lambda slot: bigD[:, 14336 + slot * 1040:14336 + (slot + 1) * 1040]
        def pPk(nm, kc):
            base = (0 if nm == "A" else 4096) + ((kc % 4) // 2) * 2048 \
                + (kc % 2) * 1024
            return bigD[:, base:base + 1024]
        attq = lambda j: bigD[:, 8192 + j * 512:8192 + (j + 1) * 512]
        qqr = bigD[:, 12288:16384].rearrange("p (c t) -> p c t", c=NCH)
        lsst = bigD[:, 16416:18464]
        outsl = lambda slot: bigD_f32[:, slot * 1024:(slot + 1) * 1024]
        abcF = bigD[:, 4096:5120]
        bbcF = bigD[:, 5120:6144]
        tfs = lambda slot: bigD[:, 6144 + slot * 1024:6144 + (slot + 1) * 1024]
        urow = bigD[0:1, 8192:12288]
        evrS = bigD[0:1, 12288:16384]
        evrR = bigD[0:1, 8192:12288]

        SA, CA1, GA, SBv, CB1, GB = 0, 1, 2, 3, 4, 5
        mcol = lambda v, ch: modc[:, v * 8 + ch:v * 8 + ch + 1]
        BQ, BK, BO, B2 = 0, 8, 16, 24
        ccol = lambda base, ch: cc[:, base + ch:base + ch + 1]
        b1col = lambda hch: cc[:, 32 + hch:32 + hch + 1]
        btcol = lambda h, kc: btab[:, h * 32 + kc:h * 32 + kc + 1]
        rrow = lambda slot, w=1024: rows[0:1, slot * 1024:slot * 1024 + w]

        E = P.emit

        def mm(e, o, l, r, st_, sp, tp=None):
            return e.matmul(out=o, lhsT=l, rhs=r, start=st_, stop=sp,
                            tile_position=tp)

        P.dma("sync", cc[:], ccx[:])
        P.dma("sync", btab[:], btabx[:])
        P.dma("sync", eye[:], eyex[:])
        P.dma("sync", sclc[:], sclx[:])
        P.dma("gpsimd", bvrb[:], bvr[:])
        xlv = x2r[:, :, OWN:TOT]
        P.dma("gpsimd", xlv, xlt[:])
        P.dma("gpsimd",
              bigA[:, 0:5 * 6144].rearrange("p (c m) -> p c m", c=5),
              wmodx[:, 0:5, :])
        t_ld = P.dma("gpsimd",
                     bigB[:, 0:3 * 6144].rearrange("p (c m) -> p c m", c=3),
                     wmodx[:, 5:8, :])
        assert t_ld == 64
        wmod_sb = [bigA[:, d_ * 6144:(d_ + 1) * 6144] for d_ in range(5)] + \
                  [bigB[:, d_ * 6144:(d_ + 1) * 6144] for d_ in range(3)]
        mod_row = bigB[0:1, 18432:18432 + 6144]

        t_ones = E("vector", lambda e: (e.memset(onec[:], 1.0),
                                        e.memset(oner[:], 1.0),
                                        e.memset(onerf[:], 1.0),
                                        e.memset(one1[:], 1.0),
                                        e.engine_nop())[-1], (), "ve")
        t_sil = E("scalar", lambda e: e.activation(out=sil[:], in_=cc[:, 64:72],
                                                   func=AF.Silu),
                  [("lds", 64), ("ldg", 64)], "se")
        t_evrow = [0] * 12
        for mt in range(12):
            slot = mt % 2
            for dch in range(NCH):
                w = []
                if dch == 0:
                    w = [("lds", 64), ("ldg", 64), ("se", t_sil)]
                    if mt >= 2:
                        w.append(("ve", t_evrow[mt - 2]))
                tok = E("tensor", (lambda e, s_=slot, d_=dch, m_=mt:
                                   mm(e, psC[0:1, s_ * 512:(s_ + 1) * 512],
                                      sil[:, d_:d_ + 1],
                                      wmod_sb[d_][:, m_ * 512:(m_ + 1) * 512],
                                      d_ == 0, d_ == NCH - 1)),
                        w, "pe" if dch == NCH - 1 else None)
            t_evrow[mt] = E("vector", (lambda e, s_=slot, m_=mt:
                                       e.tensor_copy(
                                           out=mod_row[:, m_ * 512:(m_ + 1) * 512],
                                           in_=psC[0:1, s_ * 512:(s_ + 1) * 512])),
                            [("pe", tok)], "ve")
        for i in range(48):
            w = [("ve", t_evrow[min((i * 128) // 512, 11)])]
            if i == 0:
                w.append(("ve", t_ones))
            tok = E("tensor", (lambda e, i_=i: mm(e, psD[:, i_:i_ + 1],
                                                  mod_row[:, i_ * 128:(i_ + 1) * 128],
                                                  one1[:], True, True)),
                    w, "pe" if i == 47 else None)
        t_modcol = tok
        t_modc = E("vector", lambda e: e.tensor_copy(out=modc[:], in_=psD[:, 0:48]),
                   [("pe", t_modcol)], "ve")

        def fin_mod(e):
            e.drain()
            e.tensor_scalar(out=modc[:, 8:16], in0=modc[:, 8:16], scalar1=1.0,
                            scalar2=None, op0=ALU.add)
            e.tensor_scalar(out=modc[:, 32:40], in0=modc[:, 32:40], scalar1=1.0,
                            scalar2=None, op0=ALU.add)
            e.drain()
            for base in (16, 40):
                sl = modc[:, base:base + 8]
                msk = bigD[:].bitcast(mybir.dt.int32)[:, 0:8]
                mx = bigD_f32[:, 8:16]
                mn = bigD_f32[:, 16:24]
                e.tensor_single_scalar(out=msk, in_=sl, scalar=0.0, op=ALU.is_ge)
                e.tensor_scalar_max(out=mx, in0=sl, scalar1=0.25)
                e.tensor_scalar_min(out=mn, in0=sl, scalar1=-0.25)
                e.drain()
                e.select(out=sl, mask=msk, on_true=mx, on_false=mn, add_drain=True)
                e.drain()
            return e.engine_nop()
        t_mod = E("vector", fin_mod, [("ve", t_modc)], "ve")
        if dbg:
            P.dma("sync", dbg_mod[:], modc[:], [("ve", t_mod)], sem="db")

        def emit_wl(which):
            if which == 0:
                return P.dma("gpsimd",
                             bigB[:, 0:8192].rearrange("p (c m) -> p c m",
                                                       c=NCH),
                             wkx[:], [("pe", t_modcol)], sem="wl")
            if which == 1:
                return P.dma("gpsimd",
                             bigB[:, 16640:24832].rearrange(
                                 "p (c m) -> p c m", c=NCH),
                             wqx[:], sem="wl")
            return P.dma("gpsimd",
                         bigB[:, 8320:16512].rearrange("p (c m) -> p c m",
                                                       c=NCH),
                         wvx[:], sem="wl")
        t_x = [0] * NSL
        t_vlast = [0] * NSL
        t_h1tt = [0] * NSL
        kq_free = [0, 0]
        v_free = [0, 0]
        v_free_sem = ["se", "se"]
        first_v = [True]
        t_qtmp_free = [0, 0]
        t_vstage_free = [0, 0]
        t_kev_last = [0]
        t_vdma = {}

        H1 = lambda i: (h1s if i % 2 == 0 else bigD[:, 4096:8192])
        xoth0 = bigD[:, 0:4096]
        t_statm = [0] * (NSL + 1)
        t_h1k = {}
        t_rows = [0] * (NSL + 1)
        t_tc = [0] * (NSL + 1)
        h1ws = [None] * (NSL + 1)

        def xslice(s):
            if s < 4:
                return x2r[:, :, s * CS:(s + 1) * CS]
            return xoth0.rearrange("p (c t) -> p c t", c=NCH)

        def emit_xload(s):
            if s < 4:
                w = [("x1", t_x[s - 1])] if s >= 1 else []
                t_x[s] = P.dma("gpsimd", xslice(s),
                               xt[:, :, s * CS:(s + 1) * CS], w, sem="x1")
            else:
                w = [("ve", t_mod)]
                t_x[s] = P.dma("gpsimd", xslice(s),
                               xt[:, :, s * CS:(s + 1) * CS], w, sem="x0")

        def emit_xload_pair(s2, k):
            w = [("x0", t_x[s2 - 1] if k == 0 else t_x[s2]),
                 ("ve", t_h1k[(s2 - 1, k)]), ("pe", t_statm[s2 - 1])]
            t_x[s2] = P.dma(
                "gpsimd", xslice(s2)[:, 2 * k:2 * k + 2, :],
                xt[:, 2 * k:2 * k + 2, s2 * CS:(s2 + 1) * CS], w, sem="x0")
        emit_xload(0)
        emit_wl(0)
        emit_xload(1)
        emit_wl(1)
        emit_xload(2)
        t_wl3 = emit_wl(2)
        assert t_wl3 == 48
        emit_xload(3)

        def ln_rows(width, wm, ws2):
            def f1(e):
                e.tensor_scalar(out=rrow(0, width), in0=psC[0:1, 0:width],
                                scalar1=1.0 / D, scalar2=None, op0=ALU.mult)
                e.tensor_scalar(out=rrow(1, width), in0=psC[0:1, 512:512 + width],
                                scalar1=1.0 / D, scalar2=None, op0=ALU.mult)
                e.drain()
                e.tensor_tensor(out=rrow(3, width), in0=rrow(0, width),
                                in1=rrow(0, width), op=ALU.mult)
                e.drain()
                e.tensor_tensor(out=rrow(1, width), in0=rrow(1, width),
                                in1=rrow(3, width), op=ALU.subtract)
                e.drain()
                e.tensor_scalar(out=rrow(1, width), in0=rrow(1, width),
                                scalar1=1e-5, scalar2=None, op0=ALU.add)
                return e.engine_nop()
            tv1 = E("vector", f1, [("pe", wm), ("pe", ws2)], "ve")
            def rsq(e):
                e.activation(out=rrow(3, width), in_=rrow(1, width), func=AF.Ln)
                e.drain()
                return e.activation(out=rrow(2, width), in_=rrow(3, width),
                                    func=AF.Exp, scale=-0.5)
            tse = E("scalar", rsq, [("ve", tv1)], "se")

            def f2(e):
                e.scalar_tensor_tensor(out=rrow(1, width), in0=rrow(0, width),
                                       scalar=-1.0, in1=rrow(2, width),
                                       op0=ALU.mult, op1=ALU.mult)
                return e.engine_nop()
            return E("vector", f2, [("se", tse)], "ve")

        def emit_ln(xs, width, t_ready, extra_bcast_wait, slot=None):
            sl = h1s if slot is None else slot
            txq = E("vector", (lambda e, xs_=xs, w_=width, s_=sl: e.tensor_tensor(
                out=s_[:, 0:NCH * w_].rearrange("p (c t) -> p c t", c=NCH),
                in0=xs_, in1=xs_, op=ALU.mult)), t_ready, "ve")
            tm = None
            for dch in range(NCH):
                tm = E("tensor", (lambda e, d_=dch, xs_=xs, w_=width: mm(
                    e, psC[0:1, 0:w_], onec[:], xs_[:, d_, :], d_ == 0,
                    d_ == NCH - 1)),
                    list(t_ready) if dch == 0 else [],
                    "pe" if dch == NCH - 1 else None)
            ts2 = None
            for dch in range(NCH):
                ts2 = E("tensor", (lambda e, d_=dch, w_=width, s_=sl: mm(
                    e, psC[0:1, 512:512 + w_], onec[:],
                    s_[:, d_ * w_:(d_ + 1) * w_], d_ == 0, d_ == NCH - 1)),
                    [("ve", txq)] if dch == 0 else [],
                    "pe" if dch == NCH - 1 else None)
            tr = ln_rows(width, tm, ts2)
            tb = E("tensor", (lambda e, w_=width: (
                mm(e, psA[:, 0:w_], onerf[:], rrow(2, w_), True, True),
                mm(e, psA[:, 512:512 + w_], onerf[:], rrow(1, w_), True,
                   True))[-1]),
                [("ve", tr)] + list(extra_bcast_wait), "pe")
            tc = E("vector", (lambda e, w_=width: (
                e.tensor_copy(out=abcC[:, 0:w_], in_=psA[:, 0:w_]),
                e.tensor_copy(out=bbcC[:, 0:w_], in_=psA[:, 512:512 + w_]),
                e.engine_nop())[-1]), [("pe", tb)], "ve")
            return tc, tm, tr

        def emit_lnh1_stats(i):
            xs = xslice(i)
            sl = H1(i)
            sem = "x1" if i < 4 else "x0"
            t_ready = [(sem, t_x[i]), ("ve", t_mod)]
            if i >= 1:
                t_ready.append(("ve", t_rows[i - 1]))
            if i >= 2:
                t_ready.append(("pe", t_vlast[i - 2]))
            txq = E("vector", (lambda e, xs_=xs, s_=sl: e.tensor_tensor(
                out=s_[:, 0:NCH * CS].rearrange("p (c t) -> p c t", c=NCH),
                in0=xs_, in1=xs_, op=ALU.mult)), t_ready, "ve")
            tm = None
            for dch in range(NCH):
                tm = E("tensor", (lambda e, d_=dch, xs_=xs: mm(
                    e, psC[0:1, 0:CS], onec[:], xs_[:, d_, :], d_ == 0,
                    d_ == NCH - 1)),
                    list(t_ready) if dch == 0 else [],
                    "pe" if dch == NCH - 1 else None)
            ts2 = None
            for dch in range(NCH):
                ts2 = E("tensor", (lambda e, d_=dch, s_=sl: mm(
                    e, psC[0:1, 512:512 + CS], onec[:],
                    s_[:, d_ * CS:(d_ + 1) * CS], d_ == 0, d_ == NCH - 1)),
                    [("ve", txq)] if dch == 0 else [],
                    "pe" if dch == NCH - 1 else None)
            t_statm[i] = tm
            t_rows[i] = ln_rows(CS, tm, ts2)

        def emit_lnh1_bcast(i):
            xs = xslice(i)
            sl = H1(i)
            ebw = [("ve", t_tc[i - 1])] if i >= 1 else []
            tb = E("tensor", (lambda e: (
                mm(e, psA[:, 0:CS], onerf[:], rrow(2, CS), True, True),
                mm(e, psA[:, 512:512 + CS], onerf[:], rrow(1, CS), True,
                   True))[-1]),
                [("ve", t_rows[i])] + ebw, "pe")
            t_abc = E("vector", (lambda e: (
                e.tensor_copy(out=abcC[:, 0:CS], in_=psA[:, 0:CS]),
                e.tensor_copy(out=bbcC[:, 0:CS], in_=psA[:, 512:512 + CS]),
                e.engine_nop())[-1]), [("pe", tb)], "ve")
            t_tc[i] = t_abc
            h1ws[i] = []

        def emit_h1pair(i, ch0):
            xs = xslice(i)
            sl = H1(i)
            for ch in (ch0, ch0 + 1):
                tt = E("vector", (lambda e, c=ch, xs_=xs, s_=sl: (
                    e.tensor_tensor(out=s_[:, c * CS:(c + 1) * CS],
                                    in0=xs_[:, c, :], in1=abcC[:], op=ALU.mult),
                    e.drain(),
                    e.tensor_tensor(out=s_[:, c * CS:(c + 1) * CS],
                                    in0=s_[:, c * CS:(c + 1) * CS],
                                    in1=bbcC[:], op=ALU.add))[-1]),
                    [("ve", t_tc[i])], "ve")
                ta = E("scalar", (lambda e, c=ch, s_=sl: e.activation(
                    out=s_[:, c * CS:(c + 1) * CS],
                    in_=s_[:, c * CS:(c + 1) * CS],
                    func=AF.Identity, bias=mcol(SA, c), scale=mcol(CA1, c))),
                    [("ve", tt)], "se")
                h1ws[i].append(("se", ta))
                t_h1k[(i, ch // 2)] = tt
                if ch == NCH - 1:
                    t_h1tt[i] = tt

        emit_lnh1_stats(0)
        emit_lnh1_bcast(0)
        for c0 in range(0, NCH, 2):
            emit_h1pair(0, c0)
        for s in range(NSL):
            if s + 1 < NSL:
                emit_lnh1_stats(s + 1)
            sl = H1(s)
            h1w = h1ws[s]
            for ech in range(NCH):
                slot = ech % 2
                for dch in range(NCH):
                    w = []
                    if dch == 0:
                        w = list(h1w) + [("wl", 16)]
                        if kq_free[slot]:
                            w.append(("se", kq_free[slot]))
                    tk = E("tensor", (lambda e, d_=dch, ec=ech, sl2=slot, s_=sl:
                           mm(e, psB[:, sl2 * 512:(sl2 + 1) * 512],
                              wk_sb(d_)[:, ec * 128:(ec + 1) * 128],
                              s_[:, d_ * CS:(d_ + 1) * CS], d_ == 0,
                              d_ == NCH - 1)),
                        w, "pe" if dch == NCH - 1 else None)
                tke = E("scalar", (lambda e, ec=ech, sl2=slot, ss=s: e.activation(
                    out=K_sb(ec)[:, ss * CS:(ss + 1) * CS],
                    in_=psB[:, sl2 * 512:(sl2 + 1) * 512],
                    func=AF.Identity, bias=ccol(BK, ec))), [("pe", tk)], "se")
                kq_free[slot] = tke
                t_kev_last[0] = tke
            if s == NSL - 1:
                pe_klast = P.cnt["pe"]
            if s < 4:
                for ech in range(NCH):
                    slot = ech % 2
                    for dch in range(NCH):
                        w = []
                        if dch == 0:
                            w = [("wl", 32), ("se", kq_free[slot])]
                        tq = E("tensor", (lambda e, d_=dch, ec=ech, sl2=slot,
                                          s_=sl: mm(
                            e, psB[:, sl2 * 512:(sl2 + 1) * 512],
                            wq_sb(d_)[:, ec * 128:(ec + 1) * 128],
                            s_[:, d_ * CS:(d_ + 1) * CS], d_ == 0,
                            d_ == NCH - 1)),
                            w, "pe" if dch == NCH - 1 else None)
                    qsl = ech % 2
                    w = [("pe", tq)]
                    if t_qtmp_free[qsl]:
                        w.append((f"q{qsl}", t_qtmp_free[qsl]))
                    tqe = E("scalar", (lambda e, sl2=slot, q_=qsl, ec=ech:
                            e.activation(out=qtmp(q_)[:],
                                         in_=psB[:, sl2 * 512:(sl2 + 1) * 512],
                                         func=AF.Identity, bias=ccol(BQ, ec))),
                            w, "se")
                    kq_free[slot] = tqe
                    td = P.dma("sync", q_hbm[:, ech, s * CS:(s + 1) * CS],
                               qtmp(qsl)[:], [("se", tqe)], sem=f"q{qsl}")
                    t_qtmp_free[qsl] = td
            if s + 1 < NSL:
                emit_lnh1_bcast(s + 1)

            for tt_ in range(4):
                kc = s * 4 + tt_
                vslot = tt_ % 2
                tlast = None
                for vh in range(2):
                    for dch in range(NCH):
                        w = []
                        if dch == 0:
                            w = [("wl", 48)]
                            if v_free[vh]:
                                w.append((v_free_sem[vh], v_free[vh]))
                            if first_v[0]:
                                w.append(("ve", t_mod))
                                first_v[0] = False
                            if vh == 0 and dch == 0:
                                w += h1w
                        E("tensor", (lambda e, d_=dch, t2=tt_, v_=vh, s_=sl: mm(
                            e, psD[:, v_ * 512:(v_ + 1) * 512],
                            s_[:, d_ * CS + t2 * 128:d_ * CS + (t2 + 1) * 128],
                            wv_sb(d_)[:, v_ * 512:(v_ + 1) * 512],
                            d_ == 0, False)), w, None)
                    tv = E("tensor", (lambda e, v_=vh: mm(
                        e, psD[:, v_ * 512:(v_ + 1) * 512], oner[:],
                        bvrb[0:1, v_ * 512:(v_ + 1) * 512], False, True)),
                        [("lds", 64), ("ldg", 64), ("ve", t_ones)], "pe")
                    tlast = tv
                    for j in range(8):
                        h = vh * 8 + j
                        w = [("pe", tv)]
                        if j == 0 and t_vstage_free[vslot]:
                            w.append((f"v{vslot}", t_vstage_free[vslot]))
                        if vh == 0:
                            tve = E("scalar", (lambda e, h_=h, j_=j, v_=vh,
                                               sl2=vslot, k_=kc: e.activation(
                                out=vtl(sl2)[:, h_ * 65:h_ * 65 + 64],
                                in_=psD[:, v_ * 512 + j_ * 64:
                                        v_ * 512 + (j_ + 1) * 64],
                                func=AF.Identity, scale=btcol(h_, k_))), w, "se")
                        else:
                            tve = E("vector", (lambda e, h_=h, j_=j, v_=vh,
                                               sl2=vslot, k_=kc: e.tensor_scalar(
                                out=vtl(sl2)[:, h_ * 65:h_ * 65 + 64],
                                in0=psD[:, v_ * 512 + j_ * 64:
                                        v_ * 512 + (j_ + 1) * 64],
                                scalar1=btcol(h_, k_), scalar2=None,
                                op0=ALU.mult)),
                                w, "ve" if j == 7 else None, nodrain=(j != 7))
                    v_free[vh] = tve
                    v_free_sem[vh] = "se" if vh == 0 else "ve"
                t_vlast[s] = tlast
                tbc = E("vector", (lambda e, sl2=vslot, k_=kc: e.tensor_copy(
                    out=vtl(sl2)[:].rearrange("p (h c) -> p h c", h=16)[:, :, 64:65],
                    in_=btab[:].rearrange("p (h c) -> p h c", h=16)[:, :, k_:k_ + 1])),
                    [("lds", 64), ("ldg", 64), (v_free_sem[0], v_free[0]),
                     (v_free_sem[1], v_free[1])], "ve")
                tvd = P.dma("sync", v_hbm[:, kc, :], vtl(vslot)[:], [("ve", tbc)],
                            sem=f"v{vslot}")
                t_vstage_free[vslot] = tvd
                t_vdma[kc] = tvd
                if kc == 7:
                    v_snap0 = (P.cnt["v0"], P.cnt["v1"])
                if kc == 15:
                    v_snap1 = (P.cnt["v0"], P.cnt["v1"])
                if s + 1 < NSL:
                    emit_h1pair(s + 1, 2 * tt_)
                    if s + 2 < NSL and s + 2 > 4:
                        emit_xload_pair(s + 2, tt_)
            if s + 2 == 4:
                emit_xload(4)

        xsL = x2r[:, :, OWN:TOT]
        t_abcL, _, _ = emit_ln(
            xsL, LORA,
            [("lds", 64), ("ldg", 64), ("ve", t_mod),
             ("pe", t_vlast[NSL - 1]), ("ve", t_rows[NSL - 1]),
             ("ve", t_h1tt[NSL - 1])],
            [("ve", t_tc[NSL - 1])])
        t_h1La = []
        for ch in range(NCH):
            tt = E("vector", (lambda e, c=ch: (
                e.tensor_tensor(out=h1s[:, c * LORA:(c + 1) * LORA],
                                in0=xsL[:, c, :], in1=abcC[:, 0:LORA],
                                op=ALU.mult),
                e.drain(),
                e.tensor_tensor(out=h1s[:, c * LORA:(c + 1) * LORA],
                                in0=h1s[:, c * LORA:(c + 1) * LORA],
                                in1=bbcC[:, 0:LORA], op=ALU.add))[-1]),
                [("ve", t_abcL)], "ve")
            ta = E("scalar", (lambda e, c=ch: e.activation(
                out=h1s[:, c * LORA:(c + 1) * LORA],
                in_=h1s[:, c * LORA:(c + 1) * LORA],
                func=AF.Identity, bias=mcol(SA, c), scale=mcol(CA1, c))),
                [("ve", tt)], "se")
            t_h1La.append(ta)
            if ch == NCH - 1:
                t_h1L_tt = tt
        for ech in range(NCH):
            slot = ech % 2
            for dch in range(NCH):
                w = []
                if dch == 0:
                    w = [("se", t) for t in t_h1La] + \
                        ([("se", kq_free[slot])] if kq_free[slot] else [])
                tq = E("tensor", (lambda e, d_=dch, ec=ech, sl=slot: mm(
                    e, psB[:, sl * 512:sl * 512 + LORA],
                    wq_sb(d_)[:, ec * 128:(ec + 1) * 128],
                    h1s[:, d_ * LORA:(d_ + 1) * LORA], d_ == 0, d_ == NCH - 1)),
                    w, "pe" if dch == NCH - 1 else None)
            qsl = ech % 2
            w = [("pe", tq)]
            if t_qtmp_free[qsl]:
                w.append((f"q{qsl}", t_qtmp_free[qsl]))
            tqe = E("scalar", (lambda e, sl=slot, q_=qsl, ec=ech: e.activation(
                out=qtmp(q_)[:, 0:LORA], in_=psB[:, sl * 512:sl * 512 + LORA],
                func=AF.Identity, bias=ccol(BQ, ec))), w, "se")
            kq_free[slot] = tqe
            td = P.dma("sync", q_hbm[:, ech, OWN:TOT], qtmp(qsl)[:, 0:LORA],
                       [("se", tqe)], sem=f"q{qsl}")
            t_qtmp_free[qsl] = td

        pe_C = P.cnt["pe"]
        se_C = P.cnt["se"]
        ve_C = P.cnt["ve"]
        q_tot = (P.cnt["q0"], P.cnt["q1"])
        v_tot = (P.cnt["v0"], P.cnt["v1"])
        t_dbgk = 0
        if dbg:
            t_dbgk = P.dma("sync", dbg_k[:], bigA[:], [("se", t_kev_last[0])],
                           sem="db")

        t_wo = P.dma("gpsimd", wo_sb.rearrange("p (c m) -> p c m", c=NCH), wox[:],
                     [("pe", t_vlast[NSL - 1])], sem="wo")
        qtw = [WQT] * 4 + [LORA]
        stgb = bigB[:, 18592:22688]
        P.stage_prev = 0
        qq_base = [(bigD, 12288), (bigB, 2080)]
        t_q_lastuse_slot = [pe_C, pe_C]
        t_qq_slot = [0, 0]
        t_acc_free = [ve_C] * 4
        t_acc_free_sem = ["ve"] * 4
        t_qq_lastS = [pe_C]
        t_exp_lastA = [0]
        t_exp_lastB = [0]
        t_qq = 0
        t_qq_lastuse = 0
        P.t_rs_prev = 0
        P.t_bc_prev = [0, 0]
        t_x2w = []
        t_rlapply = []
        t_lsst_free = 0
        P.vtb_last = [pe_klast, pe_C]
        t_S_gate = {"C": [("se", se_C), ("ve", ve_C)], "D": [("se", se_C)]}
        t_P_free = {}

        for qt in range(5):
            W = qtw[qt]
            qoff = qt * WQT if qt < 4 else OWN
            qslot = qt % 2
            qsb, qob = qq_base[qslot]
            if qt == 0:
                w = [("pe", pe_C), ("se", se_C), ("ve", ve_C),
                     ("q0", q_tot[0]), ("q1", q_tot[1]),
                     ("v0", v_tot[0]), ("v1", v_tot[1])]
            else:
                w = [("pe", t_q_lastuse_slot[qslot])]
                if t_qq_slot[qslot]:
                    w.append(("qq", t_qq_slot[qslot]))
            t_qq = P.dma("gpsimd",
                         qsb[:, qob:qob + 4096].rearrange(
                             "p (c t) -> p c t", c=NCH)[:, :, 0:W],
                         q_hbm[:, :, qoff:qoff + W], w, sem="qq")
            t_qq_slot[qslot] = t_qq
            if qt < 4:
                for i_ in range(4):
                    pgi = qt * 4 + i_
                    tl = 0
                    for piece in range(4):
                        w = []
                        if piece == 0:
                            w = ([("p1", P.stage_prev)] if P.stage_prev
                                 else [("pe", pe_C)])
                        if pgi < 8:
                            tl = P.dma(
                                "gpsimd",
                                stgb[:, piece * 1024:(piece + 1) * 1024]
                                .rearrange("p (d e) -> p d e", d=NCH),
                                w1x[pgi, :, piece, :, :], w, sem="p0")
                        else:
                            tl = P.dma(
                                "gpsimd",
                                stgb[:, piece * 1024:(piece + 1) * 1024]
                                .rearrange("p (k o) -> p k o", k=8),
                                w2x[pgi - 8, :, piece * 8:(piece + 1) * 8, :],
                                w, sem="p0")
                    tgt = w1e[pgi] if pgi < 8 else w2r[pgi - 8]
                    P.dma("sync", tgt[:, 0:2048], stgb[:, 0:2048],
                          [("p0", tl)], sem="p1")
                    P.stage_prev = P.dma("sync", tgt[:, 2048:4096],
                                         stgb[:, 2048:4096], [], sem="p1")
            t_lq_all = []
            t_rc_pg = []
            for pg in range(4):
                jA, jB = 2 * pg, 2 * pg + 1
                t_vtb = [0, 0]

                def vload(kcb, extra=(), pg_=pg):
                    slot = kcb % 2
                    t_vtb[slot] = P.dma(
                        "gpsimd",
                        vtb(slot)[:, 0:KCB * 260].rearrange(
                            "p (k c) -> p k c", k=KCB),
                        v_hbm[:, kcb * KCB:(kcb + 1) * KCB,
                              pg_ * 260:(pg_ + 1) * 260],
                        [("pe", P.vtb_last[slot])] + list(extra), sem=f"t{slot}")
                first = qt == 0 and pg == 0
                if qt != 4:
                    vload(0, [("v0", v_snap0[0]), ("v1", v_snap0[1])]
                          if first else [])
                    vload(1, [("v0", v_snap1[0]), ("v1", v_snap1[1])]
                          if first else [])
                t_exp = {}
                t_expk = {}
                t_evac = {}
                if qt == 4:
                    def vload(kcb, extra=(), pg_=pg):
                        slot = kcb % 2
                        t_vtb[slot] = P.dma(
                            "gpsimd",
                            vtb(slot)[:, 0:KCB * 260].rearrange(
                                "p (k c) -> p k c", k=KCB),
                            v_hbm[:, kcb * KCB:(kcb + 1) * KCB,
                                  pg_ * 260:(pg_ + 1) * 260],
                            [("pe", P.vtb_last[slot])] + list(extra),
                            sem=f"t{slot}")
                    t_vtb = [0, 0]
                    ex0 = []
                    vload(0, ex0)
                    vload(1, ex0)
                    t_sl = {}
                    for (pair, j) in (("A", jA), ("B", jB)):
                        ps = psC if pair == "A" else psD
                        w = list(t_S_gate["C" if pair == "A" else "D"])
                        t_S_gate["C" if pair == "A" else "D"] = []
                        w.append(("qq", t_qq))
                        tok = 0
                        for kc in range(KCN):
                            tok = E("tensor", (lambda e, p_=ps, j_=j, k_=kc,
                                               q_=qsb, qo_=qob: (
                                mm(e, p_[:, k_ * 16:k_ * 16 + 16],
                                   K_sb(j_)[0:64, k_ * 128:(k_ + 1) * 128],
                                   q_[0:64, qo_ + j_ * 512:qo_ + j_ * 512 + 16],
                                   True, True, (0, 0)),
                                mm(e, p_[:, 512 + k_ * 16:512 + k_ * 16 + 16],
                                   K_sb(j_)[64:128, k_ * 128:(k_ + 1) * 128],
                                   q_[64:128, qo_ + j_ * 512:qo_ + j_ * 512 + 16],
                                   True, True, (64, 0)))[-1]),
                                w if kc == 0 else [], "pe" if kc == KCN - 1 else None)
                        poff = 0 if pair == "A" else 4096
                        w2 = [("pe", tok)]
                        pfree = t_P_free.get((pair, 0))
                        if pfree:
                            w2.append(("pe", pfree))
                        texp = E("scalar", (lambda e, p_=ps, po=poff: e.activation(
                            out=bigD[:, po:po + 1024], in_=p_,
                            func=AF.Exp, scale=0.125)), w2, "se")
                        t_exp[(pair, 0)] = texp
                        t_sl[pair] = texp
                        (t_exp_lastA if pair == "A" else t_exp_lastB)[0] = texp
                        t_qq_lastS[0] = tok
                    tokp = {}
                    for kcb in range(4):
                        kslot = kcb % 2
                        for (pair, j) in (("A", jA), ("B", jB)):
                            pst = psA if pair == "A" else psB
                            poff = 0 if pair == "A" else 4096
                            for i in range(KCB):
                                kc = kcb * KCB + i
                                for hh in range(2):
                                    h = j * 2 + hh
                                    w = [("se", t_sl[pair]),
                                         (f"t{kslot}", t_vtb[kslot])]
                                    ai = (0 if pair == "A" else 2) + hh
                                    if kc == 0 and t_acc_free[ai]:
                                        w.append((t_acc_free_sem[ai],
                                                  t_acc_free[ai]))
                                    hl = h - 4 * pg
                                    tokp[pair] = E("tensor", (lambda e, p_=pst,
                                            o_=hh * 512, h_=hl, k_=kc, hh_=hh,
                                            ks=kslot, po=poff: mm(
                                        e, p_[0:65, o_:o_ + 16],
                                        vtb(ks)[:, (k_ % KCB) * 260 + h_ * 65:
                                                (k_ % KCB) * 260 + h_ * 65 + 65],
                                        bigD[:, po + hh_ * 512 + k_ * 16:
                                             po + hh_ * 512 + k_ * 16 + 16],
                                        k_ == 0, k_ == KCN - 1)),
                                        w, "pe" if (hh == 1 and i == KCB - 1)
                                        else None)
                        P.vtb_last[kslot] = tokp["B"]
                        if kcb + 2 < 4:
                            vload(kcb + 2)
                    t_P_free[("A", 0)] = tokp["A"]
                    t_P_free[("B", 0)] = tokp["B"]
                    for (pair, j) in (("A", jA), ("B", jB)):
                        tok = tokp[pair]
                        base = 0 if pair == "A" else 2
                        pst = psA if pair == "A" else psB

                        def evac(e, pst=pst, j=j, base=base, W=W):
                            for hh in range(2):
                                if hh == 0:
                                    dst = attq(j)[0:64, 0:W]
                                else:
                                    dst = attq(j)[64:128, 0:W]
                                e.tensor_copy(out=dst,
                                              in_=pst[0:64, hh * 512:hh * 512 + W])
                                e.tensor_copy(
                                    out=lsst[64:65, (base + hh) * 512:
                                             (base + hh) * 512 + W],
                                    in_=pst[64:65, hh * 512:hh * 512 + W])
                            return e.engine_nop()
                        tev = E("vector", evac,
                                [("pe", tok)] +
                                ([("rs", t_lsst_free)] if t_lsst_free else []),
                                "ve")
                        t_acc_free[base] = tev
                        t_acc_free[base + 1] = tev
                        t_acc_free_sem[base] = "ve"
                        t_acc_free_sem[base + 1] = "ve"
                        t_evac[pair] = tev
                elif True:
                    prev_pv = {}

                    def emit_S(kc, pair, j):
                        snm = "C" if pair == "A" else "D"
                        ps = psC if pair == "A" else psD
                        w = list(t_S_gate[snm])
                        t_S_gate[snm] = []
                        pr = t_exp.get((pair, kc - 1))
                        if pr:
                            w.append(("se", pr))
                        if kc == 0:
                            w.append(("qq", t_qq))
                        return E("tensor", (lambda e, p_=ps, j_=j, k_=kc, W_=W,
                                            q_=qsb, qo_=qob: (
                            mm(e, p_[:, 0:W_],
                               K_sb(j_)[0:64, k_ * 128:(k_ + 1) * 128],
                               q_[0:64, qo_ + j_ * 512:qo_ + j_ * 512 + W_],
                               True, True, (0, 0)),
                            mm(e, p_[:, 512:512 + W_],
                               K_sb(j_)[64:128, k_ * 128:(k_ + 1) * 128],
                               q_[64:128, qo_ + j_ * 512:qo_ + j_ * 512 + W_],
                               True, True, (64, 0)))[-1]),
                            w, "pe")

                    def emit_exp(kc, pair, ts_):
                        w = [("pe", ts_)]
                        pfree = t_P_free.get((pair, kc % 2))
                        if pfree:
                            w.append(("pe", pfree))
                        poff = (0 if pair == "A" else 4096) + (kc % 2) * 2048
                        ps = psC if pair == "A" else psD
                        texp = E("scalar", (lambda e, p_=ps, po=poff:
                                e.activation(
                            out=bigD[:, po:po + 1024], in_=p_,
                            func=AF.Exp, scale=0.125)), w, "se")
                        t_exp[(pair, kc)] = texp

                    def emit_PV(kc, pair, j, kslot):
                        pst = psA if pair == "A" else psB
                        tok = 0
                        for hh in range(2):
                            h = j * 2 + hh
                            w = [("se", t_exp[(pair, kc)]),
                                 (f"t{kslot}", t_vtb[kslot])]
                            ai = (0 if pair == "A" else 2) + hh
                            if kc == 0 and t_acc_free[ai]:
                                w.append((t_acc_free_sem[ai], t_acc_free[ai]))
                            poff = (0 if pair == "A" else 4096) + (kc % 2) * 2048
                            tok = E("tensor", (lambda e, p_=pst, o_=hh * 512,
                                               h_=h - 4 * pg,
                                               k_=kc, hh_=hh, W_=W, ks=kslot,
                                               po=poff: mm(
                                e, p_[0:65, o_:o_ + W_],
                                vtb(ks)[:, (k_ % KCB) * 260 + h_ * 65:
                                        (k_ % KCB) * 260 + h_ * 65 + 65],
                                bigD[:, po + hh_ * 512:po + hh_ * 512 + W_],
                                k_ == 0, k_ == KCN - 1)),
                                w, "pe" if hh == 1 else None)
                        t_P_free[(pair, kc % 2)] = tok
                        return tok

                    for kcb in range(4):
                        kslot = kcb % 2
                        for i in range(KCB):
                            kc = kcb * KCB + i
                            for (pair, j) in (("A", jA), ("B", jB)):
                                ts_ = emit_S(kc, pair, j)
                                emit_exp(kc, pair, ts_)
                                if kc >= 1:
                                    pslot = ((kc - 1) // KCB) % 2
                                    prev_pv[pair] = emit_PV(kc - 1, pair, j, pslot)
                                    if pair == "B" and (kc - 1) % KCB == KCB - 1:
                                        bdone = (kc - 1) // KCB
                                        P.vtb_last[bdone % 2] = prev_pv["B"]
                                        if bdone + 2 < 4:
                                            vload(bdone + 2)
                    for (pair, j) in (("A", jA), ("B", jB)):
                        tok = emit_PV(KCN - 1, pair, j, ((KCN - 1) // KCB) % 2)
                        base = 0 if pair == "A" else 2
                        pst = psA if pair == "A" else psB

                        def evac(e, pst=pst, j=j, base=base, W=W):
                            for hh in range(2):
                                if hh == 0:
                                    dst = attq(j)[0:64, 0:W]
                                else:
                                    dst = attq(j)[64:128, 0:W]
                                e.tensor_copy(out=dst,
                                              in_=pst[0:64, hh * 512:hh * 512 + W])
                                e.tensor_copy(
                                    out=lsst[64:65, (base + hh) * 512:
                                             (base + hh) * 512 + W],
                                    in_=pst[64:65, hh * 512:hh * 512 + W])
                            return e.engine_nop()
                        tev = E("vector", evac,
                                [("pe", tok)] +
                                ([("rs", t_lsst_free)] if t_lsst_free else []),
                                "ve")
                        t_acc_free[base] = tev
                        t_acc_free[base + 1] = tev
                        t_acc_free_sem[base] = "ve"
                        t_acc_free_sem[base + 1] = "ve"
                        t_evac[pair] = tev
                    P.vtb_last[3 % 2] = tok
                w = [("ve", t_evac["A"]), ("ve", t_evac["B"])]
                if P.t_rs_prev:
                    w.append(("rs", P.t_rs_prev))
                t_rs = P.dma("sync", rl_hbm[qt, 4 * pg:4 * pg + 4, 0:W],
                             lsst[64:65, 0:2048].rearrange(
                                 "p (s t) -> p s t", s=4)[:, :, 0:W], w,
                             sem="rs")
                t_lsst_free = t_rs
                P.t_rs_prev = t_rs
                for jj in range(2):
                    j = 2 * pg + jj
                    w = [("rs", t_rs)]
                    if P.t_bc_prev[j % 2]:
                        w.append((f"lr{j % 2}", P.t_bc_prev[j % 2]))
                    if t_rlapply:
                        w.append(("ve", t_rlapply[-1]))
                    if t_x2w:
                        w.append(("ve", t_x2w[-1]))
                    P.dma("sync", rlb[:, 0:W],
                          rl_hbm[qt, 2 * j:2 * j + 1, 0:W].broadcast_to(
                              (128, W)), w, sem=f"lr{j % 2}")
                    tbc_ = P.dma(
                        "sync", rlb[:, 512:512 + W],
                        rl_hbm[qt, 2 * j + 1:2 * j + 2, 0:W].broadcast_to(
                            (128, W)), [], sem=f"lr{j % 2}")
                    P.t_bc_prev[j % 2] = tbc_
                    if pg == 3:
                        def _rcs(e, W_=W):
                            e.activation(out=rlb[0:64, 0:W_],
                                         in_=rlb[0:64, 0:W_], func=AF.Ln)
                            e.activation(out=rlb[64:128, 512:512 + W_],
                                         in_=rlb[64:128, 512:512 + W_],
                                         func=AF.Ln)
                            e.drain()
                            e.activation(out=rlb[0:64, 0:W_],
                                         in_=rlb[0:64, 0:W_],
                                         func=AF.Exp, scale=-1.0)
                            return e.activation(
                                out=rlb[64:128, 512:512 + W_],
                                in_=rlb[64:128, 512:512 + W_],
                                func=AF.Exp, scale=-1.0)
                        trc_ = E("scalar", _rcs,
                                 [(f"lr{j % 2}", tbc_)], "se")
                        wtap = [("se", trc_)]

                        def _tap(e, j_=j, W_=W):
                            e.tensor_tensor(out=attq(j_)[0:64, 0:W_],
                                            in0=attq(j_)[0:64, 0:W_],
                                            in1=rlb[0:64, 0:W_], op=ALU.mult)
                            return e.tensor_tensor(
                                out=attq(j_)[64:128, 0:W_],
                                in0=attq(j_)[64:128, 0:W_],
                                in1=rlb[64:128, 512:512 + W_], op=ALU.mult)
                        tap_ = E("vector", _tap, wtap, "ve")
                    else:
                        def _tap(e, j_=j, W_=W):
                            with nc.allow_low_precision(
                                    "softmax denominators kept bf16"):
                                e.reciprocal(out=rlb[0:64, 0:W_],
                                             in_=rlb[0:64, 0:W_])
                                e.reciprocal(out=rlb[64:128, 512:512 + W_],
                                             in_=rlb[64:128, 512:512 + W_])
                            e.drain()
                            e.tensor_tensor(out=attq(j_)[0:64, 0:W_],
                                            in0=attq(j_)[0:64, 0:W_],
                                            in1=rlb[0:64, 0:W_], op=ALU.mult)
                            return e.tensor_tensor(
                                out=attq(j_)[64:128, 0:W_],
                                in0=attq(j_)[64:128, 0:W_],
                                in1=rlb[64:128, 512:512 + W_], op=ALU.mult)
                        tap_ = E("vector", _tap, [(f"lr{j % 2}", tbc_)], "ve")
                    t_rlapply.append(tap_)
            rlw = [("ve", t) for t in t_rlapply[-8:]]
            t_ty_free = 0
            tty = 0
            for ech in range(NCH):
                for vch in range(NCH):
                    w = []
                    if vch == 0:
                        w = (rlw if ech == 0 else []) + [("wo", 16)]
                        if ech == 0:
                            w.append(("ve", t_evac["A"]))
                        if t_ty_free:
                            w.append(("se", t_ty_free))
                    tw = E("tensor", (lambda e, v_=vch, ec=ech, W_=W: mm(
                        e, psA[:, 0:W_],
                        wo_sb[:, v_ * 1024 + ec * 128:v_ * 1024 + (ec + 1) * 128],
                        attq(v_)[:, 0:W_], v_ == 0, v_ == NCH - 1)),
                        w, "pe" if vch == NCH - 1 else None)
                w = [("pe", tw)]
                prev_x2 = t_x2w[-2] if (len(t_x2w) >= 2 and ech >= 2) else 0
                if prev_x2:
                    w.append(("ve", prev_x2))
                tsl = ech % 2
                tty = E("scalar", (lambda e, W_=W, ec=ech, sl=tsl: e.activation(
                    out=tyv(sl)[:, 0:W_], in_=psA[:, 0:W_], func=AF.Identity,
                    bias=ccol(BO, ec))), w, "se")
                t_ty_free = tty
                tx2 = E("vector", (lambda e, ec=ech, W_=W, qo=qoff, sl=tsl:
                        e.scalar_tensor_tensor(
                            out=x2v(ec)[:, qo:qo + W_], in0=tyv(sl)[:, 0:W_],
                            scalar=mcol(GA, ec), in1=x2v(ec)[:, qo:qo + W_],
                            op0=ALU.mult, op1=ALU.add)), [("se", tty)], "ve")
                t_x2w.append(tx2)
            t_q_lastuse_slot[qslot] = t_qq_lastS[0]
            t_S_gate = {"C": [("se", t_exp_lastA[0])],
                        "D": [("se", t_exp_lastB[0])]}
            t_acc_free = [tty, tty, 0, 0]
            t_acc_free_sem = ["se", "se", "ve", "ve"]
            t_acc_free[2] = t_evac["B"]
            t_acc_free[3] = t_evac["B"]
            t_P_free = {}

        pe_att = P.cnt["pe"]
        se_att = P.cnt["se"]
        stage_done = P.cnt["p1"]
        x2done = [("ve", P.cnt["ve"])]
        if dbg:
            P.dma("sync", dbg_x2[:], x2[:], list(x2done), sem="db")

        xsL2 = x2r[:, :, OWN:TOT]
        t_abc2, _, _ = emit_ln(xsL2, LORA, list(x2done), [])
        t_h2la = []
        for ch in range(NCH):
            tt = E("vector", (lambda e, c=ch: (
                e.tensor_tensor(out=h1s[:, c * LORA:(c + 1) * LORA],
                                in0=xsL2[:, c, :], in1=abcC[:, 0:LORA],
                                op=ALU.mult),
                e.drain(),
                e.tensor_tensor(out=h1s[:, c * LORA:(c + 1) * LORA],
                                in0=h1s[:, c * LORA:(c + 1) * LORA],
                                in1=bbcC[:, 0:LORA], op=ALU.add))[-1]),
                [("ve", t_abc2)], "ve")
            ta = E("scalar", (lambda e, c=ch: e.activation(
                out=h2l[:, c * LORA:(c + 1) * LORA],
                in_=h1s[:, c * LORA:(c + 1) * LORA],
                func=AF.Identity, bias=mcol(SBv, c), scale=mcol(CB1, c))),
                [("ve", tt)], "se")
            t_h2la.append(ta)
        if dbg:
            P.dma("sync", dbg_h2l[:], h2l[:], [("se", t) for t in t_h2la], sem="db")
        t_odds = E("vector", (lambda e: e.tensor_scalar(
            out=odds[:].rearrange("p (c q) -> p c q", c=NCH),
            in0=h2l[:].rearrange("p (c t) -> p c t", c=NCH)[:, :, 1:16:2],
            scalar1=sclc[:], scalar2=None, op0=ALU.mult)),
            [("se", t) for t in t_h2la] + [("lds", 64), ("ldg", 64)], "ve")
        ttk = 0
        for ch in range(NCH):
            w = [("se", t_h2la[ch]), ("lds", 64), ("ldg", 64)]
            if ch == 0:
                w += [("pe", pe_att), ("se", se_att)]
            ttk = E("tensor", (lambda e, c=ch: e.transpose(
                psS[:].bitcast(BF16)[0:16, 2048 + c * 128:2176 + c * 128],
                h2l[:, c * LORA:(c + 1) * LORA], eye[:])), w,
                "pe" if ch == NCH - 1 else None)
        t_trcp = E("vector", (lambda e: e.tensor_copy(
            out=trows[:],
            in_=psS[:].bitcast(BF16)[0:16, 2048:2048 + NCH * 128])),
            [("pe", ttk)], "ve")
        t_tr = [t_trcp] * NCH
        t_evr = []
        for p_ in range(8):
            dst = evrS if p_ < 4 else evrR
            off = (p_ % 4) * 1024
            te_ = P.dma("sync", dst[:, off:off + 1024].rearrange(
                "q (c t) -> q c t", c=NCH),
                trows[:].rearrange("q (c t) -> q c t", c=NCH)[2 * p_:2 * p_ + 1, :, :],
                [("ve", t_tr[-1])], sem="ev")
            t_evr.append(te_)
        t_evsc = E("vector", (lambda e: e.tensor_scalar(
            out=evrS[:], in0=evrS[:], scalar1=sclc[0:1, :], scalar2=None,
            op0=ALU.mult)),
            [("ev", t_evr[-1]), ("lds", 64), ("ldg", 64)], "ve")
        t_ebccp = 0
        t_ebc = []
        for i in range(4):
            w = [("ev", t_evr[-1])]
            if i == 0:
                w.append(("ve", P.cnt["ve"]))
            else:
                w.append(("ve", t_ebccp))
            tb = E("tensor", (lambda e, i_=i: (
                mm(e, psC[:, 0:512], oner[:], evrR[:, i_ * 1024:i_ * 1024 + 512],
                   True, True),
                mm(e, psC[:, 512:1024], oner[:],
                   evrR[:, i_ * 1024 + 512:(i_ + 1) * 1024], True, True))[-1]),
                w, "pe")
            t_ebccp = E("vector", (lambda e, i_=i: e.tensor_copy(
                out=ebc(i_)[:], in_=psC)), [("pe", tb)], "ve")
            t_ebc.append(t_ebccp)

        t_wsl_free = [pe_att, pe_att, pe_att]
        t_wst = [0, 0, 0]
        ssem = ("h0", "h1", "h2")
        t_ttf = [0, 0]
        t_out_free = [0, 0]
        t_tf_free = [0, 0]
        t_g_done = {}
        t_gelu_last = 0
        for fs in range(NFS):
            xs = x2r[:, :, fs * FS:(fs + 1) * FS]
            fs_bar = [("se", P.cnt["se"]), ("ve", P.cnt["ve"])] if fs > 0 else []
            w = list(x2done) + [("ve", t_ebc[-1]), ("ve", t_evsc)] if fs == 0 \
                else [("se", t_gelu_last)] + fs_bar
            txq = E("vector", (lambda e, xs_=xs: e.tensor_tensor(
                out=bigB[:, 0:8192].rearrange("p (c t) -> p c t", c=NCH),
                in0=xs_, in1=xs_, op=ALU.mult)), w, "ve")
            tm = None
            for half in range(2):
                for dch in range(NCH):
                    w = []
                    if half == 0 and dch == 0:
                        w = [("ve", txq)] + (list(x2done) if fs == 0 else fs_bar)
                    tm = E("tensor", (lambda e, d_=dch, h_=half, xs_=xs: mm(
                        e, psD[0:1, h_ * 512:(h_ + 1) * 512], onec[:],
                        xs_[:, d_, h_ * 512:(h_ + 1) * 512], d_ == 0,
                        d_ == NCH - 1)), w,
                        "pe" if dch == NCH - 1 else None)
            tmtok = tm
            ts2 = None
            for half in range(2):
                for dch in range(NCH):
                    ts2 = E("tensor", (lambda e, d_=dch, h_=half: mm(
                        e, psC[0:1, h_ * 512:(h_ + 1) * 512], onec[:],
                        bigB[:, d_ * 1024 + h_ * 512:d_ * 1024 + (h_ + 1) * 512],
                        d_ == 0, d_ == NCH - 1)), [],
                        "pe" if dch == NCH - 1 else None)

            def frows(e):
                e.tensor_scalar(out=rrow(0), in0=psD[0:1, 0:1024],
                                scalar1=1.0 / D, scalar2=None, op0=ALU.mult)
                e.tensor_scalar(out=rrow(1), in0=psC[0:1, 0:1024],
                                scalar1=1.0 / D, scalar2=None, op0=ALU.mult)
                e.drain()
                e.tensor_tensor(out=rrow(3), in0=rrow(0), in1=rrow(0),
                                op=ALU.mult)
                e.drain()
                e.tensor_tensor(out=rrow(1), in0=rrow(1), in1=rrow(3),
                                op=ALU.subtract)
                e.drain()
                e.tensor_scalar(out=rrow(1), in0=rrow(1), scalar1=1e-5,
                                scalar2=None, op0=ALU.add)
                return e.engine_nop()
            tv1 = E("vector", frows, [("pe", tmtok), ("pe", ts2)], "ve")
            def rsqF(e):
                e.activation(out=rrow(3), in_=rrow(1), func=AF.Ln)
                e.drain()
                return e.activation(out=rrow(2), in_=rrow(3),
                                    func=AF.Exp, scale=-0.5)
            ts1 = E("scalar", rsqF, [("ve", tv1)], "se")

            def f2(e):
                e.scalar_tensor_tensor(out=rrow(1), in0=rrow(0), scalar=-1.0,
                                       in1=rrow(2), op0=ALU.mult, op1=ALU.mult)
                return e.engine_nop()
            tr2 = E("vector", f2, [("se", ts1)], "ve")
            tb2 = E("tensor", (lambda e: (
                mm(e, psD[:, 0:512], onerf[:], rrow(2, 512), True, True),
                mm(e, psD[:, 512:1024], onerf[:],
                   rows[0:1, 2 * 1024 + 512:3 * 1024], True, True),
                mm(e, psC[:, 0:512], onerf[:], rrow(1, 512), True, True),
                mm(e, psC[:, 512:1024], onerf[:],
                   rows[0:1, 1024 + 512:2 * 1024], True, True))[-1]),
                [("ve", tr2)], "pe")
            tc2 = E("vector", (lambda e: (
                e.tensor_copy(out=abcF[:, 0:512], in_=psD[:, 0:512]),
                e.tensor_copy(out=abcF[:, 512:1024], in_=psD[:, 512:1024]),
                e.tensor_copy(out=bbcF[:, 0:512], in_=psC[:, 0:512]),
                e.tensor_copy(out=bbcF[:, 512:1024],
                              in_=psC[:, 512:1024]))[-1]),
                [("pe", tb2)], "ve")
            t_h2a = []
            for ch in range(NCH):
                tt = E("vector", (lambda e, c=ch, xs_=xs: (
                    e.tensor_tensor(out=h2_sb(c), in0=xs_[:, c, :], in1=abcF[:],
                                    op=ALU.mult),
                    e.drain(),
                    e.tensor_tensor(out=h2_sb(c), in0=h2_sb(c), in1=bbcF[:],
                                    op=ALU.add))[-1]), [("ve", tc2)], "ve")
                ta = E("scalar", (lambda e, c=ch: e.activation(
                    out=h2_sb(c), in_=h2_sb(c), func=AF.Identity,
                    bias=mcol(SBv, c), scale=mcol(CB1, c))), [("ve", tt)], "se")
                t_h2a.append(ta)
            h2w = [("se", t) for t in t_h2a]
            t_u = []
            tu = None
            for p_ in range(4):
                for half in range(2):
                    for dch in range(NCH):
                        w = []
                        if p_ == 0 and half == 0 and dch == 0:
                            w = h2w + [("ve", tc2)]
                        elif half == 0 and dch == 0:
                            w = [("ve", t_u[p_ - 1])]
                        tu = E("tensor", (lambda e, d_=dch, h_=half, pp=p_: mm(
                            e, psD[0:1, h_ * 512:(h_ + 1) * 512],
                            h2l[:, d_ * LORA + 2 * pp + 1:d_ * LORA + 2 * pp + 2],
                            h2_sb(d_)[:, h_ * 512:(h_ + 1) * 512], d_ == 0,
                            d_ == NCH - 1)), w,
                            "pe" if dch == NCH - 1 else None)
                tuc = E("vector", (lambda e, pp=p_: e.tensor_copy(
                    out=urow[:, pp * 1024:(pp + 1) * 1024],
                    in_=psD[0:1, 0:1024])), [("pe", tu)], "ve")
                t_u.append(tuc)
            uw = [("ve", t) for t in t_u]
            t_wl = 0
            lsem1 = ""
            for hch in range(32):
                wslot = (hch // 4) % 2
                if hch % 4 == 0:
                    gnum = hch // 4
                    w = [("pe", t_wsl_free[wslot])]
                    if t_wst[wslot]:
                        w.append((ssem[wslot], t_wst[wslot]))
                    qa = "gpsimd" if gnum % 2 == 0 else "sync"
                    if fs == 0:
                        w.append(("p1", stage_done))
                    lsem1 = ssem[wslot]
                    t_wl = P.dma(qa, wsl(wslot)[:], w1e[gnum], w,
                                 sem=lsem1)
                    t_wst[wslot] = t_wl
                p_ = hch // 8
                pst = psA if hch % 2 == 0 else psB
                tk = None
                for half in range(2):
                    for dch in range(NCH):
                        w = []
                        if dch == 0 and half == 0:
                            prev = (fs, hch - 2) if hch >= 2 else \
                                ((fs - 1, 30 + hch) if fs > 0 else None)
                            w = [(lsem1, t_wl)]
                            if prev and prev in t_g_done:
                                w.append(("se", t_g_done[prev]))
                            if hch == 0:
                                w += uw + [("ve", t_evsc)]
                                if dbg:
                                    w.append(("db", P.cnt["db"]))
                        E("tensor", (lambda e, d_=dch, h_=half, hc=hch, ws=wslot,
                                     ps_=pst: mm(
                            e, ps_[:, h_ * 512:(h_ + 1) * 512],
                            wsl(ws)[:, (hc % 4) * 1024 + d_ * 128:
                                    (hc % 4) * 1024 + (d_ + 1) * 128],
                            h2_sb(d_)[:, h_ * 512:(h_ + 1) * 512],
                            d_ == 0, False)), w, None)
                    tk = E("tensor", (lambda e, h_=half, hc=hch, pp=p_, ps_=pst:
                           mm(e, ps_[:, h_ * 512:(h_ + 1) * 512],
                              evrS[:, pp * 1024 + (hc % 8) * 128:
                                   pp * 1024 + (hc % 8) * 128 + 128],
                              urow[:, pp * 1024 + h_ * 512:
                                   pp * 1024 + (h_ + 1) * 512],
                              False, True)), [], "pe" if half == 1 else None)
                if hch % 4 == 3:
                    t_wsl_free[wslot] = tk
                tg = E("scalar", (lambda e, hc=hch, ps_=pst: e.activation(
                    out=g_sb(hc), in_=ps_[:], func=AF.Gelu, bias=b1col(hc))),
                    [("pe", tk)], "se")
                t_g_done[(fs, hch)] = tg
                t_gelu_last = tg
            gw = [("se", t_g_done[(fs, hc)]) for hc in range(32)]
            if dbg and fs == 0:
                P.dma("sync", dbg_g[:], bigA[:], gw, sem="db")
            for och in range(NCH):
                ws2 = (och + 2) % 3
                w = [("pe", t_wsl_free[ws2])]
                if t_wst[ws2]:
                    w.append((ssem[ws2], t_wst[ws2]))
                qa2 = "gpsimd" if och % 2 == 0 else "sync"
                qb2 = "sync"
                if fs == 0:
                    lsem2 = f"g{ws2}"
                    w.append(("p1", stage_done))
                    t_w2l = P.dma(qa2, wsl(ws2)[:], w2r[och], w, sem=lsem2)
                else:
                    lsem2 = ssem[ws2]
                    t_w2l = P.dma(qa2, wsl(ws2)[:], w2e[och], w, sem=lsem2)
                    t_wst[ws2] = t_w2l
                t_ed = None
                pmo = och // 2
                if fs == 0:
                    for hch in range(32):
                        li = hch // 8
                        w = [(f"g{ws2}", t_w2l)]
                        if hch == 0:
                            w.append(("ve", t_odds))
                            w.append(("ve", t_ebc[-1]))
                        t_ed = E("vector", (lambda e, hc=hch, ws=ws2, oc=och,
                                            l_=li, pm=pmo: e.scalar_tensor_tensor(
                            out=wsl(ws)[:, hc * 128:(hc + 1) * 128],
                            in0=ebc(pm)[:, (oc % 2) * 512 + l_:
                                        (oc % 2) * 512 + l_ + 509:4],
                            scalar=odds[:, (hc % 8) * 8 + 4 + pm:
                                        (hc % 8) * 8 + 5 + pm],
                            in1=wsl(ws)[:, hc * 128:(hc + 1) * 128],
                            op0=ALU.mult, op1=ALU.add)),
                            w, "ve" if hch == 31 else None,
                            nodrain=(hch != 31))
                    t_wst[ws2] = P.dma(qb2, w2e[och], wsl(ws2)[:],
                                       [("ve", t_ed)], sem=ssem[ws2])
                pst2 = psC if och % 2 == 0 else psD
                tk = None
                for half in range(2):
                    for hch in range(32):
                        w = []
                        if hch == 0 and half == 0:
                            w = gw + ([("ve", t_ed)] if t_ed else
                                      [(lsem2, t_w2l)])
                            if t_ttf[och % 2]:
                                w.append(("se", t_ttf[och % 2]))
                        tk = E("tensor", (lambda e, hc=hch, h_=half, ws=ws2,
                                          ps_=pst2: mm(
                            e, ps_[:, h_ * 512:(h_ + 1) * 512],
                            wsl(ws)[:, hc * 128:(hc + 1) * 128],
                            g_sb(hc)[:, h_ * 512:(h_ + 1) * 512],
                            hc == 0, hc == 31)), w,
                            "pe" if hch == 31 else None)
                t_wsl_free[ws2] = tk
                w = [("pe", tk)]
                if t_tf_free[och % 2]:
                    w.append(("ve", t_tf_free[och % 2]))
                ttf = E("scalar", (lambda e, oc=och, sl=och % 2, ps_=pst2:
                        e.activation(
                    out=tfs(sl)[:], in_=ps_, func=AF.Identity,
                    bias=ccol(B2, oc))), w, "se")
                t_ttf[och % 2] = ttf
                w = [("se", ttf)]
                if t_out_free[och % 2]:
                    w.append((f"o{och % 2}", t_out_free[och % 2]))
                tout = E("vector", (lambda e, oc=och, sl=och % 2, f_=fs:
                         e.scalar_tensor_tensor(
                             out=outsl(sl)[:], in0=tfs(sl)[:],
                             scalar=mcol(GB, oc),
                             in1=x2v(oc)[:, f_ * FS:(f_ + 1) * FS],
                             op0=ALU.mult, op1=ALU.add)), w, "ve")
                t_tf_free[och % 2] = tout
                tdo = P.dma("sync", out_t[:, och, fs * FS:(fs + 1) * FS],
                            outsl(och % 2)[:], [("ve", tout)], sem=f"o{och % 2}")
                t_out_free[och % 2] = tdo

        dma_sems = ("lds", "ldg", "wl", "wo", "x0", "x1", "q0", "q1", "v0",
                    "v1", "t0", "t1", "qq", "lr", "lr0", "lr1", "ev", "o0",
                    "o1", "g0", "g1", "g2", "h0", "h1", "h2", "rs", "db")
        E("sync", lambda e: e.nop(),
          [(sname, P.cnt[sname]) for sname in dma_sems], None)

        with nc.Block() as block:
            def mk(engname):
                own = {"tensor": "pe", "vector": "ve", "scalar": "se"}.get(engname)

                def f(engine):
                    last = {}
                    for fn, w, inc, n, drain in P.ops[engname]:
                        for sname, v in w:
                            if sname == own:
                                continue
                            if last.get(sname, -1) >= v:
                                continue
                            engine.wait_ge(sems[sname], v)
                            last[sname] = v
                        ins = fn(engine)
                        if drain:
                            engine.drain()
                        if inc is not None:
                            ins.then_inc(sems[inc], n)
                return f
            block.sync(mk("sync"))
            block.gpsimd(mk("gpsimd"))
            block.vector(mk("vector"))
            block.scalar(mk("scalar"))
            block.tensor(mk("tensor"))
    return nc


def _shard_inputs(inputs):
    x = np.ascontiguousarray(np.asarray(inputs["L0_attn_x"], dtype=np.float32))
    Wq = np.asarray(inputs["L0_attn_Wq"], np.float32)
    bq = np.asarray(inputs["L0_attn_bq"], np.float32)
    Wk = np.asarray(inputs["L0_attn_Wk"], np.float32)
    bk = np.asarray(inputs["L0_attn_bk"], np.float32)
    Wv = np.asarray(inputs["L0_attn_Wv"], np.float32)
    bv = np.asarray(inputs["L0_attn_bv"], np.float32)
    Wo = np.asarray(inputs["L0_attn_Wo"], np.float32)
    bo = np.asarray(inputs["L0_attn_bo"], np.float32)
    W1 = np.asarray(inputs["L0_ffn_W1"], np.float32)
    b1 = np.asarray(inputs["L0_ffn_b1"], np.float32)
    W2 = np.asarray(inputs["L0_ffn_W2"], np.float32)
    b2 = np.asarray(inputs["L0_ffn_b2"], np.float32)
    scale = np.asarray(inputs["L0_ffn_scale"], np.float32).reshape(())
    c = np.asarray(inputs["L0_c"], np.float32)
    wmod = np.asarray(inputs["L0_wmod"], np.float32)

    tile8 = lambda w: np.ascontiguousarray(
        w.reshape(8, 128, w.shape[1]).transpose(1, 0, 2))
    wqx = tile8(Wq.T)
    wkx = tile8(Wk.T)
    wvx = tile8(Wv.T)
    wox = tile8(Wo.T)
    w1x = np.ascontiguousarray(
        W1.reshape(8, 4, 128, 8, 128).transpose(0, 4, 1, 3, 2))
    w2x = np.ascontiguousarray(
        W2.reshape(8, 128, 32, 128).transpose(0, 3, 2, 1))
    wmodx = tile8(wmod)
    colchunks = lambda v: np.ascontiguousarray(v.reshape(-1, 128).T)

    slopes = np.power(2.0, np.linspace(-4.0, -0.3, NH)).astype(np.float32)
    sclb = np.ascontiguousarray(np.broadcast_to(scale, (128, 1))).astype(np.float32)
    eyeb = np.eye(128, dtype=np.float32).astype(BFnp)

    maps = []
    for core in range(8):
        b, half = core // 2, core % 2
        o0, o1 = half * OWN, (1 - half) * OWN
        perm = np.concatenate([np.arange(o0, o0 + OWN), np.arange(o1, o1 + OWN)])
        xp = x[b][perm]
        xtx = tile8(np.ascontiguousarray(xp.T))
        xltx = tile8(np.ascontiguousarray(x[b, 1:17].T))
        pos = perm.astype(np.float32)
        posr = pos.reshape(KCN, 128).T
        btabx = np.exp(-(slopes[None, :, None] * posr[:, None, :])
                       ).reshape(128, NH * KCN)
        ccx = np.concatenate([colchunks(bq), colchunks(bk), colchunks(bo),
                              colchunks(b2), colchunks(b1), colchunks(c[b])],
                             axis=1).astype(np.float32)
        maps.append(dict(
            xt=xtx, xlt=xltx, wqx=wqx, wkx=wkx, wvx=wvx, wox=wox,
            w1x=w1x, w2x=w2x, ccx=np.ascontiguousarray(ccx),
            bvr=np.ascontiguousarray(bv.reshape(1, D)), wmodx=wmodx,
            btabx=np.ascontiguousarray(btabx.astype(np.float32)),
            eyex=eyeb, sclx=sclb))
    return maps


def kernel(**inputs):
    maps = _shard_inputs(inputs)
    nc = build_nc()
    res = run_bass_kernel_spmd(nc, maps, core_ids=list(range(8)))
    out = np.empty((4, LT, D), dtype=np.float32)
    for core in range(8):
        b, half = core // 2, core % 2
        ot = np.asarray(res.results[core]["out_t"])
        blk = ot.transpose(2, 1, 0).reshape(OWN, D)
        out[b, half * OWN:(half + 1) * OWN] = blk
    return out


# version 31 (final)
# speedup vs baseline: 1.0403x; 1.0403x over previous
import contextlib
import numpy as np
import ml_dtypes
import concourse.bass as bass
import concourse.mybir as mybir
from concourse.bass_utils import run_bass_kernel_spmd

F32 = mybir.dt.float32
BF16 = mybir.dt.bfloat16
AF = mybir.ActivationFunctionType
ALU = mybir.AluOpType
BFnp = ml_dtypes.bfloat16

D = 1024
NCH = 8
LT = 4096
OWN = 2048
LORA = 16
TOT = OWN + LORA
NH = 16
HD = 64
CS = 512
NSL = LT // CS
WQT = 512
KCN = 32
KCB = 8
FS = 1024
NFS = OWN // FS

ENGS = ("sync", "gpsimd", "vector", "scalar", "tensor")
SEMS = ("pe", "ve", "se", "lds", "ldg", "wl", "wo", "x0", "x1", "q0", "q1",
        "v0", "v1", "t0", "t1", "qq", "lr", "lr0", "lr1", "ev", "o0", "o1",
        "g0", "g1", "g2", "h0", "h1", "h2", "rs", "db", "p0", "p1")


class Prog:
    def __init__(self):
        self.ops = {e: [] for e in ENGS}
        self.cnt = {s: 0 for s in SEMS}

    def emit(self, eng, fn, wait=(), inc=None, n=1, nodrain=False):
        w = tuple((k, int(v)) for (k, v) in wait if v)
        tok = None
        if inc is not None:
            self.cnt[inc] += n
            tok = self.cnt[inc]
        drain = (eng == "vector") and not nodrain
        self.ops[eng].append((fn, w, inc, n, drain))
        return tok

    def dma(self, eng, out, in_, wait=(), sem=None):
        if sem is None:
            sem = "lds" if eng == "sync" else "ldg"
        return self.emit(eng, (lambda e, o=out, i=in_: e.dma_start(out=o, in_=i)),
                         wait, sem, 16, nodrain=True)


def build_nc(dbg=False):
    nc = bass.Bass("TRN2", target_bir_lowering=False)
    dt = lambda name, shape, dtype, kind: nc.dram_tensor(
        name, shape, dtype, kind=kind).ap()

    xt = dt("xt", [128, NCH, LT], F32, "ExternalInput")
    xlt = dt("xlt", [128, NCH, LORA], F32, "ExternalInput")
    wqx = dt("wqx", [128, NCH, D], F32, "ExternalInput")
    wkx = dt("wkx", [128, NCH, D], F32, "ExternalInput")
    wvx = dt("wvx", [128, NCH, D], F32, "ExternalInput")
    wox = dt("wox", [128, NCH, D], F32, "ExternalInput")
    w1x = dt("w1x", [8, 128, 4, NCH, 128], F32, "ExternalInput")
    w2x = dt("w2x", [8, 128, 32, 128], F32, "ExternalInput")
    w1e = dt("w1e", [8, 128, 4096], BF16, "Internal")
    rl_hbm = dt("rl_hbm", [5, 16, 512], BF16, "Internal")
    w2e = dt("w2e", [8, 128, 4096], BF16, "Internal")
    w2r = dt("w2r", [8, 128, 4096], BF16, "Internal")
    ccx = dt("ccx", [128, 72], F32, "ExternalInput")
    bvr = dt("bvr", [1, D], F32, "ExternalInput")
    wmodx = dt("wmodx", [128, NCH, 6 * D], F32, "ExternalInput")
    btabx = dt("btabx", [128, 512], F32, "ExternalInput")
    eyex = dt("eyex", [128, 128], BF16, "ExternalInput")
    sclx = dt("sclx", [128, 1], F32, "ExternalInput")

    kind_scr = "ExternalOutput" if dbg else "Internal"
    q_hbm = dt("q_hbm", [128, NCH, TOT], BF16, kind_scr)
    v_hbm = dt("v_hbm", [128, KCN, NH * 65], BF16, kind_scr)
    out_t = dt("out_t", [128, NCH, OWN], F32, "ExternalOutput")
    dbg_x2 = dt("dbg_x2", [128, NCH * TOT], BF16, kind_scr) if dbg else None
    dbg_mod = dt("dbg_mod", [128, 48], F32, kind_scr) if dbg else None
    dbg_k = dt("dbg_k", [128, 32768], BF16, kind_scr) if dbg else None
    dbg_h2l = dt("dbg_h2l", [128, NCH * LORA], BF16, kind_scr) if dbg else None
    dbg_g = dt("dbg_g", [128, 32768], BF16, kind_scr) if dbg else None

    P = Prog()
    es = contextlib.ExitStack()
    with es:
        sbt = lambda name, shape, dtype: es.enter_context(
            nc.sbuf_tensor(name, shape, dtype))
        x2 = sbt("x2", [128, NCH * TOT], BF16)
        rows = sbt("rows", [1, 4096], F32)
        btab = sbt("btab", [128, 512], F32)
        lq = sbt("lq", [128, 512], BF16)
        rlq = sbt("rlq", [128, 512], BF16)
        rlp = sbt("rlp", [1, 1024], BF16)
        rlb = sbt("rlb", [128, 1024], BF16)
        modc = sbt("modc", [128, 48], F32)
        cc = sbt("cc", [128, 72], F32)
        eye = sbt("eye", [128, 128], BF16)
        onec = sbt("onec", [128, 1], BF16)
        oner = sbt("oner", [1, 128], BF16)
        onerf = sbt("onerf", [1, 128], F32)
        one1 = sbt("one1", [1, 1], BF16)
        h2l = sbt("h2l", [128, NCH * LORA], BF16)
        odds = sbt("odds", [128, 64], F32)
        sil = sbt("sil", [128, 8], BF16)
        sclc = sbt("sclc", [128, 1], F32)
        bigA = sbt("bigA", [128, 32768], BF16)
        bigB = sbt("bigB", [128, 24832], BF16)
        bigD = sbt("bigD", [128, 18688], BF16)
        psA = es.enter_context(nc.psum_tensor("psA", [128, 1024], F32))
        psB = es.enter_context(nc.psum_tensor("psB", [128, 1024], F32))
        psS = es.enter_context(nc.psum_tensor("psS", [128, 2048], F32))
        psC = psS[:, 0:1024]
        psD = psS[:, 1024:2048]
        sems = {s: es.enter_context(nc.semaphore(s)) for s in SEMS}

        bigD_f32 = bigD[:].bitcast(F32)
        bvrb = bigD[0:1, 16416:17440]
        rlp2 = bigD[0:1, 17440:18464]
        trows = bigD[0:16, 17440:18464]
        tyv = lambda sl: rlb[:, sl * 512:(sl + 1) * 512]

        x2v = lambda ch: x2[:, ch * TOT:(ch + 1) * TOT]
        x2r = x2[:].rearrange("p (c t) -> p c t", c=NCH)
        K_sb = lambda j: bigA[:, j * 4096:(j + 1) * 4096]
        g_sb = lambda hch: bigA[:, hch * 1024:(hch + 1) * 1024]
        wk_sb = lambda dch: bigB[:, dch * 1024:(dch + 1) * 1024]
        wv_sb = lambda dch: bigB[:, 8320 + dch * 1024:8320 + (dch + 1) * 1024]
        wq_sb = lambda dch: bigB[:, 16640 + dch * 1024:16640 + (dch + 1) * 1024]
        vtb = lambda slot: (bigB[:, 0:8320] if slot == 0
                            else bigB[:, 16512:24832])
        wo_sb = bigB[:, 8320:16512]
        h2_sb = lambda dch: bigB[:, dch * 1024:(dch + 1) * 1024]
        ebc = lambda i: bigB[:, 8192 + i * 1024:8192 + (i + 1) * 1024]
        wsl = lambda slot: bigB[:, 12288 + slot * 4096:12288 + (slot + 1) * 4096]
        xoth = lambda slot: bigD[:, slot * 4096:(slot + 1) * 4096]
        h1s = bigD[:, 8192:12288]
        abcC = bigD[:, 12288:12800]
        bbcC = bigD[:, 12800:13312]
        qtmp = lambda slot: bigD[:, 13312 + slot * 512:13312 + (slot + 1) * 512]
        vtl = lambda slot: bigD[:, 14336 + slot * 1040:14336 + (slot + 1) * 1040]
        def pPk(nm, kc):
            base = (0 if nm == "A" else 4096) + ((kc % 4) // 2) * 2048 \
                + (kc % 2) * 1024
            return bigD[:, base:base + 1024]
        attq = lambda j: bigD[:, 8192 + j * 512:8192 + (j + 1) * 512]
        qqr = bigD[:, 12288:16384].rearrange("p (c t) -> p c t", c=NCH)
        lsst = bigD[:, 16416:18464]
        outsl = lambda slot: bigD_f32[:, slot * 1024:(slot + 1) * 1024]
        abcF = bigD[:, 4096:5120]
        bbcF = bigD[:, 5120:6144]
        tfs = lambda slot: bigD[:, 6144 + slot * 1024:6144 + (slot + 1) * 1024]
        urow = bigD[0:1, 8192:12288]
        evrS = bigD[0:1, 12288:16384]
        evrR = bigD[0:1, 8192:12288]

        SA, CA1, GA, SBv, CB1, GB = 0, 1, 2, 3, 4, 5
        mcol = lambda v, ch: modc[:, v * 8 + ch:v * 8 + ch + 1]
        BQ, BK, BO, B2 = 0, 8, 16, 24
        ccol = lambda base, ch: cc[:, base + ch:base + ch + 1]
        b1col = lambda hch: cc[:, 32 + hch:32 + hch + 1]
        btcol = lambda h, kc: btab[:, h * 32 + kc:h * 32 + kc + 1]
        rrow = lambda slot, w=1024: rows[0:1, slot * 1024:slot * 1024 + w]

        E = P.emit

        def mm(e, o, l, r, st_, sp, tp=None):
            return e.matmul(out=o, lhsT=l, rhs=r, start=st_, stop=sp,
                            tile_position=tp)

        P.dma("sync", cc[:], ccx[:])
        P.dma("sync", btab[:], btabx[:])
        P.dma("sync", eye[:], eyex[:])
        P.dma("sync", sclc[:], sclx[:])
        P.dma("gpsimd", bvrb[:], bvr[:])
        xlv = x2r[:, :, OWN:TOT]
        P.dma("gpsimd", xlv, xlt[:])
        P.dma("gpsimd",
              bigA[:, 0:5 * 6144].rearrange("p (c m) -> p c m", c=5),
              wmodx[:, 0:5, :])
        t_ld = P.dma("gpsimd",
                     bigB[:, 0:3 * 6144].rearrange("p (c m) -> p c m", c=3),
                     wmodx[:, 5:8, :])
        assert t_ld == 64
        wmod_sb = [bigA[:, d_ * 6144:(d_ + 1) * 6144] for d_ in range(5)] + \
                  [bigB[:, d_ * 6144:(d_ + 1) * 6144] for d_ in range(3)]
        mod_row = bigB[0:1, 18432:18432 + 6144]

        t_ones = E("vector", lambda e: (e.memset(onec[:], 1.0),
                                        e.memset(oner[:], 1.0),
                                        e.memset(onerf[:], 1.0),
                                        e.memset(one1[:], 1.0),
                                        e.engine_nop())[-1], (), "ve")
        t_sil = E("scalar", lambda e: e.activation(out=sil[:], in_=cc[:, 64:72],
                                                   func=AF.Silu),
                  [("lds", 64), ("ldg", 64)], "se")
        t_evrow = [0] * 12
        for mt in range(12):
            slot = mt % 2
            for dch in range(NCH):
                w = []
                if dch == 0:
                    w = [("lds", 64), ("ldg", 64), ("se", t_sil)]
                    if mt >= 2:
                        w.append(("ve", t_evrow[mt - 2]))
                tok = E("tensor", (lambda e, s_=slot, d_=dch, m_=mt:
                                   mm(e, psC[0:1, s_ * 512:(s_ + 1) * 512],
                                      sil[:, d_:d_ + 1],
                                      wmod_sb[d_][:, m_ * 512:(m_ + 1) * 512],
                                      d_ == 0, d_ == NCH - 1)),
                        w, "pe" if dch == NCH - 1 else None)
            t_evrow[mt] = E("vector", (lambda e, s_=slot, m_=mt:
                                       e.tensor_copy(
                                           out=mod_row[:, m_ * 512:(m_ + 1) * 512],
                                           in_=psC[0:1, s_ * 512:(s_ + 1) * 512])),
                            [("pe", tok)], "ve")
        for i in range(48):
            w = [("ve", t_evrow[min((i * 128) // 512, 11)])]
            if i == 0:
                w.append(("ve", t_ones))
            tok = E("tensor", (lambda e, i_=i: mm(e, psD[:, i_:i_ + 1],
                                                  mod_row[:, i_ * 128:(i_ + 1) * 128],
                                                  one1[:], True, True)),
                    w, "pe" if i == 47 else None)
        t_modcol = tok
        t_modc = E("vector", lambda e: e.tensor_copy(out=modc[:], in_=psD[:, 0:48]),
                   [("pe", t_modcol)], "ve")

        def fin_mod(e):
            e.drain()
            e.tensor_scalar(out=modc[:, 8:16], in0=modc[:, 8:16], scalar1=1.0,
                            scalar2=None, op0=ALU.add)
            e.tensor_scalar(out=modc[:, 32:40], in0=modc[:, 32:40], scalar1=1.0,
                            scalar2=None, op0=ALU.add)
            e.drain()
            for base in (16, 40):
                sl = modc[:, base:base + 8]
                msk = bigD[:].bitcast(mybir.dt.int32)[:, 0:8]
                mx = bigD_f32[:, 8:16]
                mn = bigD_f32[:, 16:24]
                e.tensor_single_scalar(out=msk, in_=sl, scalar=0.0, op=ALU.is_ge)
                e.tensor_scalar_max(out=mx, in0=sl, scalar1=0.25)
                e.tensor_scalar_min(out=mn, in0=sl, scalar1=-0.25)
                e.drain()
                e.select(out=sl, mask=msk, on_true=mx, on_false=mn, add_drain=True)
                e.drain()
            return e.engine_nop()
        t_mod = E("vector", fin_mod, [("ve", t_modc)], "ve")
        if dbg:
            P.dma("sync", dbg_mod[:], modc[:], [("ve", t_mod)], sem="db")

        def emit_wl(which):
            if which == 0:
                return P.dma("gpsimd",
                             bigB[:, 0:8192].rearrange("p (c m) -> p c m",
                                                       c=NCH),
                             wkx[:], [("pe", t_modcol)], sem="wl")
            if which == 1:
                return P.dma("gpsimd",
                             bigB[:, 16640:24832].rearrange(
                                 "p (c m) -> p c m", c=NCH),
                             wqx[:], sem="wl")
            return P.dma("gpsimd",
                         bigB[:, 8320:16512].rearrange("p (c m) -> p c m",
                                                       c=NCH),
                         wvx[:], sem="wl")
        t_x = [0] * NSL
        t_vlast = [0] * NSL
        t_h1tt = [0] * NSL
        kq_free = [0, 0]
        v_free = [0, 0]
        v_free_sem = ["se", "se"]
        first_v = [True]
        t_qtmp_free = [0, 0]
        t_vstage_free = [0, 0]
        t_kev_last = [0]
        t_vdma = {}

        H1 = lambda i: (h1s if i % 2 == 0 else bigD[:, 4096:8192])
        xoth0 = bigD[:, 0:4096]
        t_statm = [0] * (NSL + 1)
        t_h1k = {}
        t_rows = [0] * (NSL + 1)
        t_tc = [0] * (NSL + 1)
        h1ws = [None] * (NSL + 1)

        def xslice(s):
            if s < 4:
                return x2r[:, :, s * CS:(s + 1) * CS]
            return xoth0.rearrange("p (c t) -> p c t", c=NCH)

        def emit_xload(s):
            if s < 4:
                w = [("x1", t_x[s - 1])] if s >= 1 else []
                t_x[s] = P.dma("gpsimd", xslice(s),
                               xt[:, :, s * CS:(s + 1) * CS], w, sem="x1")
            else:
                w = [("ve", t_mod)]
                t_x[s] = P.dma("gpsimd", xslice(s),
                               xt[:, :, s * CS:(s + 1) * CS], w, sem="x0")

        def emit_xload_pair(s2, k):
            w = [("x0", t_x[s2 - 1] if k == 0 else t_x[s2]),
                 ("ve", t_h1k[(s2 - 1, k)]), ("pe", t_statm[s2 - 1])]
            t_x[s2] = P.dma(
                "gpsimd", xslice(s2)[:, 2 * k:2 * k + 2, :],
                xt[:, 2 * k:2 * k + 2, s2 * CS:(s2 + 1) * CS], w, sem="x0")
        emit_xload(0)
        emit_wl(0)
        emit_xload(1)
        emit_wl(1)
        emit_xload(2)
        t_wl3 = emit_wl(2)
        assert t_wl3 == 48
        emit_xload(3)

        def ln_rows(width, wm, ws2):
            def f1(e):
                e.tensor_scalar(out=rrow(0, width), in0=psC[0:1, 0:width],
                                scalar1=1.0 / D, scalar2=None, op0=ALU.mult)
                e.tensor_scalar(out=rrow(1, width), in0=psC[0:1, 512:512 + width],
                                scalar1=1.0 / D, scalar2=None, op0=ALU.mult)
                e.drain()
                e.tensor_tensor(out=rrow(3, width), in0=rrow(0, width),
                                in1=rrow(0, width), op=ALU.mult)
                e.drain()
                e.tensor_tensor(out=rrow(1, width), in0=rrow(1, width),
                                in1=rrow(3, width), op=ALU.subtract)
                e.drain()
                e.tensor_scalar(out=rrow(1, width), in0=rrow(1, width),
                                scalar1=1e-5, scalar2=None, op0=ALU.add)
                return e.engine_nop()
            tv1 = E("vector", f1, [("pe", wm), ("pe", ws2)], "ve")
            def rsq(e):
                e.activation(out=rrow(3, width), in_=rrow(1, width), func=AF.Ln)
                e.drain()
                return e.activation(out=rrow(2, width), in_=rrow(3, width),
                                    func=AF.Exp, scale=-0.5)
            tse = E("scalar", rsq, [("ve", tv1)], "se")

            def f2(e):
                e.scalar_tensor_tensor(out=rrow(1, width), in0=rrow(0, width),
                                       scalar=-1.0, in1=rrow(2, width),
                                       op0=ALU.mult, op1=ALU.mult)
                return e.engine_nop()
            return E("vector", f2, [("se", tse)], "ve")

        def emit_ln(xs, width, t_ready, extra_bcast_wait, slot=None):
            sl = h1s if slot is None else slot
            txq = E("vector", (lambda e, xs_=xs, w_=width, s_=sl: e.tensor_tensor(
                out=s_[:, 0:NCH * w_].rearrange("p (c t) -> p c t", c=NCH),
                in0=xs_, in1=xs_, op=ALU.mult)), t_ready, "ve")
            tm = None
            for dch in range(NCH):
                tm = E("tensor", (lambda e, d_=dch, xs_=xs, w_=width: mm(
                    e, psC[0:1, 0:w_], onec[:], xs_[:, d_, :], d_ == 0,
                    d_ == NCH - 1)),
                    list(t_ready) if dch == 0 else [],
                    "pe" if dch == NCH - 1 else None)
            ts2 = None
            for dch in range(NCH):
                ts2 = E("tensor", (lambda e, d_=dch, w_=width, s_=sl: mm(
                    e, psC[0:1, 512:512 + w_], onec[:],
                    s_[:, d_ * w_:(d_ + 1) * w_], d_ == 0, d_ == NCH - 1)),
                    [("ve", txq)] if dch == 0 else [],
                    "pe" if dch == NCH - 1 else None)
            tr = ln_rows(width, tm, ts2)
            tb = E("tensor", (lambda e, w_=width: (
                mm(e, psA[:, 0:w_], onerf[:], rrow(2, w_), True, True),
                mm(e, psA[:, 512:512 + w_], onerf[:], rrow(1, w_), True,
                   True))[-1]),
                [("ve", tr)] + list(extra_bcast_wait), "pe")
            tc = E("vector", (lambda e, w_=width: (
                e.tensor_copy(out=abcC[:, 0:w_], in_=psA[:, 0:w_]),
                e.tensor_copy(out=bbcC[:, 0:w_], in_=psA[:, 512:512 + w_]),
                e.engine_nop())[-1]), [("pe", tb)], "ve")
            return tc, tm, tr

        def emit_lnh1_stats(i):
            xs = xslice(i)
            sl = H1(i)
            sem = "x1" if i < 4 else "x0"
            t_ready = [(sem, t_x[i]), ("ve", t_mod)]
            if i >= 1:
                t_ready.append(("ve", t_rows[i - 1]))
            if i >= 2:
                t_ready.append(("pe", t_vlast[i - 2]))
            txq = E("vector", (lambda e, xs_=xs, s_=sl: e.tensor_tensor(
                out=s_[:, 0:NCH * CS].rearrange("p (c t) -> p c t", c=NCH),
                in0=xs_, in1=xs_, op=ALU.mult)), t_ready, "ve")
            tm = None
            for dch in range(NCH):
                tm = E("tensor", (lambda e, d_=dch, xs_=xs: mm(
                    e, psC[0:1, 0:CS], onec[:], xs_[:, d_, :], d_ == 0,
                    d_ == NCH - 1)),
                    list(t_ready) if dch == 0 else [],
                    "pe" if dch == NCH - 1 else None)
            ts2 = None
            for dch in range(NCH):
                ts2 = E("tensor", (lambda e, d_=dch, s_=sl: mm(
                    e, psC[0:1, 512:512 + CS], onec[:],
                    s_[:, d_ * CS:(d_ + 1) * CS], d_ == 0, d_ == NCH - 1)),
                    [("ve", txq)] if dch == 0 else [],
                    "pe" if dch == NCH - 1 else None)
            t_statm[i] = tm
            t_rows[i] = ln_rows(CS, tm, ts2)

        def emit_lnh1_bcast(i):
            xs = xslice(i)
            sl = H1(i)
            ebw = [("ve", t_tc[i - 1])] if i >= 1 else []
            tb = E("tensor", (lambda e: (
                mm(e, psA[:, 0:CS], onerf[:], rrow(2, CS), True, True),
                mm(e, psA[:, 512:512 + CS], onerf[:], rrow(1, CS), True,
                   True))[-1]),
                [("ve", t_rows[i])] + ebw, "pe")
            t_abc = E("vector", (lambda e: (
                e.tensor_copy(out=abcC[:, 0:CS], in_=psA[:, 0:CS]),
                e.tensor_copy(out=bbcC[:, 0:CS], in_=psA[:, 512:512 + CS]),
                e.engine_nop())[-1]), [("pe", tb)], "ve")
            t_tc[i] = t_abc
            h1ws[i] = []

        def emit_h1pair(i, ch0):
            xs = xslice(i)
            sl = H1(i)
            for ch in (ch0, ch0 + 1):
                tt = E("vector", (lambda e, c=ch, xs_=xs, s_=sl: (
                    e.tensor_tensor(out=s_[:, c * CS:(c + 1) * CS],
                                    in0=xs_[:, c, :], in1=abcC[:], op=ALU.mult),
                    e.drain(),
                    e.tensor_tensor(out=s_[:, c * CS:(c + 1) * CS],
                                    in0=s_[:, c * CS:(c + 1) * CS],
                                    in1=bbcC[:], op=ALU.add))[-1]),
                    [("ve", t_tc[i])], "ve")
                ta = E("scalar", (lambda e, c=ch, s_=sl: e.activation(
                    out=s_[:, c * CS:(c + 1) * CS],
                    in_=s_[:, c * CS:(c + 1) * CS],
                    func=AF.Identity, bias=mcol(SA, c), scale=mcol(CA1, c))),
                    [("ve", tt)], "se")
                h1ws[i].append(("se", ta))
                t_h1k[(i, ch // 2)] = tt
                if ch == NCH - 1:
                    t_h1tt[i] = tt

        emit_lnh1_stats(0)
        emit_lnh1_bcast(0)
        for c0 in range(0, NCH, 2):
            emit_h1pair(0, c0)
        for s in range(NSL):
            if s + 1 < NSL:
                emit_lnh1_stats(s + 1)
            sl = H1(s)
            h1w = h1ws[s]
            for ech in range(NCH):
                slot = ech % 2
                for dch in range(NCH):
                    w = []
                    if dch == 0:
                        w = list(h1w) + [("wl", 16)]
                        if kq_free[slot]:
                            w.append(("se", kq_free[slot]))
                    tk = E("tensor", (lambda e, d_=dch, ec=ech, sl2=slot, s_=sl:
                           mm(e, psB[:, sl2 * 512:(sl2 + 1) * 512],
                              wk_sb(d_)[:, ec * 128:(ec + 1) * 128],
                              s_[:, d_ * CS:(d_ + 1) * CS], d_ == 0,
                              d_ == NCH - 1)),
                        w, "pe" if dch == NCH - 1 else None)
                tke = E("scalar", (lambda e, ec=ech, sl2=slot, ss=s: e.activation(
                    out=K_sb(ec)[:, ss * CS:(ss + 1) * CS],
                    in_=psB[:, sl2 * 512:(sl2 + 1) * 512],
                    func=AF.Identity, bias=ccol(BK, ec))), [("pe", tk)], "se")
                kq_free[slot] = tke
                t_kev_last[0] = tke
            if s == NSL - 1:
                pe_klast = P.cnt["pe"]
            if s < 4:
                for ech in range(NCH):
                    slot = ech % 2
                    for dch in range(NCH):
                        w = []
                        if dch == 0:
                            w = [("wl", 32), ("se", kq_free[slot])]
                        tq = E("tensor", (lambda e, d_=dch, ec=ech, sl2=slot,
                                          s_=sl: mm(
                            e, psB[:, sl2 * 512:(sl2 + 1) * 512],
                            wq_sb(d_)[:, ec * 128:(ec + 1) * 128],
                            s_[:, d_ * CS:(d_ + 1) * CS], d_ == 0,
                            d_ == NCH - 1)),
                            w, "pe" if dch == NCH - 1 else None)
                    qsl = ech % 2
                    w = [("pe", tq)]
                    if t_qtmp_free[qsl]:
                        w.append((f"q{qsl}", t_qtmp_free[qsl]))
                    tqe = E("scalar", (lambda e, sl2=slot, q_=qsl, ec=ech:
                            e.activation(out=qtmp(q_)[:],
                                         in_=psB[:, sl2 * 512:(sl2 + 1) * 512],
                                         func=AF.Identity, bias=ccol(BQ, ec))),
                            w, "se")
                    kq_free[slot] = tqe
                    td = P.dma("sync", q_hbm[:, ech, s * CS:(s + 1) * CS],
                               qtmp(qsl)[:], [("se", tqe)], sem=f"q{qsl}")
                    t_qtmp_free[qsl] = td
            if s + 1 < NSL:
                emit_lnh1_bcast(s + 1)

            for tt_ in range(4):
                kc = s * 4 + tt_
                vslot = tt_ % 2
                tlast = None
                for vh in range(2):
                    for dch in range(NCH):
                        w = []
                        if dch == 0:
                            w = [("wl", 48)]
                            if v_free[vh]:
                                w.append((v_free_sem[vh], v_free[vh]))
                            if first_v[0]:
                                w.append(("ve", t_mod))
                                first_v[0] = False
                            if vh == 0 and dch == 0:
                                w += h1w
                        E("tensor", (lambda e, d_=dch, t2=tt_, v_=vh, s_=sl: mm(
                            e, psD[:, v_ * 512:(v_ + 1) * 512],
                            s_[:, d_ * CS + t2 * 128:d_ * CS + (t2 + 1) * 128],
                            wv_sb(d_)[:, v_ * 512:(v_ + 1) * 512],
                            d_ == 0, False)), w, None)
                    tv = E("tensor", (lambda e, v_=vh: mm(
                        e, psD[:, v_ * 512:(v_ + 1) * 512], oner[:],
                        bvrb[0:1, v_ * 512:(v_ + 1) * 512], False, True)),
                        [("lds", 64), ("ldg", 64), ("ve", t_ones)], "pe")
                    tlast = tv
                    for j in range(8):
                        h = vh * 8 + j
                        w = [("pe", tv)]
                        if j == 0 and t_vstage_free[vslot]:
                            w.append((f"v{vslot}", t_vstage_free[vslot]))
                        if vh == 0:
                            tve = E("scalar", (lambda e, h_=h, j_=j, v_=vh,
                                               sl2=vslot, k_=kc: e.activation(
                                out=vtl(sl2)[:, h_ * 65:h_ * 65 + 64],
                                in_=psD[:, v_ * 512 + j_ * 64:
                                        v_ * 512 + (j_ + 1) * 64],
                                func=AF.Identity, scale=btcol(h_, k_))), w, "se")
                        else:
                            tve = E("vector", (lambda e, h_=h, j_=j, v_=vh,
                                               sl2=vslot, k_=kc: e.tensor_scalar(
                                out=vtl(sl2)[:, h_ * 65:h_ * 65 + 64],
                                in0=psD[:, v_ * 512 + j_ * 64:
                                        v_ * 512 + (j_ + 1) * 64],
                                scalar1=btcol(h_, k_), scalar2=None,
                                op0=ALU.mult)),
                                w, "ve" if j == 7 else None, nodrain=(j != 7))
                    v_free[vh] = tve
                    v_free_sem[vh] = "se" if vh == 0 else "ve"
                t_vlast[s] = tlast
                tbc = E("vector", (lambda e, sl2=vslot, k_=kc: e.tensor_copy(
                    out=vtl(sl2)[:].rearrange("p (h c) -> p h c", h=16)[:, :, 64:65],
                    in_=btab[:].rearrange("p (h c) -> p h c", h=16)[:, :, k_:k_ + 1])),
                    [("lds", 64), ("ldg", 64), (v_free_sem[0], v_free[0]),
                     (v_free_sem[1], v_free[1])], "ve")
                tvd = P.dma("sync", v_hbm[:, kc, :], vtl(vslot)[:], [("ve", tbc)],
                            sem=f"v{vslot}")
                t_vstage_free[vslot] = tvd
                t_vdma[kc] = tvd
                if kc == 7:
                    v_snap0 = (P.cnt["v0"], P.cnt["v1"])
                if kc == 15:
                    v_snap1 = (P.cnt["v0"], P.cnt["v1"])
                if s + 1 < NSL:
                    emit_h1pair(s + 1, 2 * tt_)
                    if s + 2 < NSL and s + 2 > 4:
                        emit_xload_pair(s + 2, tt_)
            if s + 2 == 4:
                emit_xload(4)

        xsL = x2r[:, :, OWN:TOT]
        t_abcL, _, _ = emit_ln(
            xsL, LORA,
            [("lds", 64), ("ldg", 64), ("ve", t_mod),
             ("pe", t_vlast[NSL - 1]), ("ve", t_rows[NSL - 1]),
             ("ve", t_h1tt[NSL - 1])],
            [("ve", t_tc[NSL - 1])])
        t_h1La = []
        for ch in range(NCH):
            tt = E("vector", (lambda e, c=ch: (
                e.tensor_tensor(out=h1s[:, c * LORA:(c + 1) * LORA],
                                in0=xsL[:, c, :], in1=abcC[:, 0:LORA],
                                op=ALU.mult),
                e.drain(),
                e.tensor_tensor(out=h1s[:, c * LORA:(c + 1) * LORA],
                                in0=h1s[:, c * LORA:(c + 1) * LORA],
                                in1=bbcC[:, 0:LORA], op=ALU.add))[-1]),
                [("ve", t_abcL)], "ve")
            ta = E("scalar", (lambda e, c=ch: e.activation(
                out=h1s[:, c * LORA:(c + 1) * LORA],
                in_=h1s[:, c * LORA:(c + 1) * LORA],
                func=AF.Identity, bias=mcol(SA, c), scale=mcol(CA1, c))),
                [("ve", tt)], "se")
            t_h1La.append(ta)
            if ch == NCH - 1:
                t_h1L_tt = tt
        for ech in range(NCH):
            slot = ech % 2
            for dch in range(NCH):
                w = []
                if dch == 0:
                    w = [("se", t) for t in t_h1La] + \
                        ([("se", kq_free[slot])] if kq_free[slot] else [])
                tq = E("tensor", (lambda e, d_=dch, ec=ech, sl=slot: mm(
                    e, psB[:, sl * 512:sl * 512 + LORA],
                    wq_sb(d_)[:, ec * 128:(ec + 1) * 128],
                    h1s[:, d_ * LORA:(d_ + 1) * LORA], d_ == 0, d_ == NCH - 1)),
                    w, "pe" if dch == NCH - 1 else None)
            qsl = ech % 2
            w = [("pe", tq)]
            if t_qtmp_free[qsl]:
                w.append((f"q{qsl}", t_qtmp_free[qsl]))
            tqe = E("scalar", (lambda e, sl=slot, q_=qsl, ec=ech: e.activation(
                out=qtmp(q_)[:, 0:LORA], in_=psB[:, sl * 512:sl * 512 + LORA],
                func=AF.Identity, bias=ccol(BQ, ec))), w, "se")
            kq_free[slot] = tqe
            td = P.dma("sync", q_hbm[:, ech, OWN:TOT], qtmp(qsl)[:, 0:LORA],
                       [("se", tqe)], sem=f"q{qsl}")
            t_qtmp_free[qsl] = td

        pe_C = P.cnt["pe"]
        se_C = P.cnt["se"]
        ve_C = P.cnt["ve"]
        q_tot = (P.cnt["q0"], P.cnt["q1"])
        v_tot = (P.cnt["v0"], P.cnt["v1"])
        t_dbgk = 0
        if dbg:
            t_dbgk = P.dma("sync", dbg_k[:], bigA[:], [("se", t_kev_last[0])],
                           sem="db")

        t_wo = P.dma("gpsimd", wo_sb.rearrange("p (c m) -> p c m", c=NCH), wox[:],
                     [("pe", t_vlast[NSL - 1])], sem="wo")
        qtw = [WQT] * 4 + [LORA]
        stgb = bigB[:, 18592:22688]
        P.stage_prev = 0
        qq_base = [(bigD, 12288), (bigB, 2080)]
        t_q_lastuse_slot = [pe_C, pe_C]
        t_qq_slot = [0, 0]
        t_acc_free = [ve_C] * 4
        t_acc_free_sem = ["ve"] * 4
        t_qq_lastS = [pe_C]
        t_exp_lastA = [0]
        t_exp_lastB = [0]
        t_qq = 0
        t_qq_lastuse = 0
        P.t_rs_prev = 0
        P.t_bc_prev = [0, 0]
        t_x2w = []
        t_rlapply = []
        t_lsst_free = 0
        P.vtb_last = [pe_klast, pe_C]
        t_S_gate = {"C": [("se", se_C), ("ve", ve_C)], "D": [("se", se_C)]}
        t_P_free = {}

        for qt in range(5):
            W = qtw[qt]
            qoff = qt * WQT if qt < 4 else OWN
            qslot = qt % 2
            qsb, qob = qq_base[qslot]
            if qt == 0:
                w = [("pe", pe_C), ("se", se_C), ("ve", ve_C),
                     ("q0", q_tot[0]), ("q1", q_tot[1]),
                     ("v0", v_tot[0]), ("v1", v_tot[1])]
            else:
                w = [("pe", t_q_lastuse_slot[qslot])]
                if t_qq_slot[qslot]:
                    w.append(("qq", t_qq_slot[qslot]))
            t_qq = P.dma("gpsimd",
                         qsb[:, qob:qob + 4096].rearrange(
                             "p (c t) -> p c t", c=NCH)[:, :, 0:W],
                         q_hbm[:, :, qoff:qoff + W], w, sem="qq")
            t_qq_slot[qslot] = t_qq

            def emit_stage(pgi):
                tl = 0
                for piece in range(4):
                    w = []
                    if piece == 0:
                        w = ([("p1", P.stage_prev)] if P.stage_prev
                             else [("pe", pe_C)])
                    if pgi < 8:
                        tl = P.dma(
                            "gpsimd",
                            stgb[:, piece * 1024:(piece + 1) * 1024]
                            .rearrange("p (d e) -> p d e", d=NCH),
                            w1x[pgi, :, piece, :, :], w, sem="p0")
                    else:
                        tl = P.dma(
                            "gpsimd",
                            stgb[:, piece * 1024:(piece + 1) * 1024]
                            .rearrange("p (k o) -> p k o", k=8),
                            w2x[pgi - 8, :, piece * 8:(piece + 1) * 8, :],
                            w, sem="p0")
                tgt = w1e[pgi] if pgi < 8 else w2r[pgi - 8]
                P.dma("sync", tgt[:, 0:2048], stgb[:, 0:2048],
                      [("p0", tl)], sem="p1")
                P.stage_prev = P.dma("sync", tgt[:, 2048:4096],
                                     stgb[:, 2048:4096], [], sem="p1")
            t_lq_all = []
            t_rc_pg = []
            for pg in range(4):
                jA, jB = 2 * pg, 2 * pg + 1
                t_vtb = [0, 0]

                def vload(kcb, extra=(), pg_=pg):
                    slot = kcb % 2
                    t_vtb[slot] = P.dma(
                        "gpsimd",
                        vtb(slot)[:, 0:KCB * 260].rearrange(
                            "p (k c) -> p k c", k=KCB),
                        v_hbm[:, kcb * KCB:(kcb + 1) * KCB,
                              pg_ * 260:(pg_ + 1) * 260],
                        [("pe", P.vtb_last[slot])] + list(extra), sem=f"t{slot}")
                first = qt == 0 and pg == 0
                if qt != 4:
                    vload(0, [("v0", v_snap0[0]), ("v1", v_snap0[1])]
                          if first else [])
                    vload(1, [("v0", v_snap1[0]), ("v1", v_snap1[1])]
                          if first else [])
                t_exp = {}
                t_expk = {}
                t_evac = {}
                if qt == 4:
                    def vload(kcb, extra=(), pg_=pg):
                        slot = kcb % 2
                        t_vtb[slot] = P.dma(
                            "gpsimd",
                            vtb(slot)[:, 0:KCB * 260].rearrange(
                                "p (k c) -> p k c", k=KCB),
                            v_hbm[:, kcb * KCB:(kcb + 1) * KCB,
                                  pg_ * 260:(pg_ + 1) * 260],
                            [("pe", P.vtb_last[slot])] + list(extra),
                            sem=f"t{slot}")
                    t_vtb = [0, 0]
                    ex0 = []
                    vload(0, ex0)
                    vload(1, ex0)
                    t_sl = {}
                    for (pair, j) in (("A", jA), ("B", jB)):
                        ps = psC if pair == "A" else psD
                        w = list(t_S_gate["C" if pair == "A" else "D"])
                        t_S_gate["C" if pair == "A" else "D"] = []
                        w.append(("qq", t_qq))
                        tok = 0
                        for kc in range(KCN):
                            tok = E("tensor", (lambda e, p_=ps, j_=j, k_=kc,
                                               q_=qsb, qo_=qob: (
                                mm(e, p_[:, k_ * 16:k_ * 16 + 16],
                                   K_sb(j_)[0:64, k_ * 128:(k_ + 1) * 128],
                                   q_[0:64, qo_ + j_ * 512:qo_ + j_ * 512 + 16],
                                   True, True, (0, 0)),
                                mm(e, p_[:, 512 + k_ * 16:512 + k_ * 16 + 16],
                                   K_sb(j_)[64:128, k_ * 128:(k_ + 1) * 128],
                                   q_[64:128, qo_ + j_ * 512:qo_ + j_ * 512 + 16],
                                   True, True, (64, 0)))[-1]),
                                w if kc == 0 else [], "pe" if kc == KCN - 1 else None)
                        poff = 0 if pair == "A" else 4096
                        w2 = [("pe", tok)]
                        pfree = t_P_free.get((pair, 0))
                        if pfree:
                            w2.append(("pe", pfree))
                        texp = E("scalar", (lambda e, p_=ps, po=poff: e.activation(
                            out=bigD[:, po:po + 1024], in_=p_,
                            func=AF.Exp, scale=0.125)), w2, "se")
                        t_exp[(pair, 0)] = texp
                        t_sl[pair] = texp
                        (t_exp_lastA if pair == "A" else t_exp_lastB)[0] = texp
                        t_qq_lastS[0] = tok
                    tokp = {}
                    for kcb in range(4):
                        kslot = kcb % 2
                        for (pair, j) in (("A", jA), ("B", jB)):
                            pst = psA if pair == "A" else psB
                            poff = 0 if pair == "A" else 4096
                            for i in range(KCB):
                                kc = kcb * KCB + i
                                for hh in range(2):
                                    h = j * 2 + hh
                                    w = [("se", t_sl[pair]),
                                         (f"t{kslot}", t_vtb[kslot])]
                                    ai = (0 if pair == "A" else 2) + hh
                                    if kc == 0 and t_acc_free[ai]:
                                        w.append((t_acc_free_sem[ai],
                                                  t_acc_free[ai]))
                                    hl = h - 4 * pg
                                    tokp[pair] = E("tensor", (lambda e, p_=pst,
                                            o_=hh * 512, h_=hl, k_=kc, hh_=hh,
                                            ks=kslot, po=poff: mm(
                                        e, p_[0:65, o_:o_ + 16],
                                        vtb(ks)[:, (k_ % KCB) * 260 + h_ * 65:
                                                (k_ % KCB) * 260 + h_ * 65 + 65],
                                        bigD[:, po + hh_ * 512 + k_ * 16:
                                             po + hh_ * 512 + k_ * 16 + 16],
                                        k_ == 0, k_ == KCN - 1)),
                                        w, "pe" if (hh == 1 and i == KCB - 1)
                                        else None)
                        P.vtb_last[kslot] = tokp["B"]
                        if kcb + 2 < 4:
                            vload(kcb + 2)
                    t_P_free[("A", 0)] = tokp["A"]
                    t_P_free[("B", 0)] = tokp["B"]
                    for (pair, j) in (("A", jA), ("B", jB)):
                        tok = tokp[pair]
                        base = 0 if pair == "A" else 2
                        pst = psA if pair == "A" else psB

                        def evac(e, pst=pst, j=j, base=base, W=W):
                            for hh in range(2):
                                if hh == 0:
                                    dst = attq(j)[0:64, 0:W]
                                else:
                                    dst = attq(j)[64:128, 0:W]
                                e.tensor_copy(out=dst,
                                              in_=pst[0:64, hh * 512:hh * 512 + W])
                                e.tensor_copy(
                                    out=lsst[64:65, (base + hh) * 512:
                                             (base + hh) * 512 + W],
                                    in_=pst[64:65, hh * 512:hh * 512 + W])
                            return e.engine_nop()
                        tev = E("vector", evac,
                                [("pe", tok)] +
                                ([("rs", t_lsst_free)] if t_lsst_free else []),
                                "ve")
                        t_acc_free[base] = tev
                        t_acc_free[base + 1] = tev
                        t_acc_free_sem[base] = "ve"
                        t_acc_free_sem[base + 1] = "ve"
                        t_evac[pair] = tev
                elif True:
                    prev_pv = {}

                    def emit_S(kc, pair, j):
                        snm = "C" if pair == "A" else "D"
                        ps = psC if pair == "A" else psD
                        w = list(t_S_gate[snm])
                        t_S_gate[snm] = []
                        pr = t_exp.get((pair, kc - 1))
                        if pr:
                            w.append(("se", pr))
                        if kc == 0:
                            w.append(("qq", t_qq))
                        return E("tensor", (lambda e, p_=ps, j_=j, k_=kc, W_=W,
                                            q_=qsb, qo_=qob: (
                            mm(e, p_[:, 0:W_],
                               K_sb(j_)[0:64, k_ * 128:(k_ + 1) * 128],
                               q_[0:64, qo_ + j_ * 512:qo_ + j_ * 512 + W_],
                               True, True, (0, 0)),
                            mm(e, p_[:, 512:512 + W_],
                               K_sb(j_)[64:128, k_ * 128:(k_ + 1) * 128],
                               q_[64:128, qo_ + j_ * 512:qo_ + j_ * 512 + W_],
                               True, True, (64, 0)))[-1]),
                            w, "pe")

                    def emit_exp(kc, pair, ts_):
                        w = [("pe", ts_)]
                        pfree = t_P_free.get((pair, kc % 2))
                        if pfree:
                            w.append(("pe", pfree))
                        poff = (0 if pair == "A" else 4096) + (kc % 2) * 2048
                        ps = psC if pair == "A" else psD
                        texp = E("scalar", (lambda e, p_=ps, po=poff:
                                e.activation(
                            out=bigD[:, po:po + 1024], in_=p_,
                            func=AF.Exp, scale=0.125)), w, "se")
                        t_exp[(pair, kc)] = texp

                    def emit_PV(kc, pair, j, kslot):
                        pst = psA if pair == "A" else psB
                        tok = 0
                        for hh in range(2):
                            h = j * 2 + hh
                            w = [("se", t_exp[(pair, kc)]),
                                 (f"t{kslot}", t_vtb[kslot])]
                            ai = (0 if pair == "A" else 2) + hh
                            if kc == 0 and t_acc_free[ai]:
                                w.append((t_acc_free_sem[ai], t_acc_free[ai]))
                            poff = (0 if pair == "A" else 4096) + (kc % 2) * 2048
                            tok = E("tensor", (lambda e, p_=pst, o_=hh * 512,
                                               h_=h - 4 * pg,
                                               k_=kc, hh_=hh, W_=W, ks=kslot,
                                               po=poff: mm(
                                e, p_[0:65, o_:o_ + W_],
                                vtb(ks)[:, (k_ % KCB) * 260 + h_ * 65:
                                        (k_ % KCB) * 260 + h_ * 65 + 65],
                                bigD[:, po + hh_ * 512:po + hh_ * 512 + W_],
                                k_ == 0, k_ == KCN - 1)),
                                w, "pe" if hh == 1 else None)
                        t_P_free[(pair, kc % 2)] = tok
                        return tok

                    for kcb in range(4):
                        kslot = kcb % 2
                        for i in range(KCB):
                            kc = kcb * KCB + i
                            for (pair, j) in (("A", jA), ("B", jB)):
                                ts_ = emit_S(kc, pair, j)
                                emit_exp(kc, pair, ts_)
                                if kc >= 1:
                                    pslot = ((kc - 1) // KCB) % 2
                                    prev_pv[pair] = emit_PV(kc - 1, pair, j, pslot)
                                    if pair == "B" and (kc - 1) % KCB == KCB - 1:
                                        bdone = (kc - 1) // KCB
                                        P.vtb_last[bdone % 2] = prev_pv["B"]
                                        if bdone + 2 < 4:
                                            vload(bdone + 2)
                                        if bdone == 0:
                                            emit_stage(qt * 4 + pg)
                    for (pair, j) in (("A", jA), ("B", jB)):
                        tok = emit_PV(KCN - 1, pair, j, ((KCN - 1) // KCB) % 2)
                        base = 0 if pair == "A" else 2
                        pst = psA if pair == "A" else psB

                        def evac(e, pst=pst, j=j, base=base, W=W):
                            for hh in range(2):
                                if hh == 0:
                                    dst = attq(j)[0:64, 0:W]
                                else:
                                    dst = attq(j)[64:128, 0:W]
                                e.tensor_copy(out=dst,
                                              in_=pst[0:64, hh * 512:hh * 512 + W])
                                e.tensor_copy(
                                    out=lsst[64:65, (base + hh) * 512:
                                             (base + hh) * 512 + W],
                                    in_=pst[64:65, hh * 512:hh * 512 + W])
                            return e.engine_nop()
                        tev = E("vector", evac,
                                [("pe", tok)] +
                                ([("rs", t_lsst_free)] if t_lsst_free else []),
                                "ve")
                        t_acc_free[base] = tev
                        t_acc_free[base + 1] = tev
                        t_acc_free_sem[base] = "ve"
                        t_acc_free_sem[base + 1] = "ve"
                        t_evac[pair] = tev
                    P.vtb_last[3 % 2] = tok
                w = [("ve", t_evac["A"]), ("ve", t_evac["B"])]
                if P.t_rs_prev:
                    w.append(("rs", P.t_rs_prev))
                t_rs = P.dma("sync", rl_hbm[qt, 4 * pg:4 * pg + 4, 0:W],
                             lsst[64:65, 0:2048].rearrange(
                                 "p (s t) -> p s t", s=4)[:, :, 0:W], w,
                             sem="rs")
                t_lsst_free = t_rs
                P.t_rs_prev = t_rs
                for jj in range(2):
                    j = 2 * pg + jj
                    w = [("rs", t_rs)]
                    if P.t_bc_prev[j % 2]:
                        w.append((f"lr{j % 2}", P.t_bc_prev[j % 2]))
                    if t_rlapply:
                        w.append(("ve", t_rlapply[-1]))
                    if t_x2w:
                        w.append(("ve", t_x2w[-1]))
                    P.dma("sync", rlb[:, 0:W],
                          rl_hbm[qt, 2 * j:2 * j + 1, 0:W].broadcast_to(
                              (128, W)), w, sem=f"lr{j % 2}")
                    tbc_ = P.dma(
                        "sync", rlb[:, 512:512 + W],
                        rl_hbm[qt, 2 * j + 1:2 * j + 2, 0:W].broadcast_to(
                            (128, W)), [], sem=f"lr{j % 2}")
                    P.t_bc_prev[j % 2] = tbc_
                    if pg == 3:
                        def _rcs(e, W_=W):
                            e.activation(out=rlb[0:64, 0:W_],
                                         in_=rlb[0:64, 0:W_], func=AF.Ln)
                            e.activation(out=rlb[64:128, 512:512 + W_],
                                         in_=rlb[64:128, 512:512 + W_],
                                         func=AF.Ln)
                            e.drain()
                            e.activation(out=rlb[0:64, 0:W_],
                                         in_=rlb[0:64, 0:W_],
                                         func=AF.Exp, scale=-1.0)
                            return e.activation(
                                out=rlb[64:128, 512:512 + W_],
                                in_=rlb[64:128, 512:512 + W_],
                                func=AF.Exp, scale=-1.0)
                        trc_ = E("scalar", _rcs,
                                 [(f"lr{j % 2}", tbc_)], "se")
                        wtap = [("se", trc_)]

                        def _tap(e, j_=j, W_=W):
                            e.tensor_tensor(out=attq(j_)[0:64, 0:W_],
                                            in0=attq(j_)[0:64, 0:W_],
                                            in1=rlb[0:64, 0:W_], op=ALU.mult)
                            return e.tensor_tensor(
                                out=attq(j_)[64:128, 0:W_],
                                in0=attq(j_)[64:128, 0:W_],
                                in1=rlb[64:128, 512:512 + W_], op=ALU.mult)
                        tap_ = E("vector", _tap, wtap, "ve")
                    else:
                        def _tap(e, j_=j, W_=W):
                            with nc.allow_low_precision(
                                    "softmax denominators kept bf16"):
                                e.reciprocal(out=rlb[0:64, 0:W_],
                                             in_=rlb[0:64, 0:W_])
                                e.reciprocal(out=rlb[64:128, 512:512 + W_],
                                             in_=rlb[64:128, 512:512 + W_])
                            e.drain()
                            e.tensor_tensor(out=attq(j_)[0:64, 0:W_],
                                            in0=attq(j_)[0:64, 0:W_],
                                            in1=rlb[0:64, 0:W_], op=ALU.mult)
                            return e.tensor_tensor(
                                out=attq(j_)[64:128, 0:W_],
                                in0=attq(j_)[64:128, 0:W_],
                                in1=rlb[64:128, 512:512 + W_], op=ALU.mult)
                        tap_ = E("vector", _tap, [(f"lr{j % 2}", tbc_)], "ve")
                    t_rlapply.append(tap_)
            rlw = [("ve", t) for t in t_rlapply[-8:]]
            t_ty_free = 0
            tty = 0
            for ech in range(NCH):
                for vch in range(NCH):
                    w = []
                    if vch == 0:
                        w = (rlw if ech == 0 else []) + [("wo", 16)]
                        if ech == 0:
                            w.append(("ve", t_evac["A"]))
                        if t_ty_free:
                            w.append(("se", t_ty_free))
                    tw = E("tensor", (lambda e, v_=vch, ec=ech, W_=W: mm(
                        e, psA[:, 0:W_],
                        wo_sb[:, v_ * 1024 + ec * 128:v_ * 1024 + (ec + 1) * 128],
                        attq(v_)[:, 0:W_], v_ == 0, v_ == NCH - 1)),
                        w, "pe" if vch == NCH - 1 else None)
                w = [("pe", tw)]
                prev_x2 = t_x2w[-2] if (len(t_x2w) >= 2 and ech >= 2) else 0
                if prev_x2:
                    w.append(("ve", prev_x2))
                tsl = ech % 2
                tty = E("scalar", (lambda e, W_=W, ec=ech, sl=tsl: e.activation(
                    out=tyv(sl)[:, 0:W_], in_=psA[:, 0:W_], func=AF.Identity,
                    bias=ccol(BO, ec))), w, "se")
                t_ty_free = tty
                tx2 = E("vector", (lambda e, ec=ech, W_=W, qo=qoff, sl=tsl:
                        e.scalar_tensor_tensor(
                            out=x2v(ec)[:, qo:qo + W_], in0=tyv(sl)[:, 0:W_],
                            scalar=mcol(GA, ec), in1=x2v(ec)[:, qo:qo + W_],
                            op0=ALU.mult, op1=ALU.add)), [("se", tty)], "ve")
                t_x2w.append(tx2)
            t_q_lastuse_slot[qslot] = t_qq_lastS[0]
            t_S_gate = {"C": [("se", t_exp_lastA[0])],
                        "D": [("se", t_exp_lastB[0])]}
            t_acc_free = [tty, tty, 0, 0]
            t_acc_free_sem = ["se", "se", "ve", "ve"]
            t_acc_free[2] = t_evac["B"]
            t_acc_free[3] = t_evac["B"]
            t_P_free = {}

        pe_att = P.cnt["pe"]
        se_att = P.cnt["se"]
        stage_done = P.cnt["p1"]
        x2done = [("ve", P.cnt["ve"])]
        if dbg:
            P.dma("sync", dbg_x2[:], x2[:], list(x2done), sem="db")

        xsL2 = x2r[:, :, OWN:TOT]
        t_abc2, _, _ = emit_ln(xsL2, LORA, list(x2done), [])
        t_h2la = []
        for ch in range(NCH):
            tt = E("vector", (lambda e, c=ch: (
                e.tensor_tensor(out=h1s[:, c * LORA:(c + 1) * LORA],
                                in0=xsL2[:, c, :], in1=abcC[:, 0:LORA],
                                op=ALU.mult),
                e.drain(),
                e.tensor_tensor(out=h1s[:, c * LORA:(c + 1) * LORA],
                                in0=h1s[:, c * LORA:(c + 1) * LORA],
                                in1=bbcC[:, 0:LORA], op=ALU.add))[-1]),
                [("ve", t_abc2)], "ve")
            ta = E("scalar", (lambda e, c=ch: e.activation(
                out=h2l[:, c * LORA:(c + 1) * LORA],
                in_=h1s[:, c * LORA:(c + 1) * LORA],
                func=AF.Identity, bias=mcol(SBv, c), scale=mcol(CB1, c))),
                [("ve", tt)], "se")
            t_h2la.append(ta)
        if dbg:
            P.dma("sync", dbg_h2l[:], h2l[:], [("se", t) for t in t_h2la], sem="db")
        t_odds = E("vector", (lambda e: e.tensor_scalar(
            out=odds[:].rearrange("p (c q) -> p c q", c=NCH),
            in0=h2l[:].rearrange("p (c t) -> p c t", c=NCH)[:, :, 1:16:2],
            scalar1=sclc[:], scalar2=None, op0=ALU.mult)),
            [("se", t) for t in t_h2la] + [("lds", 64), ("ldg", 64)], "ve")
        ttk = 0
        for ch in range(NCH):
            w = [("se", t_h2la[ch]), ("lds", 64), ("ldg", 64)]
            if ch == 0:
                w += [("pe", pe_att), ("se", se_att)]
            ttk = E("tensor", (lambda e, c=ch: e.transpose(
                psS[:].bitcast(BF16)[0:16, 2048 + c * 128:2176 + c * 128],
                h2l[:, c * LORA:(c + 1) * LORA], eye[:])), w,
                "pe" if ch == NCH - 1 else None)
        t_trcp = E("vector", (lambda e: e.tensor_copy(
            out=trows[:],
            in_=psS[:].bitcast(BF16)[0:16, 2048:2048 + NCH * 128])),
            [("pe", ttk)], "ve")
        t_tr = [t_trcp] * NCH
        t_evr = []
        for p_ in range(8):
            dst = evrS if p_ < 4 else evrR
            off = (p_ % 4) * 1024
            te_ = P.dma("sync", dst[:, off:off + 1024].rearrange(
                "q (c t) -> q c t", c=NCH),
                trows[:].rearrange("q (c t) -> q c t", c=NCH)[2 * p_:2 * p_ + 1, :, :],
                [("ve", t_tr[-1])], sem="ev")
            t_evr.append(te_)
        t_evsc = E("vector", (lambda e: e.tensor_scalar(
            out=evrS[:], in0=evrS[:], scalar1=sclc[0:1, :], scalar2=None,
            op0=ALU.mult)),
            [("ev", t_evr[-1]), ("lds", 64), ("ldg", 64)], "ve")
        t_ebccp = 0
        t_ebc = []
        for i in range(4):
            w = [("ev", t_evr[-1])]
            if i == 0:
                w.append(("ve", P.cnt["ve"]))
            else:
                w.append(("ve", t_ebccp))
            tb = E("tensor", (lambda e, i_=i: (
                mm(e, psC[:, 0:512], oner[:], evrR[:, i_ * 1024:i_ * 1024 + 512],
                   True, True),
                mm(e, psC[:, 512:1024], oner[:],
                   evrR[:, i_ * 1024 + 512:(i_ + 1) * 1024], True, True))[-1]),
                w, "pe")
            t_ebccp = E("vector", (lambda e, i_=i: e.tensor_copy(
                out=ebc(i_)[:], in_=psC)), [("pe", tb)], "ve")
            t_ebc.append(t_ebccp)

        t_wsl_free = [pe_att, pe_att, pe_att]
        t_wst = [0, 0, 0]
        ssem = ("h0", "h1", "h2")
        t_ttf = [0, 0]
        t_out_free = [0, 0]
        t_tf_free = [0, 0]
        t_g_done = {}
        t_gelu_last = 0
        for fs in range(NFS):
            xs = x2r[:, :, fs * FS:(fs + 1) * FS]
            fs_bar = [("se", P.cnt["se"]), ("ve", P.cnt["ve"])] if fs > 0 else []
            w = list(x2done) + [("ve", t_ebc[-1]), ("ve", t_evsc)] if fs == 0 \
                else [("se", t_gelu_last)] + fs_bar
            txq = E("vector", (lambda e, xs_=xs: e.tensor_tensor(
                out=bigB[:, 0:8192].rearrange("p (c t) -> p c t", c=NCH),
                in0=xs_, in1=xs_, op=ALU.mult)), w, "ve")
            tm = None
            for half in range(2):
                for dch in range(NCH):
                    w = []
                    if half == 0 and dch == 0:
                        w = [("ve", txq)] + (list(x2done) if fs == 0 else fs_bar)
                    tm = E("tensor", (lambda e, d_=dch, h_=half, xs_=xs: mm(
                        e, psD[0:1, h_ * 512:(h_ + 1) * 512], onec[:],
                        xs_[:, d_, h_ * 512:(h_ + 1) * 512], d_ == 0,
                        d_ == NCH - 1)), w,
                        "pe" if dch == NCH - 1 else None)
            tmtok = tm
            ts2 = None
            for half in range(2):
                for dch in range(NCH):
                    ts2 = E("tensor", (lambda e, d_=dch, h_=half: mm(
                        e, psC[0:1, h_ * 512:(h_ + 1) * 512], onec[:],
                        bigB[:, d_ * 1024 + h_ * 512:d_ * 1024 + (h_ + 1) * 512],
                        d_ == 0, d_ == NCH - 1)), [],
                        "pe" if dch == NCH - 1 else None)

            def frows(e):
                e.tensor_scalar(out=rrow(0), in0=psD[0:1, 0:1024],
                                scalar1=1.0 / D, scalar2=None, op0=ALU.mult)
                e.tensor_scalar(out=rrow(1), in0=psC[0:1, 0:1024],
                                scalar1=1.0 / D, scalar2=None, op0=ALU.mult)
                e.drain()
                e.tensor_tensor(out=rrow(3), in0=rrow(0), in1=rrow(0),
                                op=ALU.mult)
                e.drain()
                e.tensor_tensor(out=rrow(1), in0=rrow(1), in1=rrow(3),
                                op=ALU.subtract)
                e.drain()
                e.tensor_scalar(out=rrow(1), in0=rrow(1), scalar1=1e-5,
                                scalar2=None, op0=ALU.add)
                return e.engine_nop()
            tv1 = E("vector", frows, [("pe", tmtok), ("pe", ts2)], "ve")
            def rsqF(e):
                e.activation(out=rrow(3), in_=rrow(1), func=AF.Ln)
                e.drain()
                return e.activation(out=rrow(2), in_=rrow(3),
                                    func=AF.Exp, scale=-0.5)
            ts1 = E("scalar", rsqF, [("ve", tv1)], "se")

            def f2(e):
                e.scalar_tensor_tensor(out=rrow(1), in0=rrow(0), scalar=-1.0,
                                       in1=rrow(2), op0=ALU.mult, op1=ALU.mult)
                return e.engine_nop()
            tr2 = E("vector", f2, [("se", ts1)], "ve")
            tb2 = E("tensor", (lambda e: (
                mm(e, psD[:, 0:512], onerf[:], rrow(2, 512), True, True),
                mm(e, psD[:, 512:1024], onerf[:],
                   rows[0:1, 2 * 1024 + 512:3 * 1024], True, True),
                mm(e, psC[:, 0:512], onerf[:], rrow(1, 512), True, True),
                mm(e, psC[:, 512:1024], onerf[:],
                   rows[0:1, 1024 + 512:2 * 1024], True, True))[-1]),
                [("ve", tr2)], "pe")
            tc2 = E("vector", (lambda e: (
                e.tensor_copy(out=abcF[:, 0:512], in_=psD[:, 0:512]),
                e.tensor_copy(out=abcF[:, 512:1024], in_=psD[:, 512:1024]),
                e.tensor_copy(out=bbcF[:, 0:512], in_=psC[:, 0:512]),
                e.tensor_copy(out=bbcF[:, 512:1024],
                              in_=psC[:, 512:1024]))[-1]),
                [("pe", tb2)], "ve")
            t_h2a = []
            for ch in range(NCH):
                tt = E("vector", (lambda e, c=ch, xs_=xs: (
                    e.tensor_tensor(out=h2_sb(c), in0=xs_[:, c, :], in1=abcF[:],
                                    op=ALU.mult),
                    e.drain(),
                    e.tensor_tensor(out=h2_sb(c), in0=h2_sb(c), in1=bbcF[:],
                                    op=ALU.add))[-1]), [("ve", tc2)], "ve")
                ta = E("scalar", (lambda e, c=ch: e.activation(
                    out=h2_sb(c), in_=h2_sb(c), func=AF.Identity,
                    bias=mcol(SBv, c), scale=mcol(CB1, c))), [("ve", tt)], "se")
                t_h2a.append(ta)
            h2w = [("se", t) for t in t_h2a]
            t_u = []
            tu = None
            for p_ in range(4):
                for half in range(2):
                    for dch in range(NCH):
                        w = []
                        if p_ == 0 and half == 0 and dch == 0:
                            w = h2w + [("ve", tc2)]
                        elif half == 0 and dch == 0:
                            w = [("ve", t_u[p_ - 1])]
                        tu = E("tensor", (lambda e, d_=dch, h_=half, pp=p_: mm(
                            e, psD[0:1, h_ * 512:(h_ + 1) * 512],
                            h2l[:, d_ * LORA + 2 * pp + 1:d_ * LORA + 2 * pp + 2],
                            h2_sb(d_)[:, h_ * 512:(h_ + 1) * 512], d_ == 0,
                            d_ == NCH - 1)), w,
                            "pe" if dch == NCH - 1 else None)
                tuc = E("vector", (lambda e, pp=p_: e.tensor_copy(
                    out=urow[:, pp * 1024:(pp + 1) * 1024],
                    in_=psD[0:1, 0:1024])), [("pe", tu)], "ve")
                t_u.append(tuc)
            uw = [("ve", t) for t in t_u]
            t_wl = 0
            lsem1 = ""
            for hch in range(32):
                wslot = (hch // 4) % 2
                if hch % 4 == 0:
                    gnum = hch // 4
                    w = [("pe", t_wsl_free[wslot])]
                    if t_wst[wslot]:
                        w.append((ssem[wslot], t_wst[wslot]))
                    qa = "gpsimd" if gnum % 2 == 0 else "sync"
                    if fs == 0:
                        w.append(("p1", stage_done))
                    lsem1 = ssem[wslot]
                    t_wl = P.dma(qa, wsl(wslot)[:], w1e[gnum], w,
                                 sem=lsem1)
                    t_wst[wslot] = t_wl
                p_ = hch // 8
                pst = psA if hch % 2 == 0 else psB
                tk = None
                for half in range(2):
                    for dch in range(NCH):
                        w = []
                        if dch == 0 and half == 0:
                            prev = (fs, hch - 2) if hch >= 2 else \
                                ((fs - 1, 30 + hch) if fs > 0 else None)
                            w = [(lsem1, t_wl)]
                            if prev and prev in t_g_done:
                                w.append(("se", t_g_done[prev]))
                            if hch == 0:
                                w += uw + [("ve", t_evsc)]
                                if dbg:
                                    w.append(("db", P.cnt["db"]))
                        E("tensor", (lambda e, d_=dch, h_=half, hc=hch, ws=wslot,
                                     ps_=pst: mm(
                            e, ps_[:, h_ * 512:(h_ + 1) * 512],
                            wsl(ws)[:, (hc % 4) * 1024 + d_ * 128:
                                    (hc % 4) * 1024 + (d_ + 1) * 128],
                            h2_sb(d_)[:, h_ * 512:(h_ + 1) * 512],
                            d_ == 0, False)), w, None)
                    tk = E("tensor", (lambda e, h_=half, hc=hch, pp=p_, ps_=pst:
                           mm(e, ps_[:, h_ * 512:(h_ + 1) * 512],
                              evrS[:, pp * 1024 + (hc % 8) * 128:
                                   pp * 1024 + (hc % 8) * 128 + 128],
                              urow[:, pp * 1024 + h_ * 512:
                                   pp * 1024 + (h_ + 1) * 512],
                              False, True)), [], "pe" if half == 1 else None)
                if hch % 4 == 3:
                    t_wsl_free[wslot] = tk
                tg = E("scalar", (lambda e, hc=hch, ps_=pst: e.activation(
                    out=g_sb(hc), in_=ps_[:], func=AF.Gelu, bias=b1col(hc))),
                    [("pe", tk)], "se")
                t_g_done[(fs, hch)] = tg
                t_gelu_last = tg
            gw = [("se", t_g_done[(fs, hc)]) for hc in range(32)]
            if dbg and fs == 0:
                P.dma("sync", dbg_g[:], bigA[:], gw, sem="db")
            for och in range(NCH):
                ws2 = (och + 2) % 3
                w = [("pe", t_wsl_free[ws2])]
                if t_wst[ws2]:
                    w.append((ssem[ws2], t_wst[ws2]))
                qa2 = "gpsimd" if och % 2 == 0 else "sync"
                qb2 = "sync"
                if fs == 0:
                    lsem2 = f"g{ws2}"
                    w.append(("p1", stage_done))
                    t_w2l = P.dma(qa2, wsl(ws2)[:], w2r[och], w, sem=lsem2)
                else:
                    lsem2 = ssem[ws2]
                    t_w2l = P.dma(qa2, wsl(ws2)[:], w2e[och], w, sem=lsem2)
                    t_wst[ws2] = t_w2l
                t_ed = None
                pmo = och // 2
                if fs == 0:
                    for hch in range(32):
                        li = hch // 8
                        w = [(f"g{ws2}", t_w2l)]
                        if hch == 0:
                            w.append(("ve", t_odds))
                            w.append(("ve", t_ebc[-1]))
                        t_ed = E("vector", (lambda e, hc=hch, ws=ws2, oc=och,
                                            l_=li, pm=pmo: e.scalar_tensor_tensor(
                            out=wsl(ws)[:, hc * 128:(hc + 1) * 128],
                            in0=ebc(pm)[:, (oc % 2) * 512 + l_:
                                        (oc % 2) * 512 + l_ + 509:4],
                            scalar=odds[:, (hc % 8) * 8 + 4 + pm:
                                        (hc % 8) * 8 + 5 + pm],
                            in1=wsl(ws)[:, hc * 128:(hc + 1) * 128],
                            op0=ALU.mult, op1=ALU.add)),
                            w, "ve" if hch == 31 else None,
                            nodrain=(hch != 31))
                    t_wst[ws2] = P.dma(qb2, w2e[och], wsl(ws2)[:],
                                       [("ve", t_ed)], sem=ssem[ws2])
                pst2 = psC if och % 2 == 0 else psD
                tk = None
                for half in range(2):
                    for hch in range(32):
                        w = []
                        if hch == 0 and half == 0:
                            w = gw + ([("ve", t_ed)] if t_ed else
                                      [(lsem2, t_w2l)])
                            if t_ttf[och % 2]:
                                w.append(("se", t_ttf[och % 2]))
                        tk = E("tensor", (lambda e, hc=hch, h_=half, ws=ws2,
                                          ps_=pst2: mm(
                            e, ps_[:, h_ * 512:(h_ + 1) * 512],
                            wsl(ws)[:, hc * 128:(hc + 1) * 128],
                            g_sb(hc)[:, h_ * 512:(h_ + 1) * 512],
                            hc == 0, hc == 31)), w,
                            "pe" if hch == 31 else None)
                t_wsl_free[ws2] = tk
                w = [("pe", tk)]
                if t_tf_free[och % 2]:
                    w.append(("ve", t_tf_free[och % 2]))
                ttf = E("scalar", (lambda e, oc=och, sl=och % 2, ps_=pst2:
                        e.activation(
                    out=tfs(sl)[:], in_=ps_, func=AF.Identity,
                    bias=ccol(B2, oc))), w, "se")
                t_ttf[och % 2] = ttf
                w = [("se", ttf)]
                if t_out_free[och % 2]:
                    w.append((f"o{och % 2}", t_out_free[och % 2]))
                tout = E("vector", (lambda e, oc=och, sl=och % 2, f_=fs:
                         e.scalar_tensor_tensor(
                             out=outsl(sl)[:], in0=tfs(sl)[:],
                             scalar=mcol(GB, oc),
                             in1=x2v(oc)[:, f_ * FS:(f_ + 1) * FS],
                             op0=ALU.mult, op1=ALU.add)), w, "ve")
                t_tf_free[och % 2] = tout
                tdo = P.dma("sync", out_t[:, och, fs * FS:(fs + 1) * FS],
                            outsl(och % 2)[:], [("ve", tout)], sem=f"o{och % 2}")
                t_out_free[och % 2] = tdo

        dma_sems = ("lds", "ldg", "wl", "wo", "x0", "x1", "q0", "q1", "v0",
                    "v1", "t0", "t1", "qq", "lr", "lr0", "lr1", "ev", "o0",
                    "o1", "g0", "g1", "g2", "h0", "h1", "h2", "rs", "db")
        E("sync", lambda e: e.nop(),
          [(sname, P.cnt[sname]) for sname in dma_sems], None)

        with nc.Block() as block:
            def mk(engname):
                own = {"tensor": "pe", "vector": "ve", "scalar": "se"}.get(engname)

                def f(engine):
                    last = {}
                    for fn, w, inc, n, drain in P.ops[engname]:
                        for sname, v in w:
                            if sname == own:
                                continue
                            if last.get(sname, -1) >= v:
                                continue
                            engine.wait_ge(sems[sname], v)
                            last[sname] = v
                        ins = fn(engine)
                        if drain:
                            engine.drain()
                        if inc is not None:
                            ins.then_inc(sems[inc], n)
                return f
            block.sync(mk("sync"))
            block.gpsimd(mk("gpsimd"))
            block.vector(mk("vector"))
            block.scalar(mk("scalar"))
            block.tensor(mk("tensor"))
    return nc


def _shard_inputs(inputs):
    x = np.ascontiguousarray(np.asarray(inputs["L0_attn_x"], dtype=np.float32))
    Wq = np.asarray(inputs["L0_attn_Wq"], np.float32)
    bq = np.asarray(inputs["L0_attn_bq"], np.float32)
    Wk = np.asarray(inputs["L0_attn_Wk"], np.float32)
    bk = np.asarray(inputs["L0_attn_bk"], np.float32)
    Wv = np.asarray(inputs["L0_attn_Wv"], np.float32)
    bv = np.asarray(inputs["L0_attn_bv"], np.float32)
    Wo = np.asarray(inputs["L0_attn_Wo"], np.float32)
    bo = np.asarray(inputs["L0_attn_bo"], np.float32)
    W1 = np.asarray(inputs["L0_ffn_W1"], np.float32)
    b1 = np.asarray(inputs["L0_ffn_b1"], np.float32)
    W2 = np.asarray(inputs["L0_ffn_W2"], np.float32)
    b2 = np.asarray(inputs["L0_ffn_b2"], np.float32)
    scale = np.asarray(inputs["L0_ffn_scale"], np.float32).reshape(())
    c = np.asarray(inputs["L0_c"], np.float32)
    wmod = np.asarray(inputs["L0_wmod"], np.float32)

    tile8 = lambda w: np.ascontiguousarray(
        w.reshape(8, 128, w.shape[1]).transpose(1, 0, 2))
    wqx = tile8(Wq.T)
    wkx = tile8(Wk.T)
    wvx = tile8(Wv.T)
    wox = tile8(Wo.T)
    w1x = np.ascontiguousarray(
        W1.reshape(8, 4, 128, 8, 128).transpose(0, 4, 1, 3, 2))
    w2x = np.ascontiguousarray(
        W2.reshape(8, 128, 32, 128).transpose(0, 3, 2, 1))
    wmodx = tile8(wmod)
    colchunks = lambda v: np.ascontiguousarray(v.reshape(-1, 128).T)

    slopes = np.power(2.0, np.linspace(-4.0, -0.3, NH)).astype(np.float32)
    sclb = np.ascontiguousarray(np.broadcast_to(scale, (128, 1))).astype(np.float32)
    eyeb = np.eye(128, dtype=np.float32).astype(BFnp)

    maps = []
    for core in range(8):
        b, half = core // 2, core % 2
        o0, o1 = half * OWN, (1 - half) * OWN
        perm = np.concatenate([np.arange(o0, o0 + OWN), np.arange(o1, o1 + OWN)])
        xp = x[b][perm]
        xtx = tile8(np.ascontiguousarray(xp.T))
        xltx = tile8(np.ascontiguousarray(x[b, 1:17].T))
        pos = perm.astype(np.float32)
        posr = pos.reshape(KCN, 128).T
        btabx = np.exp(-(slopes[None, :, None] * posr[:, None, :])
                       ).reshape(128, NH * KCN)
        ccx = np.concatenate([colchunks(bq), colchunks(bk), colchunks(bo),
                              colchunks(b2), colchunks(b1), colchunks(c[b])],
                             axis=1).astype(np.float32)
        maps.append(dict(
            xt=xtx, xlt=xltx, wqx=wqx, wkx=wkx, wvx=wvx, wox=wox,
            w1x=w1x, w2x=w2x, ccx=np.ascontiguousarray(ccx),
            bvr=np.ascontiguousarray(bv.reshape(1, D)), wmodx=wmodx,
            btabx=np.ascontiguousarray(btabx.astype(np.float32)),
            eyex=eyeb, sclx=sclb))
    return maps


def kernel(**inputs):
    maps = _shard_inputs(inputs)
    nc = build_nc()
    res = run_bass_kernel_spmd(nc, maps, core_ids=list(range(8)))
    out = np.empty((4, LT, D), dtype=np.float32)
    for core in range(8):
        b, half = core // 2, core % 2
        ot = np.asarray(res.results[core]["out_t"])
        blk = ot.transpose(2, 1, 0).reshape(OWN, D)
        out[b, half * OWN:(half + 1) * OWN] = blk
    return out


# version 32
# speedup vs baseline: 1.0474x; 1.0068x over previous
import contextlib
import numpy as np
import ml_dtypes
import concourse.bass as bass
import concourse.mybir as mybir
from concourse.bass_utils import run_bass_kernel_spmd

F32 = mybir.dt.float32
BF16 = mybir.dt.bfloat16
AF = mybir.ActivationFunctionType
ALU = mybir.AluOpType
BFnp = ml_dtypes.bfloat16

D = 1024
NCH = 8
LT = 4096
OWN = 2048
LORA = 16
TOT = OWN + LORA
NH = 16
HD = 64
CS = 512
NSL = LT // CS
WQT = 512
KCN = 32
KCB = 8
FS = 1024
NFS = OWN // FS

ENGS = ("sync", "gpsimd", "vector", "scalar", "tensor")
SEMS = ("pe", "ve", "se", "lds", "ldg", "wl", "wo", "x0", "x1", "q0", "q1",
        "v0", "v1", "t0", "t1", "qq", "lr", "lr0", "lr1", "ev", "o0", "o1",
        "g0", "g1", "g2", "h0", "h1", "h2", "rs", "db", "p0", "p1")


class Prog:
    def __init__(self):
        self.ops = {e: [] for e in ENGS}
        self.cnt = {s: 0 for s in SEMS}

    def emit(self, eng, fn, wait=(), inc=None, n=1, nodrain=False):
        w = tuple((k, int(v)) for (k, v) in wait if v)
        tok = None
        if inc is not None:
            self.cnt[inc] += n
            tok = self.cnt[inc]
        drain = (eng == "vector") and not nodrain
        self.ops[eng].append((fn, w, inc, n, drain))
        return tok

    def dma(self, eng, out, in_, wait=(), sem=None):
        if sem is None:
            sem = "lds" if eng == "sync" else "ldg"
        return self.emit(eng, (lambda e, o=out, i=in_: e.dma_start(out=o, in_=i)),
                         wait, sem, 16, nodrain=True)


def build_nc(dbg=False):
    nc = bass.Bass("TRN2", target_bir_lowering=False)
    dt = lambda name, shape, dtype, kind: nc.dram_tensor(
        name, shape, dtype, kind=kind).ap()

    xt = dt("xt", [128, NCH, LT], F32, "ExternalInput")
    xlt = dt("xlt", [128, NCH, LORA], F32, "ExternalInput")
    wqx = dt("wqx", [128, NCH, D], F32, "ExternalInput")
    wkx = dt("wkx", [128, NCH, D], F32, "ExternalInput")
    wvx = dt("wvx", [128, NCH, D], F32, "ExternalInput")
    wox = dt("wox", [128, NCH, D], F32, "ExternalInput")
    w1x = dt("w1x", [8, 128, 4, NCH, 128], F32, "ExternalInput")
    w2x = dt("w2x", [8, 128, 32, 128], F32, "ExternalInput")
    w1e = dt("w1e", [8, 128, 4096], BF16, "Internal")
    rl_hbm = dt("rl_hbm", [5, 16, 512], BF16, "Internal")
    w2e = dt("w2e", [8, 128, 4096], BF16, "Internal")
    w2r = dt("w2r", [8, 128, 4096], BF16, "Internal")
    ccx = dt("ccx", [128, 72], F32, "ExternalInput")
    bvr = dt("bvr", [1, D], F32, "ExternalInput")
    wmodx = dt("wmodx", [128, NCH, 6 * D], F32, "ExternalInput")
    btabx = dt("btabx", [128, 512], F32, "ExternalInput")
    eyex = dt("eyex", [128, 128], BF16, "ExternalInput")
    sclx = dt("sclx", [128, 1], F32, "ExternalInput")

    kind_scr = "ExternalOutput" if dbg else "Internal"
    q_hbm = dt("q_hbm", [128, NCH, TOT], BF16, kind_scr)
    v_hbm = dt("v_hbm", [128, KCN, NH * 65], BF16, kind_scr)
    out_t = dt("out_t", [128, NCH, OWN], F32, "ExternalOutput")
    dbg_x2 = dt("dbg_x2", [128, NCH * TOT], BF16, kind_scr) if dbg else None
    dbg_mod = dt("dbg_mod", [128, 48], F32, kind_scr) if dbg else None
    dbg_k = dt("dbg_k", [128, 32768], BF16, kind_scr) if dbg else None
    dbg_h2l = dt("dbg_h2l", [128, NCH * LORA], BF16, kind_scr) if dbg else None
    dbg_g = dt("dbg_g", [128, 32768], BF16, kind_scr) if dbg else None

    P = Prog()
    es = contextlib.ExitStack()
    with es:
        sbt = lambda name, shape, dtype: es.enter_context(
            nc.sbuf_tensor(name, shape, dtype))
        x2 = sbt("x2", [128, NCH * TOT], BF16)
        rows = sbt("rows", [1, 4096], F32)
        btab = sbt("btab", [128, 512], F32)
        lq = sbt("lq", [128, 512], BF16)
        rlq = sbt("rlq", [128, 512], BF16)
        rlp = sbt("rlp", [1, 1024], BF16)
        rlb = sbt("rlb", [128, 1024], BF16)
        modc = sbt("modc", [128, 48], F32)
        cc = sbt("cc", [128, 72], F32)
        eye = sbt("eye", [128, 128], BF16)
        onec = sbt("onec", [128, 1], BF16)
        oner = sbt("oner", [1, 128], BF16)
        onerf = sbt("onerf", [1, 128], F32)
        one1 = sbt("one1", [1, 1], BF16)
        h2l = sbt("h2l", [128, NCH * LORA], BF16)
        odds = sbt("odds", [128, 64], F32)
        sil = sbt("sil", [128, 8], BF16)
        sclc = sbt("sclc", [128, 1], F32)
        bigA = sbt("bigA", [128, 32768], BF16)
        bigB = sbt("bigB", [128, 24832], BF16)
        bigD = sbt("bigD", [128, 18688], BF16)
        psA = es.enter_context(nc.psum_tensor("psA", [128, 1024], F32))
        psB = es.enter_context(nc.psum_tensor("psB", [128, 1024], F32))
        psS = es.enter_context(nc.psum_tensor("psS", [128, 2048], F32))
        psC = psS[:, 0:1024]
        psD = psS[:, 1024:2048]
        sems = {s: es.enter_context(nc.semaphore(s)) for s in SEMS}

        bigD_f32 = bigD[:].bitcast(F32)
        bvrb = bigD[0:1, 16416:17440]
        rlp2 = bigD[0:1, 17440:18464]
        trows = bigD[0:16, 17440:18464]
        tyv = lambda sl: rlb[:, sl * 512:(sl + 1) * 512]

        x2v = lambda ch: x2[:, ch * TOT:(ch + 1) * TOT]
        x2r = x2[:].rearrange("p (c t) -> p c t", c=NCH)
        K_sb = lambda j: bigA[:, j * 4096:(j + 1) * 4096]
        g_sb = lambda hch: bigA[:, hch * 1024:(hch + 1) * 1024]
        wk_sb = lambda dch: bigB[:, dch * 1024:(dch + 1) * 1024]
        wv_sb = lambda dch: bigB[:, 8320 + dch * 1024:8320 + (dch + 1) * 1024]
        wq_sb = lambda dch: bigB[:, 16640 + dch * 1024:16640 + (dch + 1) * 1024]
        vtb = lambda slot: (bigB[:, 0:8320] if slot == 0
                            else bigB[:, 16512:24832])
        wo_sb = bigB[:, 8320:16512]
        h2_sb = lambda dch: bigB[:, dch * 1024:(dch + 1) * 1024]
        ebc = lambda i: bigB[:, 8192 + i * 1024:8192 + (i + 1) * 1024]
        wsl = lambda slot: bigB[:, 12288 + slot * 4096:12288 + (slot + 1) * 4096]
        xoth = lambda slot: bigD[:, slot * 4096:(slot + 1) * 4096]
        h1s = bigD[:, 8192:12288]
        abcC = bigD[:, 12288:12800]
        bbcC = bigD[:, 12800:13312]
        qtmp = lambda slot: bigD[:, 13312 + slot * 512:13312 + (slot + 1) * 512]
        vtl = lambda slot: bigD[:, 14336 + slot * 1040:14336 + (slot + 1) * 1040]
        def pPk(nm, kc):
            base = (0 if nm == "A" else 4096) + ((kc % 4) // 2) * 2048 \
                + (kc % 2) * 1024
            return bigD[:, base:base + 1024]
        attq = lambda j: bigD[:, 8192 + j * 512:8192 + (j + 1) * 512]
        qqr = bigD[:, 12288:16384].rearrange("p (c t) -> p c t", c=NCH)
        lsst = bigD[:, 16416:18464]
        outsl = lambda slot: bigD_f32[:, slot * 1024:(slot + 1) * 1024]
        abcF = bigD[:, 4096:5120]
        bbcF = bigD[:, 5120:6144]
        tfs = lambda slot: bigD[:, 6144 + slot * 1024:6144 + (slot + 1) * 1024]
        urow = bigD[0:1, 8192:12288]
        evrS = bigD[0:1, 12288:16384]
        evrR = bigD[0:1, 8192:12288]

        SA, CA1, GA, SBv, CB1, GB = 0, 1, 2, 3, 4, 5
        mcol = lambda v, ch: modc[:, v * 8 + ch:v * 8 + ch + 1]
        BQ, BK, BO, B2 = 0, 8, 16, 24
        ccol = lambda base, ch: cc[:, base + ch:base + ch + 1]
        b1col = lambda hch: cc[:, 32 + hch:32 + hch + 1]
        btcol = lambda h, kc: btab[:, h * 32 + kc:h * 32 + kc + 1]
        rrow = lambda slot, w=1024: rows[0:1, slot * 1024:slot * 1024 + w]

        E = P.emit

        def mm(e, o, l, r, st_, sp, tp=None):
            return e.matmul(out=o, lhsT=l, rhs=r, start=st_, stop=sp,
                            tile_position=tp)

        P.dma("sync", cc[:], ccx[:])
        P.dma("sync", btab[:], btabx[:])
        P.dma("sync", eye[:], eyex[:])
        P.dma("sync", sclc[:], sclx[:])
        P.dma("gpsimd", bvrb[:], bvr[:])
        xlv = x2r[:, :, OWN:TOT]
        P.dma("gpsimd", xlv, xlt[:])
        P.dma("gpsimd",
              bigA[:, 0:5 * 6144].rearrange("p (c m) -> p c m", c=5),
              wmodx[:, 0:5, :])
        t_ld = P.dma("gpsimd",
                     bigB[:, 0:3 * 6144].rearrange("p (c m) -> p c m", c=3),
                     wmodx[:, 5:8, :])
        assert t_ld == 64
        wmod_sb = [bigA[:, d_ * 6144:(d_ + 1) * 6144] for d_ in range(5)] + \
                  [bigB[:, d_ * 6144:(d_ + 1) * 6144] for d_ in range(3)]
        mod_row = bigB[0:1, 18432:18432 + 6144]

        t_ones = E("vector", lambda e: (e.memset(onec[:], 1.0),
                                        e.memset(oner[:], 1.0),
                                        e.memset(onerf[:], 1.0),
                                        e.memset(one1[:], 1.0),
                                        e.engine_nop())[-1], (), "ve")
        t_sil = E("scalar", lambda e: e.activation(out=sil[:], in_=cc[:, 64:72],
                                                   func=AF.Silu),
                  [("lds", 64), ("ldg", 64)], "se")
        t_evrow = [0] * 12
        for mt in range(12):
            slot = mt % 2
            for dch in range(NCH):
                w = []
                if dch == 0:
                    w = [("lds", 64), ("ldg", 64), ("se", t_sil)]
                    if mt >= 2:
                        w.append(("ve", t_evrow[mt - 2]))
                tok = E("tensor", (lambda e, s_=slot, d_=dch, m_=mt:
                                   mm(e, psC[0:1, s_ * 512:(s_ + 1) * 512],
                                      sil[:, d_:d_ + 1],
                                      wmod_sb[d_][:, m_ * 512:(m_ + 1) * 512],
                                      d_ == 0, d_ == NCH - 1)),
                        w, "pe" if dch == NCH - 1 else None)
            t_evrow[mt] = E("vector", (lambda e, s_=slot, m_=mt:
                                       e.tensor_copy(
                                           out=mod_row[:, m_ * 512:(m_ + 1) * 512],
                                           in_=psC[0:1, s_ * 512:(s_ + 1) * 512])),
                            [("pe", tok)], "ve")
        for i in range(48):
            w = [("ve", t_evrow[min((i * 128) // 512, 11)])]
            if i == 0:
                w.append(("ve", t_ones))
            tok = E("tensor", (lambda e, i_=i: mm(e, psD[:, i_:i_ + 1],
                                                  mod_row[:, i_ * 128:(i_ + 1) * 128],
                                                  one1[:], True, True)),
                    w, "pe" if i == 47 else None)
        t_modcol = tok
        t_modc = E("vector", lambda e: e.tensor_copy(out=modc[:], in_=psD[:, 0:48]),
                   [("pe", t_modcol)], "ve")

        def fin_mod(e):
            e.drain()
            e.tensor_scalar(out=modc[:, 8:16], in0=modc[:, 8:16], scalar1=1.0,
                            scalar2=None, op0=ALU.add)
            e.tensor_scalar(out=modc[:, 32:40], in0=modc[:, 32:40], scalar1=1.0,
                            scalar2=None, op0=ALU.add)
            e.drain()
            for base in (16, 40):
                sl = modc[:, base:base + 8]
                msk = bigD[:].bitcast(mybir.dt.int32)[:, 0:8]
                mx = bigD_f32[:, 8:16]
                mn = bigD_f32[:, 16:24]
                e.tensor_single_scalar(out=msk, in_=sl, scalar=0.0, op=ALU.is_ge)
                e.tensor_scalar_max(out=mx, in0=sl, scalar1=0.25)
                e.tensor_scalar_min(out=mn, in0=sl, scalar1=-0.25)
                e.drain()
                e.select(out=sl, mask=msk, on_true=mx, on_false=mn, add_drain=True)
                e.drain()
            return e.engine_nop()
        t_mod = E("vector", fin_mod, [("ve", t_modc)], "ve")
        if dbg:
            P.dma("sync", dbg_mod[:], modc[:], [("ve", t_mod)], sem="db")

        def emit_wl(which):
            if which == 0:
                return P.dma("gpsimd",
                             bigB[:, 0:8192].rearrange("p (c m) -> p c m",
                                                       c=NCH),
                             wkx[:], [("pe", t_modcol)], sem="wl")
            if which == 1:
                return P.dma("gpsimd",
                             bigB[:, 16640:24832].rearrange(
                                 "p (c m) -> p c m", c=NCH),
                             wqx[:], sem="wl")
            return P.dma("gpsimd",
                         bigB[:, 8320:16512].rearrange("p (c m) -> p c m",
                                                       c=NCH),
                         wvx[:], sem="wl")
        t_x = [0] * NSL
        t_vlast = [0] * NSL
        t_h1tt = [0] * NSL
        kq_free = [0, 0]
        v_free = [0, 0]
        v_free_sem = ["se", "se"]
        first_v = [True]
        t_qtmp_free = [0, 0]
        t_vstage_free = [0, 0]
        t_kev_last = [0]
        t_vdma = {}

        H1 = lambda i: (h1s if i % 2 == 0 else bigD[:, 4096:8192])
        xoth0 = bigD[:, 0:4096]
        t_statm = [0] * (NSL + 1)
        t_h1k = {}
        t_rows = [0] * (NSL + 1)
        t_tc = [0] * (NSL + 1)
        h1ws = [None] * (NSL + 1)

        def xslice(s):
            if s < 4:
                return x2r[:, :, s * CS:(s + 1) * CS]
            return xoth0.rearrange("p (c t) -> p c t", c=NCH)

        def emit_xload(s):
            if s < 4:
                w = [("x1", t_x[s - 1])] if s >= 1 else []
                t_x[s] = P.dma("gpsimd", xslice(s),
                               xt[:, :, s * CS:(s + 1) * CS], w, sem="x1")
            else:
                w = [("ve", t_mod)]
                t_x[s] = P.dma("gpsimd", xslice(s),
                               xt[:, :, s * CS:(s + 1) * CS], w, sem="x0")

        def emit_xload_pair(s2, k):
            w = [("x0", t_x[s2 - 1] if k == 0 else t_x[s2]),
                 ("ve", t_h1k[(s2 - 1, k)]), ("pe", t_statm[s2 - 1])]
            t_x[s2] = P.dma(
                "gpsimd", xslice(s2)[:, 2 * k:2 * k + 2, :],
                xt[:, 2 * k:2 * k + 2, s2 * CS:(s2 + 1) * CS], w, sem="x0")
        emit_xload(0)
        emit_wl(0)
        emit_xload(1)
        emit_wl(1)
        emit_xload(2)
        t_wl3 = emit_wl(2)
        assert t_wl3 == 48
        emit_xload(3)

        def ln_rows(width, wm, ws2):
            def f1(e):
                e.tensor_scalar(out=rrow(0, width), in0=psC[0:1, 0:width],
                                scalar1=1.0 / D, scalar2=None, op0=ALU.mult)
                e.tensor_scalar(out=rrow(1, width), in0=psC[0:1, 512:512 + width],
                                scalar1=1.0 / D, scalar2=None, op0=ALU.mult)
                e.drain()
                e.tensor_tensor(out=rrow(3, width), in0=rrow(0, width),
                                in1=rrow(0, width), op=ALU.mult)
                e.drain()
                e.tensor_tensor(out=rrow(1, width), in0=rrow(1, width),
                                in1=rrow(3, width), op=ALU.subtract)
                e.drain()
                e.tensor_scalar(out=rrow(1, width), in0=rrow(1, width),
                                scalar1=1e-5, scalar2=None, op0=ALU.add)
                return e.engine_nop()
            tv1 = E("vector", f1, [("pe", wm), ("pe", ws2)], "ve")
            def rsq(e):
                e.activation(out=rrow(3, width), in_=rrow(1, width), func=AF.Ln)
                e.drain()
                return e.activation(out=rrow(2, width), in_=rrow(3, width),
                                    func=AF.Exp, scale=-0.5)
            tse = E("scalar", rsq, [("ve", tv1)], "se")

            def f2(e):
                e.scalar_tensor_tensor(out=rrow(1, width), in0=rrow(0, width),
                                       scalar=-1.0, in1=rrow(2, width),
                                       op0=ALU.mult, op1=ALU.mult)
                return e.engine_nop()
            return E("vector", f2, [("se", tse)], "ve")

        def emit_ln(xs, width, t_ready, extra_bcast_wait, slot=None):
            sl = h1s if slot is None else slot
            txq = E("vector", (lambda e, xs_=xs, w_=width, s_=sl: e.tensor_tensor(
                out=s_[:, 0:NCH * w_].rearrange("p (c t) -> p c t", c=NCH),
                in0=xs_, in1=xs_, op=ALU.mult)), t_ready, "ve")
            tm = None
            for dch in range(NCH):
                tm = E("tensor", (lambda e, d_=dch, xs_=xs, w_=width: mm(
                    e, psC[0:1, 0:w_], onec[:], xs_[:, d_, :], d_ == 0,
                    d_ == NCH - 1)),
                    list(t_ready) if dch == 0 else [],
                    "pe" if dch == NCH - 1 else None)
            ts2 = None
            for dch in range(NCH):
                ts2 = E("tensor", (lambda e, d_=dch, w_=width, s_=sl: mm(
                    e, psC[0:1, 512:512 + w_], onec[:],
                    s_[:, d_ * w_:(d_ + 1) * w_], d_ == 0, d_ == NCH - 1)),
                    [("ve", txq)] if dch == 0 else [],
                    "pe" if dch == NCH - 1 else None)
            tr = ln_rows(width, tm, ts2)
            tb = E("tensor", (lambda e, w_=width: (
                mm(e, psA[:, 0:w_], onerf[:], rrow(2, w_), True, True),
                mm(e, psA[:, 512:512 + w_], onerf[:], rrow(1, w_), True,
                   True))[-1]),
                [("ve", tr)] + list(extra_bcast_wait), "pe")
            tc = E("vector", (lambda e, w_=width: (
                e.tensor_copy(out=abcC[:, 0:w_], in_=psA[:, 0:w_]),
                e.tensor_copy(out=bbcC[:, 0:w_], in_=psA[:, 512:512 + w_]),
                e.engine_nop())[-1]), [("pe", tb)], "ve")
            return tc, tm, tr

        def emit_lnh1_stats(i):
            xs = xslice(i)
            sl = H1(i)
            sem = "x1" if i < 4 else "x0"
            t_ready = [(sem, t_x[i]), ("ve", t_mod)]
            if i >= 1:
                t_ready.append(("ve", t_rows[i - 1]))
            if i >= 2:
                t_ready.append(("pe", t_vlast[i - 2]))
            txq = E("vector", (lambda e, xs_=xs, s_=sl: e.tensor_tensor(
                out=s_[:, 0:NCH * CS].rearrange("p (c t) -> p c t", c=NCH),
                in0=xs_, in1=xs_, op=ALU.mult)), t_ready, "ve")
            tm = None
            for dch in range(NCH):
                tm = E("tensor", (lambda e, d_=dch, xs_=xs: mm(
                    e, psC[0:1, 0:CS], onec[:], xs_[:, d_, :], d_ == 0,
                    d_ == NCH - 1)),
                    list(t_ready) if dch == 0 else [],
                    "pe" if dch == NCH - 1 else None)
            ts2 = None
            for dch in range(NCH):
                ts2 = E("tensor", (lambda e, d_=dch, s_=sl: mm(
                    e, psC[0:1, 512:512 + CS], onec[:],
                    s_[:, d_ * CS:(d_ + 1) * CS], d_ == 0, d_ == NCH - 1)),
                    [("ve", txq)] if dch == 0 else [],
                    "pe" if dch == NCH - 1 else None)
            t_statm[i] = tm
            t_rows[i] = ln_rows(CS, tm, ts2)

        def emit_lnh1_bcast(i):
            xs = xslice(i)
            sl = H1(i)
            ebw = [("ve", t_tc[i - 1])] if i >= 1 else []
            tb = E("tensor", (lambda e: (
                mm(e, psA[:, 0:CS], onerf[:], rrow(2, CS), True, True),
                mm(e, psA[:, 512:512 + CS], onerf[:], rrow(1, CS), True,
                   True))[-1]),
                [("ve", t_rows[i])] + ebw, "pe")
            t_abc = E("vector", (lambda e: (
                e.tensor_copy(out=abcC[:, 0:CS], in_=psA[:, 0:CS]),
                e.tensor_copy(out=bbcC[:, 0:CS], in_=psA[:, 512:512 + CS]),
                e.engine_nop())[-1]), [("pe", tb)], "ve")
            t_tc[i] = t_abc
            h1ws[i] = []

        def emit_h1pair(i, ch0):
            xs = xslice(i)
            sl = H1(i)
            for ch in (ch0, ch0 + 1):
                tt = E("vector", (lambda e, c=ch, xs_=xs, s_=sl: (
                    e.tensor_tensor(out=s_[:, c * CS:(c + 1) * CS],
                                    in0=xs_[:, c, :], in1=abcC[:], op=ALU.mult),
                    e.drain(),
                    e.tensor_tensor(out=s_[:, c * CS:(c + 1) * CS],
                                    in0=s_[:, c * CS:(c + 1) * CS],
                                    in1=bbcC[:], op=ALU.add))[-1]),
                    [("ve", t_tc[i])], "ve")
                ta = E("scalar", (lambda e, c=ch, s_=sl: e.activation(
                    out=s_[:, c * CS:(c + 1) * CS],
                    in_=s_[:, c * CS:(c + 1) * CS],
                    func=AF.Identity, bias=mcol(SA, c), scale=mcol(CA1, c))),
                    [("ve", tt)], "se")
                h1ws[i].append(("se", ta))
                t_h1k[(i, ch // 2)] = tt
                if ch == NCH - 1:
                    t_h1tt[i] = tt

        emit_lnh1_stats(0)
        emit_lnh1_bcast(0)
        for c0 in range(0, NCH, 2):
            emit_h1pair(0, c0)
        for s in range(NSL):
            if s + 1 < NSL:
                emit_lnh1_stats(s + 1)
            sl = H1(s)
            h1w = h1ws[s]
            for ech in range(NCH):
                slot = ech % 2
                for dch in range(NCH):
                    w = []
                    if dch == 0:
                        w = list(h1w) + [("wl", 16)]
                        if kq_free[slot]:
                            w.append(("se", kq_free[slot]))
                    tk = E("tensor", (lambda e, d_=dch, ec=ech, sl2=slot, s_=sl:
                           mm(e, psB[:, sl2 * 512:(sl2 + 1) * 512],
                              wk_sb(d_)[:, ec * 128:(ec + 1) * 128],
                              s_[:, d_ * CS:(d_ + 1) * CS], d_ == 0,
                              d_ == NCH - 1)),
                        w, "pe" if dch == NCH - 1 else None)
                tke = E("scalar", (lambda e, ec=ech, sl2=slot, ss=s: e.activation(
                    out=K_sb(ec)[:, ss * CS:(ss + 1) * CS],
                    in_=psB[:, sl2 * 512:(sl2 + 1) * 512],
                    func=AF.Identity, bias=ccol(BK, ec))), [("pe", tk)], "se")
                kq_free[slot] = tke
                t_kev_last[0] = tke
            if s == NSL - 1:
                pe_klast = P.cnt["pe"]
            if s < 4:
                for ech in range(NCH):
                    slot = ech % 2
                    for dch in range(NCH):
                        w = []
                        if dch == 0:
                            w = [("wl", 32), ("se", kq_free[slot])]
                        tq = E("tensor", (lambda e, d_=dch, ec=ech, sl2=slot,
                                          s_=sl: mm(
                            e, psB[:, sl2 * 512:(sl2 + 1) * 512],
                            wq_sb(d_)[:, ec * 128:(ec + 1) * 128],
                            s_[:, d_ * CS:(d_ + 1) * CS], d_ == 0,
                            d_ == NCH - 1)),
                            w, "pe" if dch == NCH - 1 else None)
                    qsl = ech % 2
                    w = [("pe", tq)]
                    if t_qtmp_free[qsl]:
                        w.append((f"q{qsl}", t_qtmp_free[qsl]))
                    tqe = E("scalar", (lambda e, sl2=slot, q_=qsl, ec=ech:
                            e.activation(out=qtmp(q_)[:],
                                         in_=psB[:, sl2 * 512:(sl2 + 1) * 512],
                                         func=AF.Identity, bias=ccol(BQ, ec))),
                            w, "se")
                    kq_free[slot] = tqe
                    td = P.dma("sync", q_hbm[:, ech, s * CS:(s + 1) * CS],
                               qtmp(qsl)[:], [("se", tqe)], sem=f"q{qsl}")
                    t_qtmp_free[qsl] = td
            if s + 1 < NSL:
                emit_lnh1_bcast(s + 1)

            for tt_ in range(4):
                kc = s * 4 + tt_
                vslot = tt_ % 2
                tlast = None
                for vh in range(2):
                    for dch in range(NCH):
                        w = []
                        if dch == 0:
                            w = [("wl", 48)]
                            if v_free[vh]:
                                w.append((v_free_sem[vh], v_free[vh]))
                            if first_v[0]:
                                w.append(("ve", t_mod))
                                first_v[0] = False
                            if vh == 0 and dch == 0:
                                w += h1w
                        E("tensor", (lambda e, d_=dch, t2=tt_, v_=vh, s_=sl: mm(
                            e, psD[:, v_ * 512:(v_ + 1) * 512],
                            s_[:, d_ * CS + t2 * 128:d_ * CS + (t2 + 1) * 128],
                            wv_sb(d_)[:, v_ * 512:(v_ + 1) * 512],
                            d_ == 0, False)), w, None)
                    tv = E("tensor", (lambda e, v_=vh: mm(
                        e, psD[:, v_ * 512:(v_ + 1) * 512], oner[:],
                        bvrb[0:1, v_ * 512:(v_ + 1) * 512], False, True)),
                        [("lds", 64), ("ldg", 64), ("ve", t_ones)], "pe")
                    tlast = tv
                    for j in range(8):
                        h = vh * 8 + j
                        w = [("pe", tv)]
                        if j == 0 and t_vstage_free[vslot]:
                            w.append((f"v{vslot}", t_vstage_free[vslot]))
                        if vh == 0:
                            tve = E("scalar", (lambda e, h_=h, j_=j, v_=vh,
                                               sl2=vslot, k_=kc: e.activation(
                                out=vtl(sl2)[:, h_ * 65:h_ * 65 + 64],
                                in_=psD[:, v_ * 512 + j_ * 64:
                                        v_ * 512 + (j_ + 1) * 64],
                                func=AF.Identity, scale=btcol(h_, k_))), w, "se")
                        else:
                            tve = E("vector", (lambda e, h_=h, j_=j, v_=vh,
                                               sl2=vslot, k_=kc: e.tensor_scalar(
                                out=vtl(sl2)[:, h_ * 65:h_ * 65 + 64],
                                in0=psD[:, v_ * 512 + j_ * 64:
                                        v_ * 512 + (j_ + 1) * 64],
                                scalar1=btcol(h_, k_), scalar2=None,
                                op0=ALU.mult)),
                                w, "ve" if j == 7 else None, nodrain=(j != 7))
                    v_free[vh] = tve
                    v_free_sem[vh] = "se" if vh == 0 else "ve"
                t_vlast[s] = tlast
                tbc = E("vector", (lambda e, sl2=vslot, k_=kc: e.tensor_copy(
                    out=vtl(sl2)[:].rearrange("p (h c) -> p h c", h=16)[:, :, 64:65],
                    in_=btab[:].rearrange("p (h c) -> p h c", h=16)[:, :, k_:k_ + 1])),
                    [("lds", 64), ("ldg", 64), (v_free_sem[0], v_free[0]),
                     (v_free_sem[1], v_free[1])], "ve")
                tvd = P.dma("sync", v_hbm[:, kc, :], vtl(vslot)[:], [("ve", tbc)],
                            sem=f"v{vslot}")
                t_vstage_free[vslot] = tvd
                t_vdma[kc] = tvd
                if kc == 7:
                    v_snap0 = (P.cnt["v0"], P.cnt["v1"])
                if kc == 15:
                    v_snap1 = (P.cnt["v0"], P.cnt["v1"])
                if s + 1 < NSL:
                    emit_h1pair(s + 1, 2 * tt_)
                    if s + 2 < NSL and s + 2 > 4:
                        emit_xload_pair(s + 2, tt_)
            if s + 2 == 4:
                emit_xload(4)

        xsL = x2r[:, :, OWN:TOT]
        t_abcL, _, _ = emit_ln(
            xsL, LORA,
            [("lds", 64), ("ldg", 64), ("ve", t_mod),
             ("pe", t_vlast[NSL - 1]), ("ve", t_rows[NSL - 1]),
             ("ve", t_h1tt[NSL - 1])],
            [("ve", t_tc[NSL - 1])])
        t_h1La = []
        for ch in range(NCH):
            tt = E("vector", (lambda e, c=ch: (
                e.tensor_tensor(out=h1s[:, c * LORA:(c + 1) * LORA],
                                in0=xsL[:, c, :], in1=abcC[:, 0:LORA],
                                op=ALU.mult),
                e.drain(),
                e.tensor_tensor(out=h1s[:, c * LORA:(c + 1) * LORA],
                                in0=h1s[:, c * LORA:(c + 1) * LORA],
                                in1=bbcC[:, 0:LORA], op=ALU.add))[-1]),
                [("ve", t_abcL)], "ve")
            ta = E("scalar", (lambda e, c=ch: e.activation(
                out=h1s[:, c * LORA:(c + 1) * LORA],
                in_=h1s[:, c * LORA:(c + 1) * LORA],
                func=AF.Identity, bias=mcol(SA, c), scale=mcol(CA1, c))),
                [("ve", tt)], "se")
            t_h1La.append(ta)
            if ch == NCH - 1:
                t_h1L_tt = tt
        for ech in range(NCH):
            slot = ech % 2
            for dch in range(NCH):
                w = []
                if dch == 0:
                    w = [("se", t) for t in t_h1La] + \
                        ([("se", kq_free[slot])] if kq_free[slot] else [])
                tq = E("tensor", (lambda e, d_=dch, ec=ech, sl=slot: mm(
                    e, psB[:, sl * 512:sl * 512 + LORA],
                    wq_sb(d_)[:, ec * 128:(ec + 1) * 128],
                    h1s[:, d_ * LORA:(d_ + 1) * LORA], d_ == 0, d_ == NCH - 1)),
                    w, "pe" if dch == NCH - 1 else None)
            qsl = ech % 2
            w = [("pe", tq)]
            if t_qtmp_free[qsl]:
                w.append((f"q{qsl}", t_qtmp_free[qsl]))
            tqe = E("scalar", (lambda e, sl=slot, q_=qsl, ec=ech: e.activation(
                out=qtmp(q_)[:, 0:LORA], in_=psB[:, sl * 512:sl * 512 + LORA],
                func=AF.Identity, bias=ccol(BQ, ec))), w, "se")
            kq_free[slot] = tqe
            td = P.dma("sync", q_hbm[:, ech, OWN:TOT], qtmp(qsl)[:, 0:LORA],
                       [("se", tqe)], sem=f"q{qsl}")
            t_qtmp_free[qsl] = td

        pe_C = P.cnt["pe"]
        se_C = P.cnt["se"]
        ve_C = P.cnt["ve"]
        q_tot = (P.cnt["q0"], P.cnt["q1"])
        v_tot = (P.cnt["v0"], P.cnt["v1"])
        t_dbgk = 0
        if dbg:
            t_dbgk = P.dma("sync", dbg_k[:], bigA[:], [("se", t_kev_last[0])],
                           sem="db")

        t_wo = P.dma("gpsimd", wo_sb.rearrange("p (c m) -> p c m", c=NCH), wox[:],
                     [("pe", t_vlast[NSL - 1])], sem="wo")
        qtw = [WQT] * 4 + [LORA]
        stgb = bigB[:, 18592:22688]
        P.stage_prev = 0
        qq_base = [(bigD, 12288), (bigB, 2080)]
        t_q_lastuse_slot = [pe_C, pe_C]
        t_qq_slot = [0, 0]
        t_acc_free = [ve_C] * 4
        t_acc_free_sem = ["ve"] * 4
        t_qq_lastS = [pe_C]
        t_exp_lastA = [0]
        t_exp_lastB = [0]
        t_qq = 0
        t_qq_lastuse = 0
        P.t_rs_prev = 0
        P.t_bc_prev = [0, 0]
        P.t_tap_prev = [0, 0]
        t_x2w = []
        t_rlapply = []
        t_lsst_free = 0
        P.vtb_last = [pe_klast, pe_C]
        t_S_gate = {"C": [("se", se_C), ("ve", ve_C)], "D": [("se", se_C)]}
        t_P_free = {}

        for qt in range(5):
            W = qtw[qt]
            qoff = qt * WQT if qt < 4 else OWN
            qslot = qt % 2
            qsb, qob = qq_base[qslot]
            if qt == 0:
                w = [("pe", pe_C), ("se", se_C), ("ve", ve_C),
                     ("q0", q_tot[0]), ("q1", q_tot[1]),
                     ("v0", v_tot[0]), ("v1", v_tot[1])]
            else:
                w = [("pe", t_q_lastuse_slot[qslot])]
                if t_qq_slot[qslot]:
                    w.append(("qq", t_qq_slot[qslot]))
            t_qq = P.dma("gpsimd",
                         qsb[:, qob:qob + 4096].rearrange(
                             "p (c t) -> p c t", c=NCH)[:, :, 0:W],
                         q_hbm[:, :, qoff:qoff + W], w, sem="qq")
            t_qq_slot[qslot] = t_qq

            def emit_stage(pgi):
                tl = 0
                for piece in range(4):
                    w = []
                    if piece == 0:
                        w = ([("p1", P.stage_prev)] if P.stage_prev
                             else [("pe", pe_C)])
                    if pgi < 8:
                        tl = P.dma(
                            "gpsimd",
                            stgb[:, piece * 1024:(piece + 1) * 1024]
                            .rearrange("p (d e) -> p d e", d=NCH),
                            w1x[pgi, :, piece, :, :], w, sem="p0")
                    else:
                        tl = P.dma(
                            "gpsimd",
                            stgb[:, piece * 1024:(piece + 1) * 1024]
                            .rearrange("p (k o) -> p k o", k=8),
                            w2x[pgi - 8, :, piece * 8:(piece + 1) * 8, :],
                            w, sem="p0")
                tgt = w1e[pgi] if pgi < 8 else w2r[pgi - 8]
                P.dma("sync", tgt[:, 0:2048], stgb[:, 0:2048],
                      [("p0", tl)], sem="p1")
                P.stage_prev = P.dma("sync", tgt[:, 2048:4096],
                                     stgb[:, 2048:4096], [], sem="p1")
            t_lq_all = []
            t_rc_pg = []
            for pg in range(4):
                jA, jB = 2 * pg, 2 * pg + 1
                t_vtb = [0, 0]

                def vload(kcb, extra=(), pg_=pg):
                    slot = kcb % 2
                    t_vtb[slot] = P.dma(
                        "gpsimd",
                        vtb(slot)[:, 0:KCB * 260].rearrange(
                            "p (k c) -> p k c", k=KCB),
                        v_hbm[:, kcb * KCB:(kcb + 1) * KCB,
                              pg_ * 260:(pg_ + 1) * 260],
                        [("pe", P.vtb_last[slot])] + list(extra), sem=f"t{slot}")
                first = qt == 0 and pg == 0
                if qt != 4:
                    vload(0, [("v0", v_snap0[0]), ("v1", v_snap0[1])]
                          if first else [])
                    vload(1, [("v0", v_snap1[0]), ("v1", v_snap1[1])]
                          if first else [])
                t_exp = {}
                t_expk = {}
                t_evac = {}
                if qt == 4:
                    def vload(kcb, extra=(), pg_=pg):
                        slot = kcb % 2
                        t_vtb[slot] = P.dma(
                            "gpsimd",
                            vtb(slot)[:, 0:KCB * 260].rearrange(
                                "p (k c) -> p k c", k=KCB),
                            v_hbm[:, kcb * KCB:(kcb + 1) * KCB,
                                  pg_ * 260:(pg_ + 1) * 260],
                            [("pe", P.vtb_last[slot])] + list(extra),
                            sem=f"t{slot}")
                    t_vtb = [0, 0]
                    ex0 = []
                    vload(0, ex0)
                    vload(1, ex0)
                    t_sl = {}
                    for (pair, j) in (("A", jA), ("B", jB)):
                        ps = psC if pair == "A" else psD
                        w = list(t_S_gate["C" if pair == "A" else "D"])
                        t_S_gate["C" if pair == "A" else "D"] = []
                        w.append(("qq", t_qq))
                        tok = 0
                        for kc in range(KCN):
                            tok = E("tensor", (lambda e, p_=ps, j_=j, k_=kc,
                                               q_=qsb, qo_=qob: (
                                mm(e, p_[:, k_ * 16:k_ * 16 + 16],
                                   K_sb(j_)[0:64, k_ * 128:(k_ + 1) * 128],
                                   q_[0:64, qo_ + j_ * 512:qo_ + j_ * 512 + 16],
                                   True, True, (0, 0)),
                                mm(e, p_[:, 512 + k_ * 16:512 + k_ * 16 + 16],
                                   K_sb(j_)[64:128, k_ * 128:(k_ + 1) * 128],
                                   q_[64:128, qo_ + j_ * 512:qo_ + j_ * 512 + 16],
                                   True, True, (64, 0)))[-1]),
                                w if kc == 0 else [], "pe" if kc == KCN - 1 else None)
                        poff = 0 if pair == "A" else 4096
                        w2 = [("pe", tok)]
                        pfree = t_P_free.get((pair, 0))
                        if pfree:
                            w2.append(("pe", pfree))
                        texp = E("scalar", (lambda e, p_=ps, po=poff: e.activation(
                            out=bigD[:, po:po + 1024], in_=p_,
                            func=AF.Exp, scale=0.125)), w2, "se")
                        t_exp[(pair, 0)] = texp
                        t_sl[pair] = texp
                        (t_exp_lastA if pair == "A" else t_exp_lastB)[0] = texp
                        t_qq_lastS[0] = tok
                    tokp = {}
                    for kcb in range(4):
                        kslot = kcb % 2
                        for (pair, j) in (("A", jA), ("B", jB)):
                            pst = psA if pair == "A" else psB
                            poff = 0 if pair == "A" else 4096
                            for i in range(KCB):
                                kc = kcb * KCB + i
                                for hh in range(2):
                                    h = j * 2 + hh
                                    w = [("se", t_sl[pair]),
                                         (f"t{kslot}", t_vtb[kslot])]
                                    ai = (0 if pair == "A" else 2) + hh
                                    if kc == 0 and t_acc_free[ai]:
                                        w.append((t_acc_free_sem[ai],
                                                  t_acc_free[ai]))
                                    hl = h - 4 * pg
                                    tokp[pair] = E("tensor", (lambda e, p_=pst,
                                            o_=hh * 512, h_=hl, k_=kc, hh_=hh,
                                            ks=kslot, po=poff: mm(
                                        e, p_[0:65, o_:o_ + 16],
                                        vtb(ks)[:, (k_ % KCB) * 260 + h_ * 65:
                                                (k_ % KCB) * 260 + h_ * 65 + 65],
                                        bigD[:, po + hh_ * 512 + k_ * 16:
                                             po + hh_ * 512 + k_ * 16 + 16],
                                        k_ == 0, k_ == KCN - 1)),
                                        w, "pe" if (hh == 1 and i == KCB - 1)
                                        else None)
                        P.vtb_last[kslot] = tokp["B"]
                        if kcb + 2 < 4:
                            vload(kcb + 2)
                    t_P_free[("A", 0)] = tokp["A"]
                    t_P_free[("B", 0)] = tokp["B"]
                    for (pair, j) in (("A", jA), ("B", jB)):
                        tok = tokp[pair]
                        base = 0 if pair == "A" else 2
                        pst = psA if pair == "A" else psB

                        def evac(e, pst=pst, j=j, base=base, W=W):
                            for hh in range(2):
                                if hh == 0:
                                    dst = attq(j)[0:64, 0:W]
                                else:
                                    dst = attq(j)[64:128, 0:W]
                                e.tensor_copy(out=dst,
                                              in_=pst[0:64, hh * 512:hh * 512 + W])
                                e.tensor_copy(
                                    out=lsst[64:65, (base + hh) * 512:
                                             (base + hh) * 512 + W],
                                    in_=pst[64:65, hh * 512:hh * 512 + W])
                            return e.engine_nop()
                        tev = E("vector", evac,
                                [("pe", tok)] +
                                ([("rs", t_lsst_free)] if t_lsst_free else []),
                                "ve")
                        t_acc_free[base] = tev
                        t_acc_free[base + 1] = tev
                        t_acc_free_sem[base] = "ve"
                        t_acc_free_sem[base + 1] = "ve"
                        t_evac[pair] = tev
                elif True:
                    prev_pv = {}

                    def emit_S(kc, pair, j):
                        snm = "C" if pair == "A" else "D"
                        ps = psC if pair == "A" else psD
                        w = list(t_S_gate[snm])
                        t_S_gate[snm] = []
                        pr = t_exp.get((pair, kc - 1))
                        if pr:
                            w.append(("se", pr))
                        if kc == 0:
                            w.append(("qq", t_qq))
                        return E("tensor", (lambda e, p_=ps, j_=j, k_=kc, W_=W,
                                            q_=qsb, qo_=qob: (
                            mm(e, p_[:, 0:W_],
                               K_sb(j_)[0:64, k_ * 128:(k_ + 1) * 128],
                               q_[0:64, qo_ + j_ * 512:qo_ + j_ * 512 + W_],
                               True, True, (0, 0)),
                            mm(e, p_[:, 512:512 + W_],
                               K_sb(j_)[64:128, k_ * 128:(k_ + 1) * 128],
                               q_[64:128, qo_ + j_ * 512:qo_ + j_ * 512 + W_],
                               True, True, (64, 0)))[-1]),
                            w, "pe")

                    def emit_exp(kc, pair, ts_):
                        w = [("pe", ts_)]
                        pfree = t_P_free.get((pair, kc % 2))
                        if pfree:
                            w.append(("pe", pfree))
                        poff = (0 if pair == "A" else 4096) + (kc % 2) * 2048
                        ps = psC if pair == "A" else psD
                        texp = E("scalar", (lambda e, p_=ps, po=poff:
                                e.activation(
                            out=bigD[:, po:po + 1024], in_=p_,
                            func=AF.Exp, scale=0.125)), w, "se")
                        t_exp[(pair, kc)] = texp

                    def emit_PV(kc, pair, j, kslot):
                        pst = psA if pair == "A" else psB
                        tok = 0
                        for hh in range(2):
                            h = j * 2 + hh
                            w = [("se", t_exp[(pair, kc)]),
                                 (f"t{kslot}", t_vtb[kslot])]
                            ai = (0 if pair == "A" else 2) + hh
                            if kc == 0 and t_acc_free[ai]:
                                w.append((t_acc_free_sem[ai], t_acc_free[ai]))
                            poff = (0 if pair == "A" else 4096) + (kc % 2) * 2048
                            tok = E("tensor", (lambda e, p_=pst, o_=hh * 512,
                                               h_=h - 4 * pg,
                                               k_=kc, hh_=hh, W_=W, ks=kslot,
                                               po=poff: mm(
                                e, p_[0:65, o_:o_ + W_],
                                vtb(ks)[:, (k_ % KCB) * 260 + h_ * 65:
                                        (k_ % KCB) * 260 + h_ * 65 + 65],
                                bigD[:, po + hh_ * 512:po + hh_ * 512 + W_],
                                k_ == 0, k_ == KCN - 1)),
                                w, "pe" if hh == 1 else None)
                        t_P_free[(pair, kc % 2)] = tok
                        return tok

                    for kcb in range(4):
                        kslot = kcb % 2
                        for i in range(KCB):
                            kc = kcb * KCB + i
                            for (pair, j) in (("A", jA), ("B", jB)):
                                ts_ = emit_S(kc, pair, j)
                                emit_exp(kc, pair, ts_)
                                if kc >= 1:
                                    pslot = ((kc - 1) // KCB) % 2
                                    prev_pv[pair] = emit_PV(kc - 1, pair, j, pslot)
                                    if pair == "B" and (kc - 1) % KCB == KCB - 1:
                                        bdone = (kc - 1) // KCB
                                        P.vtb_last[bdone % 2] = prev_pv["B"]
                                        if bdone + 2 < 4:
                                            vload(bdone + 2)
                                        if bdone == 0:
                                            emit_stage(qt * 4 + pg)
                    for (pair, j) in (("A", jA), ("B", jB)):
                        tok = emit_PV(KCN - 1, pair, j, ((KCN - 1) // KCB) % 2)
                        base = 0 if pair == "A" else 2
                        pst = psA if pair == "A" else psB

                        def evac(e, pst=pst, j=j, base=base, W=W):
                            for hh in range(2):
                                if hh == 0:
                                    dst = attq(j)[0:64, 0:W]
                                else:
                                    dst = attq(j)[64:128, 0:W]
                                e.tensor_copy(out=dst,
                                              in_=pst[0:64, hh * 512:hh * 512 + W])
                                e.tensor_copy(
                                    out=lsst[64:65, (base + hh) * 512:
                                             (base + hh) * 512 + W],
                                    in_=pst[64:65, hh * 512:hh * 512 + W])
                            return e.engine_nop()
                        tev = E("vector", evac,
                                [("pe", tok)] +
                                ([("rs", t_lsst_free)] if t_lsst_free else []),
                                "ve")
                        t_acc_free[base] = tev
                        t_acc_free[base + 1] = tev
                        t_acc_free_sem[base] = "ve"
                        t_acc_free_sem[base + 1] = "ve"
                        t_evac[pair] = tev
                    P.vtb_last[3 % 2] = tok
                w = [("ve", t_evac["A"]), ("ve", t_evac["B"])]
                if P.t_rs_prev:
                    w.append(("rs", P.t_rs_prev))
                t_rs = P.dma("sync", rl_hbm[qt, 4 * pg:4 * pg + 4, 0:W],
                             lsst[64:65, 0:2048].rearrange(
                                 "p (s t) -> p s t", s=4)[:, :, 0:W], w,
                             sem="rs")
                t_lsst_free = t_rs
                P.t_rs_prev = t_rs
                for jj in range(2):
                    j = 2 * pg + jj
                    if j % 2 == 0:
                        d0, d1 = rlb[:, 0:W], rlb[:, 512:512 + W]
                    else:
                        d0, d1 = lq[:, 0:W], rlq[:, 0:W]
                    w = [("rs", t_rs)]
                    if P.t_bc_prev[j % 2]:
                        w.append((f"lr{j % 2}", P.t_bc_prev[j % 2]))
                    if P.t_tap_prev[j % 2]:
                        w.append(("ve", P.t_tap_prev[j % 2]))
                    if j % 2 == 0 and t_x2w:
                        w.append(("ve", t_x2w[-1]))
                    P.dma("sync", d0,
                          rl_hbm[qt, 2 * j:2 * j + 1, 0:W].broadcast_to(
                              (128, W)), w, sem=f"lr{j % 2}")
                    tbc_ = P.dma(
                        "sync", d1,
                        rl_hbm[qt, 2 * j + 1:2 * j + 2, 0:W].broadcast_to(
                            (128, W)), [], sem=f"lr{j % 2}")
                    P.t_bc_prev[j % 2] = tbc_
                    if pg == 3:
                        def _rcs(e, d0_=d0, d1_=d1, W_=W):
                            e.activation(out=d0_[0:64, 0:W_],
                                         in_=d0_[0:64, 0:W_], func=AF.Ln)
                            e.activation(out=d1_[64:128, 0:W_],
                                         in_=d1_[64:128, 0:W_],
                                         func=AF.Ln)
                            e.drain()
                            e.activation(out=d0_[0:64, 0:W_],
                                         in_=d0_[0:64, 0:W_],
                                         func=AF.Exp, scale=-1.0)
                            return e.activation(
                                out=d1_[64:128, 0:W_],
                                in_=d1_[64:128, 0:W_],
                                func=AF.Exp, scale=-1.0)
                        trc_ = E("scalar", _rcs,
                                 [(f"lr{j % 2}", tbc_)], "se")

                        def _tap(e, j_=j, d0_=d0, d1_=d1, W_=W):
                            e.tensor_tensor(out=attq(j_)[0:64, 0:W_],
                                            in0=attq(j_)[0:64, 0:W_],
                                            in1=d0_[0:64, 0:W_], op=ALU.mult)
                            return e.tensor_tensor(
                                out=attq(j_)[64:128, 0:W_],
                                in0=attq(j_)[64:128, 0:W_],
                                in1=d1_[64:128, 0:W_], op=ALU.mult)
                        tap_ = E("vector", _tap, [("se", trc_)], "ve")
                    else:
                        def _tap(e, j_=j, d0_=d0, d1_=d1, W_=W):
                            with nc.allow_low_precision(
                                    "softmax denominators kept bf16"):
                                e.reciprocal(out=d0_[0:64, 0:W_],
                                             in_=d0_[0:64, 0:W_])
                                e.reciprocal(out=d1_[64:128, 0:W_],
                                             in_=d1_[64:128, 0:W_])
                            e.drain()
                            e.tensor_tensor(out=attq(j_)[0:64, 0:W_],
                                            in0=attq(j_)[0:64, 0:W_],
                                            in1=d0_[0:64, 0:W_], op=ALU.mult)
                            return e.tensor_tensor(
                                out=attq(j_)[64:128, 0:W_],
                                in0=attq(j_)[64:128, 0:W_],
                                in1=d1_[64:128, 0:W_], op=ALU.mult)
                        tap_ = E("vector", _tap, [(f"lr{j % 2}", tbc_)], "ve")
                    P.t_tap_prev[j % 2] = tap_
                    t_rlapply.append(tap_)
            rlw = [("ve", t) for t in t_rlapply[-8:]]
            t_ty_free = 0
            tty = 0
            for ech in range(NCH):
                for vch in range(NCH):
                    w = []
                    if vch == 0:
                        w = (rlw if ech == 0 else []) + [("wo", 16)]
                        if ech == 0:
                            w.append(("ve", t_evac["A"]))
                        if t_ty_free:
                            w.append(("se", t_ty_free))
                    tw = E("tensor", (lambda e, v_=vch, ec=ech, W_=W: mm(
                        e, psA[:, 0:W_],
                        wo_sb[:, v_ * 1024 + ec * 128:v_ * 1024 + (ec + 1) * 128],
                        attq(v_)[:, 0:W_], v_ == 0, v_ == NCH - 1)),
                        w, "pe" if vch == NCH - 1 else None)
                w = [("pe", tw)]
                prev_x2 = t_x2w[-2] if (len(t_x2w) >= 2 and ech >= 2) else 0
                if prev_x2:
                    w.append(("ve", prev_x2))
                tsl = ech % 2
                tty = E("scalar", (lambda e, W_=W, ec=ech, sl=tsl: e.activation(
                    out=tyv(sl)[:, 0:W_], in_=psA[:, 0:W_], func=AF.Identity,
                    bias=ccol(BO, ec))), w, "se")
                t_ty_free = tty
                tx2 = E("vector", (lambda e, ec=ech, W_=W, qo=qoff, sl=tsl:
                        e.scalar_tensor_tensor(
                            out=x2v(ec)[:, qo:qo + W_], in0=tyv(sl)[:, 0:W_],
                            scalar=mcol(GA, ec), in1=x2v(ec)[:, qo:qo + W_],
                            op0=ALU.mult, op1=ALU.add)), [("se", tty)], "ve")
                t_x2w.append(tx2)
            t_q_lastuse_slot[qslot] = t_qq_lastS[0]
            t_S_gate = {"C": [("se", t_exp_lastA[0])],
                        "D": [("se", t_exp_lastB[0])]}
            t_acc_free = [tty, tty, 0, 0]
            t_acc_free_sem = ["se", "se", "ve", "ve"]
            t_acc_free[2] = t_evac["B"]
            t_acc_free[3] = t_evac["B"]
            t_P_free = {}

        pe_att = P.cnt["pe"]
        se_att = P.cnt["se"]
        stage_done = P.cnt["p1"]
        x2done = [("ve", P.cnt["ve"])]
        if dbg:
            P.dma("sync", dbg_x2[:], x2[:], list(x2done), sem="db")

        xsL2 = x2r[:, :, OWN:TOT]
        t_abc2, _, _ = emit_ln(xsL2, LORA, list(x2done), [])
        t_h2la = []
        for ch in range(NCH):
            tt = E("vector", (lambda e, c=ch: (
                e.tensor_tensor(out=h1s[:, c * LORA:(c + 1) * LORA],
                                in0=xsL2[:, c, :], in1=abcC[:, 0:LORA],
                                op=ALU.mult),
                e.drain(),
                e.tensor_tensor(out=h1s[:, c * LORA:(c + 1) * LORA],
                                in0=h1s[:, c * LORA:(c + 1) * LORA],
                                in1=bbcC[:, 0:LORA], op=ALU.add))[-1]),
                [("ve", t_abc2)], "ve")
            ta = E("scalar", (lambda e, c=ch: e.activation(
                out=h2l[:, c * LORA:(c + 1) * LORA],
                in_=h1s[:, c * LORA:(c + 1) * LORA],
                func=AF.Identity, bias=mcol(SBv, c), scale=mcol(CB1, c))),
                [("ve", tt)], "se")
            t_h2la.append(ta)
        if dbg:
            P.dma("sync", dbg_h2l[:], h2l[:], [("se", t) for t in t_h2la], sem="db")
        t_odds = E("vector", (lambda e: e.tensor_scalar(
            out=odds[:].rearrange("p (c q) -> p c q", c=NCH),
            in0=h2l[:].rearrange("p (c t) -> p c t", c=NCH)[:, :, 1:16:2],
            scalar1=sclc[:], scalar2=None, op0=ALU.mult)),
            [("se", t) for t in t_h2la] + [("lds", 64), ("ldg", 64)], "ve")
        ttk = 0
        for ch in range(NCH):
            w = [("se", t_h2la[ch]), ("lds", 64), ("ldg", 64)]
            if ch == 0:
                w += [("pe", pe_att), ("se", se_att)]
            ttk = E("tensor", (lambda e, c=ch: e.transpose(
                psS[:].bitcast(BF16)[0:16, 2048 + c * 128:2176 + c * 128],
                h2l[:, c * LORA:(c + 1) * LORA], eye[:])), w,
                "pe" if ch == NCH - 1 else None)
        t_trcp = E("vector", (lambda e: e.tensor_copy(
            out=trows[:],
            in_=psS[:].bitcast(BF16)[0:16, 2048:2048 + NCH * 128])),
            [("pe", ttk)], "ve")
        t_tr = [t_trcp] * NCH
        t_evr = []
        for p_ in range(8):
            dst = evrS if p_ < 4 else evrR
            off = (p_ % 4) * 1024
            te_ = P.dma("sync", dst[:, off:off + 1024].rearrange(
                "q (c t) -> q c t", c=NCH),
                trows[:].rearrange("q (c t) -> q c t", c=NCH)[2 * p_:2 * p_ + 1, :, :],
                [("ve", t_tr[-1])], sem="ev")
            t_evr.append(te_)
        t_evsc = E("vector", (lambda e: e.tensor_scalar(
            out=evrS[:], in0=evrS[:], scalar1=sclc[0:1, :], scalar2=None,
            op0=ALU.mult)),
            [("ev", t_evr[-1]), ("lds", 64), ("ldg", 64)], "ve")
        t_ebccp = 0
        t_ebc = []
        for i in range(4):
            w = [("ev", t_evr[-1])]
            if i == 0:
                w.append(("ve", P.cnt["ve"]))
            else:
                w.append(("ve", t_ebccp))
            tb = E("tensor", (lambda e, i_=i: (
                mm(e, psC[:, 0:512], oner[:], evrR[:, i_ * 1024:i_ * 1024 + 512],
                   True, True),
                mm(e, psC[:, 512:1024], oner[:],
                   evrR[:, i_ * 1024 + 512:(i_ + 1) * 1024], True, True))[-1]),
                w, "pe")
            t_ebccp = E("vector", (lambda e, i_=i: e.tensor_copy(
                out=ebc(i_)[:], in_=psC)), [("pe", tb)], "ve")
            t_ebc.append(t_ebccp)

        t_wsl_free = [pe_att, pe_att, pe_att]
        t_wst = [0, 0, 0]
        ssem = ("h0", "h1", "h2")
        t_ttf = [0, 0]
        t_out_free = [0, 0]
        t_tf_free = [0, 0]
        t_g_done = {}
        t_gelu_last = 0
        for fs in range(NFS):
            xs = x2r[:, :, fs * FS:(fs + 1) * FS]
            fs_bar = [("se", P.cnt["se"]), ("ve", P.cnt["ve"])] if fs > 0 else []
            w = list(x2done) + [("ve", t_ebc[-1]), ("ve", t_evsc)] if fs == 0 \
                else [("se", t_gelu_last)] + fs_bar
            txq = E("vector", (lambda e, xs_=xs: e.tensor_tensor(
                out=bigB[:, 0:8192].rearrange("p (c t) -> p c t", c=NCH),
                in0=xs_, in1=xs_, op=ALU.mult)), w, "ve")
            tm = None
            for half in range(2):
                for dch in range(NCH):
                    w = []
                    if half == 0 and dch == 0:
                        w = [("ve", txq)] + (list(x2done) if fs == 0 else fs_bar)
                    tm = E("tensor", (lambda e, d_=dch, h_=half, xs_=xs: mm(
                        e, psD[0:1, h_ * 512:(h_ + 1) * 512], onec[:],
                        xs_[:, d_, h_ * 512:(h_ + 1) * 512], d_ == 0,
                        d_ == NCH - 1)), w,
                        "pe" if dch == NCH - 1 else None)
            tmtok = tm
            ts2 = None
            for half in range(2):
                for dch in range(NCH):
                    ts2 = E("tensor", (lambda e, d_=dch, h_=half: mm(
                        e, psC[0:1, h_ * 512:(h_ + 1) * 512], onec[:],
                        bigB[:, d_ * 1024 + h_ * 512:d_ * 1024 + (h_ + 1) * 512],
                        d_ == 0, d_ == NCH - 1)), [],
                        "pe" if dch == NCH - 1 else None)

            def frows(e):
                e.tensor_scalar(out=rrow(0), in0=psD[0:1, 0:1024],
                                scalar1=1.0 / D, scalar2=None, op0=ALU.mult)
                e.tensor_scalar(out=rrow(1), in0=psC[0:1, 0:1024],
                                scalar1=1.0 / D, scalar2=None, op0=ALU.mult)
                e.drain()
                e.tensor_tensor(out=rrow(3), in0=rrow(0), in1=rrow(0),
                                op=ALU.mult)
                e.drain()
                e.tensor_tensor(out=rrow(1), in0=rrow(1), in1=rrow(3),
                                op=ALU.subtract)
                e.drain()
                e.tensor_scalar(out=rrow(1), in0=rrow(1), scalar1=1e-5,
                                scalar2=None, op0=ALU.add)
                return e.engine_nop()
            tv1 = E("vector", frows, [("pe", tmtok), ("pe", ts2)], "ve")
            def rsqF(e):
                e.activation(out=rrow(3), in_=rrow(1), func=AF.Ln)
                e.drain()
                return e.activation(out=rrow(2), in_=rrow(3),
                                    func=AF.Exp, scale=-0.5)
            ts1 = E("scalar", rsqF, [("ve", tv1)], "se")

            def f2(e):
                e.scalar_tensor_tensor(out=rrow(1), in0=rrow(0), scalar=-1.0,
                                       in1=rrow(2), op0=ALU.mult, op1=ALU.mult)
                return e.engine_nop()
            tr2 = E("vector", f2, [("se", ts1)], "ve")
            tb2 = E("tensor", (lambda e: (
                mm(e, psD[:, 0:512], onerf[:], rrow(2, 512), True, True),
                mm(e, psD[:, 512:1024], onerf[:],
                   rows[0:1, 2 * 1024 + 512:3 * 1024], True, True),
                mm(e, psC[:, 0:512], onerf[:], rrow(1, 512), True, True),
                mm(e, psC[:, 512:1024], onerf[:],
                   rows[0:1, 1024 + 512:2 * 1024], True, True))[-1]),
                [("ve", tr2)], "pe")
            tc2 = E("vector", (lambda e: (
                e.tensor_copy(out=abcF[:, 0:512], in_=psD[:, 0:512]),
                e.tensor_copy(out=abcF[:, 512:1024], in_=psD[:, 512:1024]),
                e.tensor_copy(out=bbcF[:, 0:512], in_=psC[:, 0:512]),
                e.tensor_copy(out=bbcF[:, 512:1024],
                              in_=psC[:, 512:1024]))[-1]),
                [("pe", tb2)], "ve")
            t_h2a = []
            for ch in range(NCH):
                tt = E("vector", (lambda e, c=ch, xs_=xs: (
                    e.tensor_tensor(out=h2_sb(c), in0=xs_[:, c, :], in1=abcF[:],
                                    op=ALU.mult),
                    e.drain(),
                    e.tensor_tensor(out=h2_sb(c), in0=h2_sb(c), in1=bbcF[:],
                                    op=ALU.add))[-1]), [("ve", tc2)], "ve")
                ta = E("scalar", (lambda e, c=ch: e.activation(
                    out=h2_sb(c), in_=h2_sb(c), func=AF.Identity,
                    bias=mcol(SBv, c), scale=mcol(CB1, c))), [("ve", tt)], "se")
                t_h2a.append(ta)
            h2w = [("se", t) for t in t_h2a]
            t_u = []
            tu = None
            for p_ in range(4):
                for half in range(2):
                    for dch in range(NCH):
                        w = []
                        if p_ == 0 and half == 0 and dch == 0:
                            w = h2w + [("ve", tc2)]
                        elif half == 0 and dch == 0:
                            w = [("ve", t_u[p_ - 1])]
                        tu = E("tensor", (lambda e, d_=dch, h_=half, pp=p_: mm(
                            e, psD[0:1, h_ * 512:(h_ + 1) * 512],
                            h2l[:, d_ * LORA + 2 * pp + 1:d_ * LORA + 2 * pp + 2],
                            h2_sb(d_)[:, h_ * 512:(h_ + 1) * 512], d_ == 0,
                            d_ == NCH - 1)), w,
                            "pe" if dch == NCH - 1 else None)
                tuc = E("vector", (lambda e, pp=p_: e.tensor_copy(
                    out=urow[:, pp * 1024:(pp + 1) * 1024],
                    in_=psD[0:1, 0:1024])), [("pe", tu)], "ve")
                t_u.append(tuc)
            uw = [("ve", t) for t in t_u]
            t_wl = 0
            lsem1 = ""
            for hch in range(32):
                wslot = (hch // 4) % 2
                if hch % 4 == 0:
                    gnum = hch // 4
                    w = [("pe", t_wsl_free[wslot])]
                    if t_wst[wslot]:
                        w.append((ssem[wslot], t_wst[wslot]))
                    qa = "gpsimd" if gnum % 2 == 0 else "sync"
                    if fs == 0:
                        w.append(("p1", stage_done))
                    lsem1 = ssem[wslot]
                    t_wl = P.dma(qa, wsl(wslot)[:], w1e[gnum], w,
                                 sem=lsem1)
                    t_wst[wslot] = t_wl
                p_ = hch // 8
                pst = psA if hch % 2 == 0 else psB
                tk = None
                for half in range(2):
                    for dch in range(NCH):
                        w = []
                        if dch == 0 and half == 0:
                            prev = (fs, hch - 2) if hch >= 2 else \
                                ((fs - 1, 30 + hch) if fs > 0 else None)
                            w = [(lsem1, t_wl)]
                            if prev and prev in t_g_done:
                                w.append(("se", t_g_done[prev]))
                            if hch == 0:
                                w += uw + [("ve", t_evsc)]
                                if dbg:
                                    w.append(("db", P.cnt["db"]))
                        E("tensor", (lambda e, d_=dch, h_=half, hc=hch, ws=wslot,
                                     ps_=pst: mm(
                            e, ps_[:, h_ * 512:(h_ + 1) * 512],
                            wsl(ws)[:, (hc % 4) * 1024 + d_ * 128:
                                    (hc % 4) * 1024 + (d_ + 1) * 128],
                            h2_sb(d_)[:, h_ * 512:(h_ + 1) * 512],
                            d_ == 0, False)), w, None)
                    tk = E("tensor", (lambda e, h_=half, hc=hch, pp=p_, ps_=pst:
                           mm(e, ps_[:, h_ * 512:(h_ + 1) * 512],
                              evrS[:, pp * 1024 + (hc % 8) * 128:
                                   pp * 1024 + (hc % 8) * 128 + 128],
                              urow[:, pp * 1024 + h_ * 512:
                                   pp * 1024 + (h_ + 1) * 512],
                              False, True)), [], "pe" if half == 1 else None)
                if hch % 4 == 3:
                    t_wsl_free[wslot] = tk
                tg = E("scalar", (lambda e, hc=hch, ps_=pst: e.activation(
                    out=g_sb(hc), in_=ps_[:], func=AF.Gelu, bias=b1col(hc))),
                    [("pe", tk)], "se")
                t_g_done[(fs, hch)] = tg
                t_gelu_last = tg
            gw = [("se", t_g_done[(fs, hc)]) for hc in range(32)]
            if dbg and fs == 0:
                P.dma("sync", dbg_g[:], bigA[:], gw, sem="db")
            for och in range(NCH):
                ws2 = (och + 2) % 3
                w = [("pe", t_wsl_free[ws2])]
                if t_wst[ws2]:
                    w.append((ssem[ws2], t_wst[ws2]))
                qa2 = "gpsimd" if och % 2 == 0 else "sync"
                qb2 = "sync"
                if fs == 0:
                    lsem2 = f"g{ws2}"
                    w.append(("p1", stage_done))
                    t_w2l = P.dma(qa2, wsl(ws2)[:], w2r[och], w, sem=lsem2)
                else:
                    lsem2 = ssem[ws2]
                    t_w2l = P.dma(qa2, wsl(ws2)[:], w2e[och], w, sem=lsem2)
                    t_wst[ws2] = t_w2l
                t_ed = None
                pmo = och // 2
                if fs == 0:
                    for hch in range(32):
                        li = hch // 8
                        w = [(f"g{ws2}", t_w2l)]
                        if hch == 0:
                            w.append(("ve", t_odds))
                            w.append(("ve", t_ebc[-1]))
                        t_ed = E("vector", (lambda e, hc=hch, ws=ws2, oc=och,
                                            l_=li, pm=pmo: e.scalar_tensor_tensor(
                            out=wsl(ws)[:, hc * 128:(hc + 1) * 128],
                            in0=ebc(pm)[:, (oc % 2) * 512 + l_:
                                        (oc % 2) * 512 + l_ + 509:4],
                            scalar=odds[:, (hc % 8) * 8 + 4 + pm:
                                        (hc % 8) * 8 + 5 + pm],
                            in1=wsl(ws)[:, hc * 128:(hc + 1) * 128],
                            op0=ALU.mult, op1=ALU.add)),
                            w, "ve" if hch == 31 else None,
                            nodrain=(hch != 31))
                    t_wst[ws2] = P.dma(qb2, w2e[och], wsl(ws2)[:],
                                       [("ve", t_ed)], sem=ssem[ws2])
                pst2 = psC if och % 2 == 0 else psD
                tk = None
                for half in range(2):
                    for hch in range(32):
                        w = []
                        if hch == 0 and half == 0:
                            w = gw + ([("ve", t_ed)] if t_ed else
                                      [(lsem2, t_w2l)])
                            if t_ttf[och % 2]:
                                w.append(("se", t_ttf[och % 2]))
                        tk = E("tensor", (lambda e, hc=hch, h_=half, ws=ws2,
                                          ps_=pst2: mm(
                            e, ps_[:, h_ * 512:(h_ + 1) * 512],
                            wsl(ws)[:, hc * 128:(hc + 1) * 128],
                            g_sb(hc)[:, h_ * 512:(h_ + 1) * 512],
                            hc == 0, hc == 31)), w,
                            "pe" if hch == 31 else None)
                t_wsl_free[ws2] = tk
                w = [("pe", tk)]
                if t_tf_free[och % 2]:
                    w.append(("ve", t_tf_free[och % 2]))
                ttf = E("scalar", (lambda e, oc=och, sl=och % 2, ps_=pst2:
                        e.activation(
                    out=tfs(sl)[:], in_=ps_, func=AF.Identity,
                    bias=ccol(B2, oc))), w, "se")
                t_ttf[och % 2] = ttf
                w = [("se", ttf)]
                if t_out_free[och % 2]:
                    w.append((f"o{och % 2}", t_out_free[och % 2]))
                tout = E("vector", (lambda e, oc=och, sl=och % 2, f_=fs:
                         e.scalar_tensor_tensor(
                             out=outsl(sl)[:], in0=tfs(sl)[:],
                             scalar=mcol(GB, oc),
                             in1=x2v(oc)[:, f_ * FS:(f_ + 1) * FS],
                             op0=ALU.mult, op1=ALU.add)), w, "ve")
                t_tf_free[och % 2] = tout
                tdo = P.dma("sync", out_t[:, och, fs * FS:(fs + 1) * FS],
                            outsl(och % 2)[:], [("ve", tout)], sem=f"o{och % 2}")
                t_out_free[och % 2] = tdo

        dma_sems = ("lds", "ldg", "wl", "wo", "x0", "x1", "q0", "q1", "v0",
                    "v1", "t0", "t1", "qq", "lr", "lr0", "lr1", "ev", "o0",
                    "o1", "g0", "g1", "g2", "h0", "h1", "h2", "rs", "db")
        E("sync", lambda e: e.nop(),
          [(sname, P.cnt[sname]) for sname in dma_sems], None)

        with nc.Block() as block:
            def mk(engname):
                own = {"tensor": "pe", "vector": "ve", "scalar": "se"}.get(engname)

                def f(engine):
                    last = {}
                    for fn, w, inc, n, drain in P.ops[engname]:
                        for sname, v in w:
                            if sname == own:
                                continue
                            if last.get(sname, -1) >= v:
                                continue
                            engine.wait_ge(sems[sname], v)
                            last[sname] = v
                        ins = fn(engine)
                        if drain:
                            engine.drain()
                        if inc is not None:
                            ins.then_inc(sems[inc], n)
                return f
            block.sync(mk("sync"))
            block.gpsimd(mk("gpsimd"))
            block.vector(mk("vector"))
            block.scalar(mk("scalar"))
            block.tensor(mk("tensor"))
    return nc


def _shard_inputs(inputs):
    x = np.ascontiguousarray(np.asarray(inputs["L0_attn_x"], dtype=np.float32))
    Wq = np.asarray(inputs["L0_attn_Wq"], np.float32)
    bq = np.asarray(inputs["L0_attn_bq"], np.float32)
    Wk = np.asarray(inputs["L0_attn_Wk"], np.float32)
    bk = np.asarray(inputs["L0_attn_bk"], np.float32)
    Wv = np.asarray(inputs["L0_attn_Wv"], np.float32)
    bv = np.asarray(inputs["L0_attn_bv"], np.float32)
    Wo = np.asarray(inputs["L0_attn_Wo"], np.float32)
    bo = np.asarray(inputs["L0_attn_bo"], np.float32)
    W1 = np.asarray(inputs["L0_ffn_W1"], np.float32)
    b1 = np.asarray(inputs["L0_ffn_b1"], np.float32)
    W2 = np.asarray(inputs["L0_ffn_W2"], np.float32)
    b2 = np.asarray(inputs["L0_ffn_b2"], np.float32)
    scale = np.asarray(inputs["L0_ffn_scale"], np.float32).reshape(())
    c = np.asarray(inputs["L0_c"], np.float32)
    wmod = np.asarray(inputs["L0_wmod"], np.float32)

    tile8 = lambda w: np.ascontiguousarray(
        w.reshape(8, 128, w.shape[1]).transpose(1, 0, 2))
    wqx = tile8(Wq.T)
    wkx = tile8(Wk.T)
    wvx = tile8(Wv.T)
    wox = tile8(Wo.T)
    w1x = np.ascontiguousarray(
        W1.reshape(8, 4, 128, 8, 128).transpose(0, 4, 1, 3, 2))
    w2x = np.ascontiguousarray(
        W2.reshape(8, 128, 32, 128).transpose(0, 3, 2, 1))
    wmodx = tile8(wmod)
    colchunks = lambda v: np.ascontiguousarray(v.reshape(-1, 128).T)

    slopes = np.power(2.0, np.linspace(-4.0, -0.3, NH)).astype(np.float32)
    sclb = np.ascontiguousarray(np.broadcast_to(scale, (128, 1))).astype(np.float32)
    eyeb = np.eye(128, dtype=np.float32).astype(BFnp)

    maps = []
    for core in range(8):
        b, half = core // 2, core % 2
        o0, o1 = half * OWN, (1 - half) * OWN
        perm = np.concatenate([np.arange(o0, o0 + OWN), np.arange(o1, o1 + OWN)])
        xp = x[b][perm]
        xtx = tile8(np.ascontiguousarray(xp.T))
        xltx = tile8(np.ascontiguousarray(x[b, 1:17].T))
        pos = perm.astype(np.float32)
        posr = pos.reshape(KCN, 128).T
        btabx = np.exp(-(slopes[None, :, None] * posr[:, None, :])
                       ).reshape(128, NH * KCN)
        ccx = np.concatenate([colchunks(bq), colchunks(bk), colchunks(bo),
                              colchunks(b2), colchunks(b1), colchunks(c[b])],
                             axis=1).astype(np.float32)
        maps.append(dict(
            xt=xtx, xlt=xltx, wqx=wqx, wkx=wkx, wvx=wvx, wox=wox,
            w1x=w1x, w2x=w2x, ccx=np.ascontiguousarray(ccx),
            bvr=np.ascontiguousarray(bv.reshape(1, D)), wmodx=wmodx,
            btabx=np.ascontiguousarray(btabx.astype(np.float32)),
            eyex=eyeb, sclx=sclb))
    return maps


def kernel(**inputs):
    maps = _shard_inputs(inputs)
    nc = build_nc()
    res = run_bass_kernel_spmd(nc, maps, core_ids=list(range(8)))
    out = np.empty((4, LT, D), dtype=np.float32)
    for core in range(8):
        b, half = core // 2, core % 2
        ot = np.asarray(res.results[core]["out_t"])
        blk = ot.transpose(2, 1, 0).reshape(OWN, D)
        out[b, half * OWN:(half + 1) * OWN] = blk
    return out
